# version 02 (final)
# baseline (speedup 1.0000x reference)
import numpy as np
import concourse.bass as bass
import concourse.mybir as mybir
from concourse.bass_utils import run_bass_kernel_spmd
from concourse.alu_op_type import AluOpType as OP
from concourse.expressions import RuntimeValue
from contextlib import ExitStack, contextmanager

F32 = mybir.dt.float32
BF16 = mybir.dt.bfloat16
I32 = mybir.dt.int32
AF = mybir.ActivationFunctionType
AX = mybir.AxisListType
PE = mybir.EngineType.PE
DVE = mybir.EngineType.DVE
Pool = mybir.EngineType.Pool
SP = mybir.EngineType.SP
ACT = mybir.EngineType.Activation

T = 2048
D = 2048
NT = T // 128
ND = D // 128
H = 16
NCC = 34
INTER = 4096
I2 = 6464
I2P = 6656
NOC = I2P // 128
ISC = f32 = None


def f32c(x):
    return np.float32(x)


SC_ATT = 1.0 / np.sqrt(128.0)



def rope_perm():
    perm = np.zeros(128, dtype=np.int64)
    for i in range(32):
        perm[i] = 2 * i
        perm[32 + i] = 64 + 2 * i
        perm[64 + i] = 2 * i + 1
        perm[96 + i] = 64 + 2 * i + 1
    return perm


def host_prepare(inputs):
    f = lambda k: np.ascontiguousarray(np.asarray(inputs[k], dtype=np.float32))
    Hm = {}
    perm = rope_perm()

    WqT = np.ascontiguousarray(f("L0_attn_Wq").T)
    Wq_p = WqT.copy()
    for h in range(H):
        Wq_p[:, h * 128:(h + 1) * 128] = WqT[:, h * 128 + perm]
    Hm["WqT"] = Wq_p
    Hm["WkT"] = np.ascontiguousarray(f("L0_attn_Wk").T[:, perm])
    Hm["WvT"] = np.ascontiguousarray(f("L0_attn_Wv").T)
    Hm["WoT"] = np.ascontiguousarray(f("L0_attn_Wo").T)

    half = 64
    theta = 1.0 / 10000.0 ** (np.arange(0, half, 2, dtype=np.float64) / half)
    hh = np.arange(H, dtype=np.float64)
    ang = theta[:, None] * hh[None, :]
    cos = np.cos(ang)
    sin = np.sin(ang)
    cos128 = np.tile(cos, (4, 1)).astype(np.float32)
    sinH = np.concatenate([-np.tile(sin, (2, 1)), np.tile(sin, (2, 1))],
                          axis=0).astype(np.float32)
    Hm["cosF"] = np.ascontiguousarray(cos128)
    Hm["sinF"] = np.ascontiguousarray(sinH)

    r = np.arange(128)[:, None]
    c = np.arange(512)[None, :]
    Hm["diagmask"] = np.ascontiguousarray(
        np.concatenate([(c >= 128 * j + r).astype(np.float32) for j in range(4)], axis=1))

    Hm["identity"] = np.eye(128, dtype=np.float32)
    Hm["ones_col"] = np.ones((128, 1), dtype=np.float32)
    Hm["ones_row"] = np.ones((1, 128), dtype=np.float32)

    Hm["WcT"] = np.ascontiguousarray(f("L0_ffn_Wc").T)
    Hm["genes"] = f("L0_ffn_genes")
    Hm["WdT"] = np.ascontiguousarray(f("L0_ffn_Wd").T)
    Hm["WuT"] = np.ascontiguousarray(f("L0_ffn_Wu").T)
    Hm["l0_normw_bc"] = np.tile(f("L0_ffn_norm_w")[None, :], (128, 1))
    Hm["l0_temp_bc"] = np.tile(f("L0_ffn_temperature").reshape(1, 1), (128, 1))
    Hm["l0_scale_bc"] = np.tile(f("L0_ffn_scale").reshape(1, 1), (128, 1))

    Hm["W_z"] = f("L1_attn_W_z")
    Hm["W_xBC"] = f("L1_attn_W_xBC")
    Hm["W_dt"] = f("L1_attn_W_dt")
    cw = f("L1_attn_conv_w")
    diag = np.zeros((4, NCC, 128, 128), dtype=np.float32)
    for j in range(4):
        for cc in range(NCC):
            np.fill_diagonal(diag[j, cc], cw[cc * 128:(cc + 1) * 128, j])
    Hm["conv_diag"] = diag.reshape(4 * NCC * 128, 128)
    Hm["conv_b_fm"] = np.ascontiguousarray(f("L1_attn_conv_b").reshape(NCC, 128).T)
    Hm["dt_bias"] = f("L1_attn_dt_bias").reshape(64, 1)
    Hm["A_log"] = f("L1_attn_A_log").reshape(64, 1)
    Hm["D_bc"] = np.tile(np.repeat(f("L1_attn_D"), 64)[None, :], (128, 1))
    Hm["l1_normw_bc"] = np.tile(f("L1_attn_norm_w")[None, :], (128, 1))
    Hm["W_out"] = f("L1_attn_W_out")
    Hm["zeros4096"] = np.zeros((128, 4096), dtype=np.float32)
    tt_ = np.arange(128)
    Hm["tri01"] = (tt_[None, :] >= tt_[:, None]).astype(np.float32)

    wg = np.zeros((D, I2P), dtype=np.float32)
    wg[:, 0:I2] = f("L1_ffn_w_gate").T
    wu = np.zeros((D, I2P), dtype=np.float32)
    wu[:, 0:I2] = f("L1_ffn_w_up").T
    wd = np.zeros((I2P, D), dtype=np.float32)
    wd[0:I2, :] = f("L1_ffn_w_down").T
    Hm["wgT"] = wg
    Hm["wuT"] = wu
    Hm["wdT"] = wd
    return Hm


HOST_SHAPES = {
    "WqT": (D, D), "WkT": (D, 128), "WvT": (D, 128), "WoT": (D, D),
    "cosF": (128, H), "sinF": (128, H), "diagmask": (128, 4 * 512),
    "identity": (128, 128), "ones_col": (128, 1), "ones_row": (1, 128),
    "WcT": (D, 4096), "genes": (4096, 256), "WdT": (256, D), "WuT": (256, D),
    "l0_normw_bc": (128, D), "l0_temp_bc": (128, 1), "l0_scale_bc": (128, 1),
    "W_z": (D, 4096), "W_xBC": (D, 4352), "W_dt": (D, 64),
    "conv_diag": (4 * NCC * 128, 128), "conv_b_fm": (128, NCC),
    "dt_bias": (64, 1), "A_log": (64, 1), "D_bc": (128, 4096),
    "l1_normw_bc": (128, 4096), "W_out": (4096, D),
    "zeros4096": (128, 4096), "tri01": (128, 128),
    "wgT": (D, I2P), "wuT": (D, I2P), "wdT": (I2P, D),
}

SCRATCH = {
    "x1": ((T, D), F32), "x2": ((T, D), F32), "x3": ((T, D), F32),
    "wq_bf": ((D, D + 128), BF16),
    "wk_bf": ((D, 128), BF16), "wv_bf": ((D, 128), BF16), "wo_bf": ((D, D), BF16),
    "wc_bf": ((D, 4096), BF16), "genes_bf": ((4096, 256), BF16),
    "wd_bf": ((256, D), BF16), "wu_bf": ((256, D), BF16),
    "wz_bf": ((D, 4096), BF16), "wxbc_bf": ((D, 4352), BF16),
    "wdt_bf": ((D, 64), BF16), "convdiag_bf": ((4 * NCC * 128, 128), BF16),
    "wout_bf": ((4096, D), BF16),
    "z_scr": ((T, 4096), BF16), "xtm_scr": ((T, 4096), BF16),
    "ynorm_scr": ((4096, T), BF16), "cum_scr": ((64, T), F32),
    "g_scr": ((T, I2P), BF16),
    "qg_scr": ((I2P, T), BF16),
    "invs_scr": ((T, 1), F32),
}


@contextmanager
def _my_fori(nc, trips, regs):
    engines = regs.engines
    loop_name = f"myfori_{nc.next_id()}"
    loop_start = f"{loop_name}_loop"
    loop_end = f"{loop_name}_end"
    nc.regs_mov(regs, 0)
    nc.br(loop_start, engines=engines)
    with nc.body(loop_start, valid_engines=engines):
        yield RuntimeValue(val=regs, min_val=0, max_val=trips - 1)
        nc.regs_alu(regs, regs, 1, op=OP.add)
        nc.br_lt(regs, trips, on_true=loop_start, on_false=loop_end,
                 engines=engines)
    nc.switch_bb(loop_end)


class Ctx:
    def __init__(self, nc, es):
        self.nc = nc
        self.es = es
        self.d = {}
        self.sems = {}
        self.cnt = {}
        self.loop_rh = None

    def fori(self, trips, engines):
        if self.loop_rh is None:
            self.loop_rh = self.nc.alloc_registers(
                "loopreg", engines=[PE, DVE, ACT, SP, Pool])
        RH = type(self.loop_rh)
        sub = RH([self.loop_rh[e] for e in engines])
        return _my_fori(self.nc, trips, sub)

    def sem(self, name):
        if name not in self.sems:
            self.sems[name] = self.es.enter_context(self.nc.semaphore(name))
            self.cnt[name] = 0
        return self.sems[name]

    def barrier(self):
        self.nc.all_engine_barrier()



class Trk:

    def __init__(self, nc, eng_type, eng_obj, sem, name):
        self.nc = nc
        self.eng = eng_obj
        self.sem = sem
        self.rh = nc.alloc_registers(name, engines=[eng_type])
        nc.regs_mov(self.rh, 0)
        self.runC = 0
        self.body_first = False
        self.loop_i = None
        self.coeff = 0

    def begin_body(self, loop_i, coeff):
        self.body_first = True
        self.loop_i = loop_i
        self.coeff = coeff

    def wait(self, C):
        if self.loop_i is not None:
            if not self.body_first and C <= self.runC:
                return
            self.body_first = False
            self.runC = C
            self.nc.regs_alu(self.rh, self.loop_i, self.coeff, op=OP.mult)
            self.nc.regs_alu(self.rh, self.rh, C, op=OP.add)
            self.nc.regs_alu(self.rh, self.rh, 0, op=OP.max)
            self.eng.wait_ge(self.sem, self.rh)
            return
        if C <= 0:
            return
        d = C - self.runC
        if d > 0:
            self.runC = C
            self.nc.regs_alu(self.rh, self.rh, d, op=OP.add)
            self.eng.wait_ge(self.sem, self.rh)

    def free(self):
        for h in self.rh.handles:
            self.nc.free_register(h)


class Sync:
    ET = {"pe": PE, "dv": DVE, "ac": ACT, "sp": SP, "po": Pool}

    def __init__(self, ctx, tag):
        self.ctx = ctx
        self.nc = ctx.nc
        self.eng = {"pe": ctx.nc.tensor, "dv": ctx.nc.vector, "ac": ctx.nc.scalar,
                    "sp": ctx.nc.sync, "po": ctx.nc.gpsimd}
        self.tag = tag
        self.p = {}
        self.n = {}
        self.trk = {}
        self.in_loop = False
        self.coeffs = {}

    def _sem(self, key):
        if key not in self.p:
            self.p[key] = self.ctx.sem(f"{self.tag}_{key}")
            self.n[key] = 0
        return self.p[key]

    def done(self, e, inst):
        inst.then_inc(self._sem(e), 1)
        self.n[e] = self.n.get(e, 0) + 1
        return self.n[e]

    def dma(self, q, inst):
        inst.then_inc(self._sem(q), 16)
        self.n[q] = self.n.get(q, 0) + 16
        return self.n[q]

    def _trk(self, ce, key):
        k = (ce, key)
        if k not in self.trk:
            assert not self.in_loop, f"trk {k} first used inside loop - predeclare"
            self.trk[k] = Trk(self.nc, self.ET[ce], self.eng[ce], self._sem(key),
                              f"{self.tag}_{ce}_{key}"[:24])
        return self.trk[k]

    def wait(self, ce, key, C):
        self._trk(ce, key).wait(C)

    def waitn(self, ce, key, lag=0):
        self._trk(ce, key).wait(self.n.get(key, 0) - lag)

    def begin_loop(self, pairs, coeffs):
        for ce, key in pairs:
            self._trk(ce, key)
        self.coeffs = dict(coeffs)
        self.loop_n0 = dict(self.n)
        self.in_loop = True

    def iter_start(self, i):
        for (ce, key), t in self.trk.items():
            if key in self.coeffs:
                t.begin_body(i, self.coeffs[key])

    def end_loop(self, trips):
        for key, co in self.coeffs.items():
            got = self.n.get(key, 0) - self.loop_n0.get(key, 0)
            assert got == co, f"{self.tag}/{key}: emitted {got} per iter != declared {co}"
            self.n[key] = self.loop_n0.get(key, 0) + co * trips
        for (ce, key), t in self.trk.items():
            if key in self.coeffs:
                if not t.body_first:
                    t.runC += self.coeffs[key] * (trips - 1)
                t.body_first = False
                t.loop_i = None
        self.in_loop = False
        self.coeffs = {}

    def free(self):
        for t in self.trk.values():
            t.free()
        self.trk = {}



P0_PAIRS = [("WqT", "wq_bf"), ("WkT", "wk_bf"), ("WvT", "wv_bf"), ("WoT", "wo_bf"),
            ("WcT", "wc_bf"), ("genes", "genes_bf"), ("WdT", "wd_bf"), ("WuT", "wu_bf"),
            ("W_z", "wz_bf"), ("W_xBC", "wxbc_bf"), ("W_dt", "wdt_bf"),
            ("conv_diag", "convdiag_bf"), ("W_out", "wout_bf")]


def emit_p0(ctx):
    nc, d = ctx.nc, ctx.d
    s = ctx.sem("p0")
    for src, dst in P0_PAIRS:
        if dst == "wq_bf":
            nc.gpsimd.dma_start(out=d[dst][:, 0:D], in_=d[src]).then_inc(s, 16)
            nc.gpsimd.dma_start(out=d[dst][:, D:D + 128], in_=d["WkT"]).then_inc(s, 16)
            ctx.cnt["p0"] = ctx.cnt.get("p0", 0) + 16
        else:
            nc.gpsimd.dma_start(out=d[dst], in_=d[src]).then_inc(s, 16)
        ctx.cnt["p0"] = ctx.cnt.get("p0", 0) + 16
    ctx.p0_total = ctx.cnt["p0"]


def emit_p1(ctx):
    nc, d = ctx.nc, ctx.d
    es = ExitStack()
    sb = lambda n, shp, dt=F32: es.enter_context(nc.sbuf_tensor(n, shp, dt))

    ident = sb("p1_ident", [128, 128], BF16)
    cosF = sb("p1_cosF", [128, H], F32)
    sinF = sb("p1_sinF", [128, H], F32)
    csst = sb("p1_csst", [128, 2], F32)
    onec = sb("p1_onec", [128, 1], BF16)
    oner = sb("p1_oner", [1, 128], BF16)
    oner32 = sb("p1_oner32", [1, 128], F32)
    onec32 = sb("p1_onec32", [128, 1], F32)
    ln1 = sb("p1_ln1", [128, ND * T], BF16)
    ktile = sb("p1_k", [128, T], BF16)
    vtm = sb("p1_v", [128, NT * 128], BF16)
    Ofm = sb("p1_O", [128, H * T], BF16)
    wqh = sb("p1_wqh", [128, ND * 128], BF16)
    qA = sb("p1_qA", [128, T], F32)
    qB = sb("p1_qB", [128, T], F32)
    qbf = sb("p1_qbf", [128, T], BF16)
    rowsbf = sb("p1_rowsbf", [128, T], BF16)
    rows32 = sb("p1_rows32", [128, 1024], F32)
    nrow = rowsbf[0:1, 0:T]
    invz = rows32[0:1, 0:512]
    zsb = rows32[0:1, 512:1024]
    nkmax = sb("p1_nkmax", [1, 1], F32)
    negub = sb("p1_negub", [1, T], BF16)
    xw = [sb(f"p1_xw{i}", [128, D], F32) for i in range(2)]
    qC = xw[0]
    stat = sb("p1_stat", [128, 8], F32)

    x0v = d["x0"].rearrange("(t p) n -> t p n", p=128)
    x1v = d["x1"].rearrange("(t p) n -> t p n", p=128)
    wqv = d["wq_bf"].rearrange("(c p) n -> p c n", p=128)
    wkv = d["wk_bf"].rearrange("(c p) n -> p c n", p=128)
    wvv = d["wv_bf"].rearrange("(c p) n -> p c n", p=128)
    wov = d["wo_bf"].rearrange("(c p) n -> p c n", p=128)

    with es:
        for eng in (nc.sync, nc.gpsimd, nc.scalar):
            eng.wait_ge(ctx.sem("p0"), ctx.p0_total)
        cn = ctx.sem("p1_cn")
        for dst, srcn in ((ident, "identity"), (cosF, "cosF"), (sinF, "sinF"),
                          (onec, "ones_col"), (oner, "ones_row"),
                          (oner32, "ones_row"), (onec32, "ones_col")):
            nc.gpsimd.dma_start(out=dst[:], in_=d[srcn]).then_inc(cn, 16)
        NCON = 7 * 16
        for eng in (nc.vector, nc.scalar, nc.tensor):
            eng.wait_ge(cn, NCON)

        with nc.psum_tensor("p1_pstr0", [128, 1024], BF16) as pstr0, \
             nc.psum_tensor("p1_pstr1", [128, 1024], BF16) as pstr1, \
             nc.sbuf_tensor("p1_lnbf0", [128, D], BF16) as lnbf0, \
             nc.sbuf_tensor("p1_lnbf1", [128, D], BF16) as lnbf1:
            pstr = [pstr0, pstr1]
            lnbf = [lnbf0, lnbf1]
            sy = Sync(ctx, "pa")
            sy.dma("xa", nc.sync.dma_start(out=xw[0][:], in_=x0v[0]))

            def p1a_tile(u, h2):
                xq = ("xa", "xb")[h2]
                sy.waitn("dv", xq)
                sy.waitn("ac", xq)
                if h2 == 0:
                    sy.waitn("sp", "dv", 2)
                    sy.dma("xb", nc.sync.dma_start(out=xw[1][:], in_=x0v[2 * u + 1]))
                elif not (isinstance(u, int) and u == NT // 2 - 1):
                    sy.waitn("sp", "dv", 2)
                    sy.dma("xa", nc.sync.dma_start(out=xw[0][:], in_=x0v[2 * u + 2]))
                sy.waitn("dv", "dv")
                sy.done("dv", nc.vector.tensor_reduce(out=stat[:, 0:1], in_=xw[h2][:],
                                                      axis=AX.X, op=OP.add))
                sy.waitn("ac", "ac")
                sy.done("ac", nc.scalar.activation(out=qA[:, 0:D], in_=xw[h2][:],
                                                   func=AF.Square, accum_out=stat[:, 1:2]))
                sy.waitn("dv", "dv")
                sy.done("dv", nc.vector.tensor_scalar_mul(stat[:, 2:3], stat[:, 0:1],
                                                          float(1.0 / D)))
                sy.waitn("dv", "dv")
                sy.done("dv", nc.vector.tensor_tensor(out=stat[:, 3:4], in0=stat[:, 2:3],
                                                      in1=stat[:, 2:3], op=OP.mult))
                sy.waitn("dv", "dv")
                sy.waitn("dv", "ac")
                sy.done("dv", nc.vector.scalar_tensor_tensor(
                    out=stat[:, 4:5], in0=stat[:, 1:2], scalar=float(1.0 / D),
                    in1=stat[:, 3:4], op0=OP.mult, op1=OP.subtract))
                sy.waitn("ac", "dv")
                sy.done("ac", nc.scalar.activation(out=stat[:, 5:6], in_=stat[:, 4:5],
                                                   func=AF.Sqrt, bias=f32c(1e-5)))
                sy.waitn("dv", "ac")
                sy.done("dv", nc.vector.reciprocal(stat[:, 6:7], stat[:, 5:6]))
                sy.waitn("dv", "dv")
                sy.done("dv", nc.vector.tensor_tensor(out=stat[:, 7:8], in0=stat[:, 2:3],
                                                      in1=stat[:, 6:7], op=OP.mult))
                sy.waitn("dv", "dv")
                sy.waitn("dv", "pe", 2)
                sy.done("dv", nc.vector.tensor_scalar(out=lnbf[h2][:], in0=xw[h2][:],
                                                      scalar1=stat[:, 6:7],
                                                      scalar2=stat[:, 7:8],
                                                      op0=OP.mult, op1=OP.subtract))
                for half in range(2):
                    sy.waitn("pe", "dv")
                    sy.waitn("pe", "dv", 8 - half)
                    inst = None
                    for c8 in range(8):
                        c = half * 8 + c8
                        inst = nc.tensor.transpose(pstr[half][:, c8 * 128:(c8 + 1) * 128],
                                                   lnbf[h2][:, c * 128:(c + 1) * 128],
                                                   ident[:])
                    sy.done("pe", inst)
                tsl = bass.ts(2 * u + h2, 128)
                for half in range(2):
                    sy.waitn("dv", "pe", 1 - half)
                    src2 = pstr[half][:].rearrange("p (c n) -> p c n", n=128)
                    dstv = ln1[:].rearrange("p (c t) -> p c t", t=T)
                    sy.done("dv", nc.vector.tensor_copy(
                        dstv[:, half * 8:(half + 1) * 8, tsl], src2))

            sy.begin_loop([("dv", "xa"), ("dv", "xb"), ("ac", "xa"), ("ac", "xb"),
                           ("sp", "dv"), ("dv", "dv"), ("dv", "ac"), ("ac", "dv"),
                           ("ac", "ac"), ("pe", "dv"), ("dv", "pe")],
                          {"xa": 16, "xb": 16, "dv": 18, "ac": 4, "pe": 4})
            with ctx.fori(NT // 2 - 1, [PE, DVE, SP, ACT]) as u:
                sy.iter_start(u)
                p1a_tile(u, 0)
                p1a_tile(u, 1)
            sy.end_loop(NT // 2 - 1)
            p1a_tile(NT // 2 - 1, 0)
            p1a_tile(NT // 2 - 1, 1)
            sy.free()
        ctx.barrier()

        with nc.psum_tensor("p1_psq", [128, 2048], F32) as psq, \
             nc.psum_tensor("p1_pss0", [128, 512], F32) as pss0, \
             nc.psum_tensor("p1_pss1", [128, 512], F32) as pss1, \
             nc.psum_tensor("p1_pso", [128, 512], F32) as pso, \
             nc.psum_tensor("p1_psz", [1, 512], F32) as psz, \
             nc.sbuf_tensor("p1_pt0", [128, 512], BF16) as pt0, \
             nc.sbuf_tensor("p1_pt1", [128, 512], BF16) as pt1, \
             nc.sbuf_tensor("p1_pt3", [128, 512], BF16) as pt3, \
             nc.sbuf_tensor("p1_invzbc", [128, 512], F32) as invzbc, \
             nc.sbuf_tensor("p1_dmask", [128, 4 * 512], BF16) as dmask:
            pss = [pss0, pss1]
            ptile = [pt0, pt1]

            sy = Sync(ctx, "pb")
            nc.gpsimd.dma_start(out=dmask[:], in_=d["diagmask"]).then_inc(cn, 16)
            nc.tensor.wait_ge(cn, NCON + 16)
            sy.dma("wk", nc.sync.dma_start(
                out=wqh[:].rearrange("p (c n) -> p c n", n=128), in_=wkv))
            sy.dma("wk", nc.sync.dma_start(
                out=qbf[:].rearrange("p (c n) -> p c n", n=128), in_=wvv))
            sy.wait("pe", "wk", 32)
            inst = None
            for tq in range(4):
                for c in range(ND):
                    inst = nc.tensor.matmul(psq[:, tq * 512:(tq + 1) * 512],
                                            wqh[:, c * 128:(c + 1) * 128],
                                            ln1[:, c * T + tq * 512: c * T + (tq + 1) * 512],
                                            start=(c == 0), stop=(c == ND - 1))
            sy.done("pe", inst)
            for tt in range(NT):
                sy.waitn("pe", "dv", 1)
                inst = None
                for c in range(ND):
                    inst = nc.tensor.matmul(pss[tt % 2][:, 0:128],
                                            ln1[:, c * T + tt * 128: c * T + (tt + 1) * 128],
                                            qbf[:, c * 128:(c + 1) * 128],
                                            start=(c == 0), stop=(c == ND - 1))
                sy.done("pe", inst)
                sy.waitn("dv", "pe")
                sy.done("dv", nc.vector.tensor_copy(vtm[:, tt * 128:(tt + 1) * 128],
                                                    pss[tt % 2][:, 0:128]))
            sy.wait("dv", "pe", 1)
            sy.done("dv", nc.vector.tensor_copy(qA[:], psq[:]))
            sy.waitn("dv", "dv")
            sy.done("dv", nc.vector.tensor_copy(ktile[:], qA[:]))
            sy.waitn("ac", "dv")
            sy.done("ac", nc.scalar.activation(out=qB[:], in_=qA[:], func=AF.Square))
            for tq in range(4):
                sy.waitn("pe", "ac")
                sy.done("pe", nc.tensor.matmul(psz[:], onec32[:],
                                               qB[:, tq * 512:(tq + 1) * 512],
                                               start=True, stop=True,
                                               skip_group_check=True))
                sy.waitn("ac", "pe")
                sy.done("ac", nc.scalar.activation(out=nrow[:, tq * 512:(tq + 1) * 512],
                                                   in_=psz[0:1, :], func=AF.Sqrt))
            sy.waitn("dv", "ac")
            sy.done("dv", nc.vector.tensor_reduce(out=nkmax[:], in_=nrow, axis=AX.X,
                                                  op=OP.max))
            sy.waitn("pe", "dv")
            sy.free()
            ctx.barrier()

            sy = Sync(ctx, "pc")
            og = [0, 4, 12, 24]
            wslice = lambda hh: wqv[:, :, bass.ts(hh, 128)]
            sy.dma("wq", nc.sync.dma_start(
                out=wqh[:].rearrange("p (c n) -> p c n", n=128), in_=wslice(0)))

            def s_group(g, o, kt, qc, par, diag):
                sy.waitn("pe", "ac", 1)
                nc.tensor.matmul(pss[par][:], ktile[:, kt * 128:(kt + 1) * 128],
                                 qbf[:, qc * 512:(qc + 1) * 512], start=True, stop=False)
                sy.done("pe", nc.tensor.matmul(pss[par][:], oner[:],
                                               negub[0:1, qc * 512:(qc + 1) * 512],
                                               start=False, stop=True))
                sy.waitn("ac", "pe")
                sy.waitn("ac", "pe", 3)
                sy.done("ac", nc.scalar.activation(out=ptile[par][:], in_=pss[par][:],
                                                   func=AF.Exp, scale=f32c(SC_ATT)))
                if diag is not None:
                    sy.waitn("dv", "ac")
                    sy.done("dv", nc.vector.tensor_tensor(
                        out=ptile[par][:], in0=ptile[par][:],
                        in1=dmask[:, diag * 512:(diag + 1) * 512], op=OP.mult))

            def pv_group(g, o, kt, qc, par, first, diag):
                if diag is not None:
                    sy.waitn("pe", "dv")
                else:
                    sy.waitn("pe", "ac")
                nc.tensor.matmul(pso[:], vtm[:, kt * 128:(kt + 1) * 128], ptile[par][:],
                                 start=first, stop=False, skip_group_check=True)
                sy.done("pe", nc.tensor.matmul(psz[:], onec[:], ptile[par][:],
                                               start=first, stop=False,
                                               skip_group_check=True))

            def head_body(h):
                sy.waitn("pe", "wq")
                sy.waitn("pe", "dv", 38)
                inst = None
                for tq in range(4):
                    for c in range(ND):
                        inst = nc.tensor.matmul(
                            psq[:, tq * 512:(tq + 1) * 512],
                            wqh[:, c * 128:(c + 1) * 128],
                            ln1[:, c * T + tq * 512: c * T + (tq + 1) * 512],
                            start=(c == 0), stop=(c == ND - 1))
                sy.done("pe", inst)
                sy.waitn("sp", "pe")
                sy.dma("wq", nc.sync.dma_start(
                    out=wqh[:].rearrange("p (c n) -> p c n", n=128), in_=wslice(h + 1)))
                sy.waitn("po", "dv", 34)
                hx = bass.ts(h, 1)
                nc.gpsimd.tensor_copy(csst[:, 0:1], cosF[:, hx])
                sy.done("po", nc.gpsimd.tensor_copy(csst[:, 1:2], sinF[:, hx]))
                sy.waitn("dv", "pe")
                sy.waitn("dv", "ac", 44)
                sy.done("dv", nc.vector.tensor_copy(qA[:], psq[:]))
                sy.waitn("dv", "dv")
                sy.done("dv", nc.vector.tensor_copy(qC[0:64, :], qA[64:128, :]))
                sy.done("dv", nc.vector.tensor_copy(qC[64:128, :], qA[0:64, :]))
                sy.waitn("dv", "po")
                sy.done("dv", nc.vector.tensor_scalar_mul(qB[:], qA[:], csst[:, 0:1]))
                sy.waitn("dv", "dv")
                sy.done("dv", nc.vector.scalar_tensor_tensor(out=qB[:], in0=qC[:],
                                                             scalar=csst[:, 1:2], in1=qB[:],
                                                             op0=OP.mult, op1=OP.add))
                sy.waitn("dv", "dv")
                sy.waitn("dv", "pe", 1)
                sy.done("dv", nc.vector.tensor_copy(qbf[:], qB[:]))
                sy.waitn("ac", "dv")
                sy.done("ac", nc.scalar.activation(out=qA[:], in_=qB[:], func=AF.Square))
                for tq in range(4):
                    sy.waitn("pe", "ac")
                    sy.waitn("pe", "dv", 9)
                    sy.done("pe", nc.tensor.matmul(psz[:], onec32[:],
                                                   qA[:, tq * 512:(tq + 1) * 512],
                                                   start=True, stop=True,
                                                   skip_group_check=True))
                    sy.waitn("ac", "pe")
                    sy.done("ac", nc.scalar.activation(
                        out=nrow[:, tq * 512:(tq + 1) * 512], in_=psz[0:1, :],
                        func=AF.Sqrt))
                sy.waitn("dv", "ac")
                sy.waitn("dv", "pe", 5)
                sy.done("dv", nc.vector.tensor_scalar(out=negub[0:1, :], in0=nrow,
                                                      scalar1=nkmax[0:1, 0:1],
                                                      scalar2=-1.0,
                                                      op0=OP.mult, op1=OP.mult))
                for qc in range(4):
                    sy.waitn("pe", "dv")
                    g = 0
                    for kt in range(4 * qc + 4):
                        diag = kt - 4 * qc if kt >= 4 * qc else None
                        o = og[qc] + (4 + kt if diag is None else diag)
                        par = g % 2
                        s_group(g, o, kt, qc, par, diag)
                        pv_group(g, o, kt, qc, par, first=(g == 0), diag=diag)
                        g += 1
                    sy.waitn("dv", "pe")
                    sy.done("dv", nc.vector.tensor_copy(zsb, psz[0:1, :]))
                    sy.waitn("dv", "dv")
                    sy.done("dv", nc.vector.reciprocal(invz, zsb))
                    sy.waitn("pe", "dv")
                    sy.waitn("pe", "ac")
                    sy.done("pe", nc.tensor.matmul(pss[0][:], oner32[:], invz,
                                                   start=True, stop=True,
                                                   skip_group_check=True))
                    sy.waitn("dv", "pe")
                    sy.done("dv", nc.vector.tensor_copy(invzbc[:], pss[0][:]))
                    sy.waitn("dv", "dv")
                    sy.waitn("dv", "po")
                    sy.done("dv", nc.vector.tensor_tensor(out=pt3[:], in0=pso[:],
                                                          in1=invzbc[:], op=OP.mult))
                    sy.waitn("po", "dv")
                    sy.done("po", nc.gpsimd.tensor_copy(
                        Ofm[:, bass.ts(4 * h + qc, 512)], pt3[:]))

            pairs = [("pe", "wq"), ("pe", "dv"), ("pe", "ac"), ("sp", "pe"),
                     ("dv", "pe"), ("dv", "ac"), ("dv", "dv"), ("dv", "po"),
                     ("ac", "dv"), ("ac", "pe"), ("ac", "ac"), ("po", "dv"),
                     ("po", "pe")]
            sy.begin_loop(pairs, {"wq": 16, "pe": 89, "dv": 39, "ac": 45, "po": 5})
            with ctx.fori(H, [PE, ACT, DVE, SP, Pool]) as h:
                sy.iter_start(h)
                head_body(h)
            sy.end_loop(H)
            nc.sync.wait_ge(sy.p["wq"], 16 * (H + 1))
            sy.free()
            ctx.barrier()

            sy = Sync(ctx, "pd")
            sy.dma("wo", nc.sync.dma_start(
                out=ln1[:].rearrange("p (c n) -> p c n", n=2048), in_=wov))
            nc.tensor.wait_ge(sy.p["wo"], 16)

            def p1d_tile(u, par):
                xq = ("xa", "xb")[par]
                oq = ("oa", "ob")[par]
                sy.waitn("sp", "dv", 1)
                sy.dma(xq, nc.sync.dma_start(out=xw[par][:], in_=x0v[2 * u + par]))
                ostg = (qbf, ktile)[par]
                sy.waitn("po", "pe", 4)
                ov = Ofm[:].rearrange("p (g t) -> p g t", t=T)
                sy.done("po", nc.gpsimd.tensor_copy(
                    ostg[:].rearrange("p (g n) -> p g n", n=128),
                    ov[:, :, bass.ts(2 * u + par, 128)]))
                sy.waitn("pe", "po")
                sy.waitn("pe", "dv")
                for dp in range(4):
                    inst = None
                    for hh in range(H):
                        inst = nc.tensor.matmul(
                            psq[:, dp * 512:(dp + 1) * 512],
                            ostg[:, hh * 128:(hh + 1) * 128],
                            ln1[:, hh * 2048 + dp * 512: hh * 2048 + (dp + 1) * 512],
                            start=(hh == 0), stop=(hh == H - 1))
                    sy.done("pe", inst)
                sy.waitn("dv", "pe")
                sy.waitn("dv", xq)
                sy.waitn("dv", oq)
                x1b = (qA, qB)[par]
                sy.done("dv", nc.vector.tensor_tensor(out=x1b[:], in0=psq[:],
                                                      in1=xw[par][:], op=OP.add))
                sy.waitn("ac", "dv")
                sy.dma(oq, nc.scalar.dma_start(out=x1v[2 * u + par], in_=x1b[:]))

            sy.begin_loop([("sp", "dv"), ("po", "pe"), ("pe", "po"), ("pe", "dv"),
                           ("dv", "pe"), ("dv", "xa"), ("dv", "xb"), ("dv", "oa"),
                           ("dv", "ob"), ("ac", "dv")],
                          {"xa": 16, "xb": 16, "oa": 16, "ob": 16,
                           "pe": 8, "dv": 2, "po": 2})
            with ctx.fori(NT // 2, [PE, DVE, SP, ACT, Pool]) as u:
                sy.iter_start(u)
                p1d_tile(u, 0)
                p1d_tile(u, 1)
            sy.end_loop(NT // 2)
            nc.sync.wait_ge(sy.p["oa"], 16 * NT // 2)
            nc.sync.wait_ge(sy.p["ob"], 16 * NT // 2)
            sy.free()
        ctx.barrier()


EPS32 = float(np.finfo(np.float32).eps)


def emit_p2(ctx):
    nc, d = ctx.nc, ctx.d
    es = ExitStack()
    sb = lambda n, shp, dt=F32: es.enter_context(nc.sbuf_tensor(n, shp, dt))

    ident = sb("p2_ident", [128, 128], BF16)
    wc = sb("p2_wcT", [128, 16 * 2048], BF16)
    wcs = [sb(f"p2_wcs{i}", [128, 16 * 512], BF16) for i in range(2)]
    genes = sb("p2_genes", [128, 32 * 256], BF16)
    wd = sb("p2_wd", [128, 2 * 2048], BF16)
    wu = sb("p2_wu", [128, 2 * 2048], BF16)
    normw = sb("p2_normw", [128, D], BF16)
    invtemp = sb("p2_invtemp", [128, 1], F32)
    scalesc = sb("p2_scale", [128, 1], F32)
    stat = sb("p2_stat", [128, 12], F32)
    xw = [sb(f"p2_xw{i}", [128, D], F32) for i in range(2)]
    xn_tm = sb("p2_xntm", [128, D], F32)
    xnbf = sb("p2_xnbf", [128, D], BF16)
    xnfm = sb("p2_xnfm", [128, ND * 128], BF16)
    logits = sb("p2_logits", [128, 4096], F32)
    lrep = sb("p2_lrep", [128, 4096], F32)
    down = lrep[:, 0:2048]
    up = lrep[:, 2048:4096]
    mxall = sb("p2_mx", [128, 32], F32)
    negm = sb("p2_negm", [128, 1], F32)
    zrow = sb("p2_z", [128, 2], F32)
    ebuf = sb("p2_e", [128, 4096], BF16)
    wT = ebuf
    wbuf = lrep[:, 0:2048].bitcast(BF16)
    exprs = sb("p2_expr", [128, 2 * 128], BF16)

    x1v = d["x1"].rearrange("(t p) n -> t p n", p=128)
    x2v = d["x2"].rearrange("(t p) n -> t p n", p=128)
    wcv = d["wc_bf"].rearrange("(c p) n -> p c n", p=128)
    gnv = d["genes_bf"].rearrange("(g p) n -> p g n", p=128)
    wdv = d["wd_bf"].rearrange("(c p) n -> p c n", p=128)
    wuv = d["wu_bf"].rearrange("(c p) n -> p c n", p=128)

    with es, \
         nc.psum_tensor("p2_ptr0", [128, 1024], BF16) as ptr0, \
         nc.psum_tensor("p2_ptr1", [128, 1024], BF16) as ptr1, \
         nc.psum_tensor("p2_plg0", [128, 512], F32) as plg0, \
         nc.psum_tensor("p2_plg1", [128, 512], F32) as plg1, \
         nc.psum_tensor("p2_plg2", [128, 512], F32) as plg2, \
         nc.psum_tensor("p2_plg3", [128, 512], F32) as plg3, \
         nc.psum_tensor("p2_pex", [128, 256], F32) as pex:
        ptr = [ptr0, ptr1]
        plg = [plg0, plg1, plg2, plg3]
        sy = Sync(ctx, "p2")
        cn = ctx.sem("p2_cn")
        nc.gpsimd.dma_start(out=ident[:], in_=d["identity"]).then_inc(cn, 16)
        nc.gpsimd.dma_start(out=wc[:].rearrange("p (c n) -> p c n", n=2048),
                            in_=wcv[:, :, 0:2048]).then_inc(cn, 16)
        nc.gpsimd.dma_start(out=genes[:].rearrange("p (g n) -> p g n", n=256),
                            in_=gnv).then_inc(cn, 16)
        nc.gpsimd.dma_start(out=wd[:].rearrange("p (c n) -> p c n", n=2048),
                            in_=wdv).then_inc(cn, 16)
        nc.gpsimd.dma_start(out=wu[:].rearrange("p (c n) -> p c n", n=2048),
                            in_=wuv).then_inc(cn, 16)
        nc.gpsimd.dma_start(out=normw[:], in_=d["l0_normw_bc"]).then_inc(cn, 16)
        nc.gpsimd.dma_start(out=invtemp[:], in_=d["l0_temp_bc"]).then_inc(cn, 16)
        nc.gpsimd.dma_start(out=scalesc[:], in_=d["l0_scale_bc"]).then_inc(cn, 16)
        NCN = 8 * 16
        for e in (nc.vector, nc.scalar, nc.tensor, nc.sync):
            e.wait_ge(cn, NCN)
        dvp = ctx.sem("p2_pre")
        nc.vector.tensor_scalar_max(invtemp[:], invtemp[:], 0.1).then_inc(dvp, 1)
        nc.vector.wait_ge(dvp, 1)
        nc.vector.reciprocal(invtemp[:], invtemp[:]).then_inc(dvp, 1)
        for e in (nc.scalar, nc.tensor):
            e.wait_ge(dvp, 2)
        sy.dma("xa", nc.sync.dma_start(out=xw[0][:], in_=x1v[0]))

        def tile(u, par, prefetch=True):
            xq = ("xa", "xb")[par]
            oq = ("oa", "ob")[par]
            tt = 2 * u + par
            b = dict(sy.n)
            sy.waitn("pe", "dv")
            sy.waitn("dv", "pe")
            sy.waitn("ac", "dv")
            if prefetch:
                sy.waitn("sp", ("ob", "oa")[par])
                sy.dma(("xb", "xa")[par],
                       nc.sync.dma_start(out=xw[1 - par][:], in_=x1v[tt + 1]))
            sy.waitn("dv", xq)
            sy.waitn("ac", xq)
            sy.waitn("dv", "dv")
            sy.done("dv", nc.vector.tensor_reduce(out=stat[:, 0:1], in_=xw[par][:],
                                                  axis=AX.X, op=OP.add))
            sy.waitn("ac", "ac")
            sy.done("ac", nc.scalar.activation(out=lrep[:, 0:D], in_=xw[par][:],
                                               func=AF.Square, accum_out=stat[:, 1:2]))
            sy.waitn("dv", "dv")
            sy.done("dv", nc.vector.tensor_scalar_mul(stat[:, 2:3], stat[:, 0:1],
                                                      float(1.0 / D)))
            sy.waitn("dv", "dv")
            sy.done("dv", nc.vector.tensor_tensor(out=stat[:, 3:4], in0=stat[:, 2:3],
                                                  in1=stat[:, 2:3], op=OP.mult))
            sy.waitn("dv", "dv")
            sy.waitn("dv", "ac")
            sy.done("dv", nc.vector.scalar_tensor_tensor(out=stat[:, 4:5], in0=stat[:, 1:2],
                                                         scalar=float(1.0 / D),
                                                         in1=stat[:, 3:4],
                                                         op0=OP.mult, op1=OP.subtract))
            sy.waitn("ac", "dv")
            sy.done("ac", nc.scalar.activation(out=stat[:, 5:6], in_=stat[:, 4:5],
                                               func=AF.Sqrt, bias=f32c(1e-5)))
            sy.waitn("dv", "ac")
            sy.done("dv", nc.vector.reciprocal(stat[:, 6:7], stat[:, 5:6]))
            sy.waitn("dv", "dv")
            sy.done("dv", nc.vector.tensor_tensor(out=stat[:, 7:8], in0=stat[:, 4:5],
                                                  in1=stat[:, 6:7], op=OP.mult))
            sy.waitn("dv", "dv")
            sy.done("dv", nc.vector.tensor_tensor(out=stat[:, 8:9], in0=stat[:, 7:8],
                                                  in1=stat[:, 6:7], op=OP.mult))
            sy.waitn("ac", "dv")
            sy.done("ac", nc.scalar.activation(out=stat[:, 9:10], in_=stat[:, 8:9],
                                               func=AF.Sqrt, bias=f32c(EPS32)))
            sy.waitn("dv", "ac")
            sy.done("dv", nc.vector.reciprocal(stat[:, 10:11], stat[:, 9:10]))
            sy.waitn("dv", "dv")
            sy.done("dv", nc.vector.tensor_tensor(out=stat[:, 10:11], in0=stat[:, 10:11],
                                                  in1=stat[:, 6:7], op=OP.mult))
            sy.waitn("dv", "dv")
            sy.done("dv", nc.vector.tensor_tensor(out=stat[:, 11:12], in0=stat[:, 2:3],
                                                  in1=stat[:, 10:11], op=OP.mult))
            sy.waitn("dv", "dv")
            sy.done("dv", nc.vector.tensor_scalar(out=xn_tm[:], in0=xw[par][:],
                                                  scalar1=stat[:, 10:11],
                                                  scalar2=stat[:, 11:12],
                                                  op0=OP.mult, op1=OP.subtract))
            sy.waitn("dv", "dv")
            sy.done("dv", nc.vector.tensor_tensor(out=xn_tm[:], in0=xn_tm[:],
                                                  in1=normw[:], op=OP.mult))
            sy.waitn("dv", "dv")
            r_xnbf = sy.done("dv", nc.vector.tensor_copy(xnbf[:], xn_tm[:]))
            for half in range(2):
                sy.wait("pe", "dv", r_xnbf)
                inst = None
                for c8 in range(8):
                    c = half * 8 + c8
                    inst = nc.tensor.transpose(ptr[half][:, c8 * 128:(c8 + 1) * 128],
                                               xnbf[:, c * 128:(c + 1) * 128], ident[:])
                sy.done("pe", inst)
            r_trev = []
            for half in range(2):
                sy.waitn("dv", "pe")
                src2 = ptr[half][:].rearrange("p (c n) -> p c n", n=128)
                r_trev.append(sy.done("dv", nc.vector.tensor_copy(
                    xnfm[:].rearrange("p (c n) -> p c n", n=128)[:, half * 8:(half + 1) * 8, :],
                    src2)))
            sy.waitn("sp", "pe")
            for gc in (4, 5):
                sy.dma(("wca", "wcb")[gc % 2], nc.sync.dma_start(
                    out=wcs[gc % 2][:].rearrange("p (c n) -> p c n", n=512),
                    in_=wcv[:, :, gc * 512:(gc + 1) * 512]))
            r_lg = []
            r_mm = []
            for gc in range(8):
                sy.waitn("pe", "dv")
                if gc >= 4:
                    sy.wait("pe", "dv", r_lg[gc - 4])
                    sy.waitn("pe", ("wca", "wcb")[gc % 2])
                inst = None
                for c in range(ND):
                    if gc < 4:
                        wsl = wc[:, c * 2048 + gc * 512: c * 2048 + (gc + 1) * 512]
                    else:
                        wsl = wcs[gc % 2][:, c * 512:(c + 1) * 512]
                    inst = nc.tensor.matmul(
                        plg[gc % 4][:], xnfm[:, c * 128:(c + 1) * 128], wsl,
                        start=(c == 0), stop=(c == ND - 1))
                r_mm.append(sy.done("pe", inst))
                if gc in (4, 5):
                    sy.wait("sp", "pe", r_mm[gc])
                    g2 = gc + 2
                    sy.dma(("wca", "wcb")[g2 % 2], nc.sync.dma_start(
                        out=wcs[g2 % 2][:].rearrange("p (c n) -> p c n", n=512),
                        in_=wcv[:, :, g2 * 512:(g2 + 1) * 512]))
                sy.waitn("dv", "pe")
                r_lg.append(sy.done("dv", nc.vector.tensor_scalar_mul(
                    logits[:, gc * 512:(gc + 1) * 512], plg[gc % 4][:], invtemp[:, 0:1])))
            sy.waitn("dv", "dv")
            sy.done("dv", nc.vector.max(out=mxall[:, 0:8], in_=logits[:]))
            sy.waitn("dv", "dv")
            sy.done("dv", nc.vector.match_replace(out=lrep[:], in_to_replace=mxall[:, 0:8],
                                                  in_values=logits[:], imm_value=-1e30))
            for rnd in range(1, 4):
                sy.waitn("dv", "dv")
                sy.done("dv", nc.vector.max(out=mxall[:, 8 * rnd:8 * rnd + 8], in_=lrep[:]))
                if rnd < 3:
                    sy.waitn("dv", "dv")
                    sy.done("dv", nc.vector.match_replace(
                        out=lrep[:], in_to_replace=mxall[:, 8 * rnd:8 * rnd + 8],
                        in_values=lrep[:], imm_value=-1e30))
            sy.waitn("dv", "dv")
            sy.done("dv", nc.vector.tensor_scalar_mul(negm[:, 0:1], mxall[:, 0:1], -1.0))
            sy.waitn("ac", "dv")
            r_exp = sy.done("ac", nc.scalar.activation(out=ebuf[:], in_=logits[:],
                                                       func=AF.Exp, bias=negm[:, 0:1]))
            sy.wait("dv", "ac", r_exp)
            r_w = sy.done("dv", nc.vector.scalar_tensor_tensor(
                out=wbuf, in0=logits[:], scalar=mxall[:, 31:32], in1=ebuf[:],
                op0=OP.is_ge, op1=OP.mult, accum_out=zrow[:, 0:1]))
            sy.waitn("dv", "dv")
            r_invz = sy.done("dv", nc.vector.reciprocal(zrow[:, 1:2], zrow[:, 0:1]))
            r_wtev = []
            for g4 in range(4):
                sy.wait("pe", "dv", r_w)
                if g4 >= 2:
                    sy.wait("pe", "dv", r_wtev[g4 - 2])
                inst = None
                for c8 in range(8):
                    c = g4 * 8 + c8
                    inst = nc.tensor.transpose(ptr[g4 % 2][:, c8 * 128:(c8 + 1) * 128],
                                               wbuf[:, c * 128:(c + 1) * 128], ident[:])
                sy.done("pe", inst)
                sy.waitn("dv", "pe")
                src2 = ptr[g4 % 2][:].rearrange("p (c n) -> p c n", n=128)
                r_wtev.append(sy.done("dv", nc.vector.tensor_copy(
                    wT[:].rearrange("p (c n) -> p c n", n=128)[:, g4 * 8:(g4 + 1) * 8, :],
                    src2)))
            for ch in range(2):
                sy.waitn("pe", "dv")
                inst = None
                for g in range(32):
                    inst = nc.tensor.matmul(
                        pex[:, ch * 128:(ch + 1) * 128],
                        genes[:, g * 256 + ch * 128: g * 256 + (ch + 1) * 128],
                        wT[:, g * 128:(g + 1) * 128],
                        start=(g == 0), stop=(g == 31))
                sy.done("pe", inst)
            sy.waitn("dv", "pe")
            r_expr = sy.done("dv", nc.vector.tensor_copy(exprs[:], pex[:]))
            for dp in range(4):
                sy.wait("pe", "dv", r_expr)
                inst = None
                for ch in range(2):
                    inst = nc.tensor.matmul(
                        plg[dp][:], exprs[:, ch * 128:(ch + 1) * 128],
                        wd[:, ch * 2048 + dp * 512: ch * 2048 + (dp + 1) * 512],
                        start=(ch == 0), stop=(ch == 1))
                sy.done("pe", inst)
                sy.waitn("ac", "pe")
                sy.wait("ac", "dv", r_invz)
                sy.done("ac", nc.scalar.activation(out=down[:, dp * 512:(dp + 1) * 512],
                                                   in_=plg[dp][:], func=AF.Tanh,
                                                   scale=zrow[:, 1:2]))
            for dp in range(4):
                sy.waitn("pe", "ac")
                inst = None
                for ch in range(2):
                    inst = nc.tensor.matmul(
                        plg[dp][:], exprs[:, ch * 128:(ch + 1) * 128],
                        wu[:, ch * 2048 + dp * 512: ch * 2048 + (dp + 1) * 512],
                        start=(ch == 0), stop=(ch == 1))
                sy.done("pe", inst)
                sy.waitn("dv", "pe")
                sy.done("dv", nc.vector.tensor_scalar_mul(
                    up[:, dp * 512:(dp + 1) * 512], plg[dp][:], zrow[:, 1:2]))
            sy.waitn("dv", "dv")
            sy.waitn("dv", "ac")
            sy.done("dv", nc.vector.tensor_tensor(out=down[:], in0=down[:], in1=xn_tm[:],
                                                  op=OP.mult))
            sy.waitn("dv", "dv")
            sy.done("dv", nc.vector.tensor_tensor(out=down[:], in0=down[:], in1=up[:],
                                                  op=OP.mult))
            sy.waitn("dv", "dv")
            r_x2 = sy.done("dv", nc.vector.scalar_tensor_tensor(
                out=xw[par][:], in0=down[:], scalar=scalesc[:, 0:1], in1=xw[par][:],
                op0=OP.mult, op1=OP.add))
            sy.wait("ac", "dv", r_x2)
            sy.dma(oq, nc.scalar.dma_start(out=x2v[tt], in_=xw[par][:]))

        N_DV, N_AC, N_PE = 45, 8, 24
        pairs = [("sp", "oa"), ("sp", "ob"), ("sp", "pe"),
                 ("dv", "xa"), ("dv", "xb"), ("ac", "xa"), ("ac", "xb"),
                 ("dv", "dv"), ("dv", "ac"), ("dv", "pe"),
                 ("ac", "dv"), ("ac", "ac"), ("ac", "pe"), ("pe", "dv"), ("pe", "ac"),
                 ("pe", "wca"), ("pe", "wcb")]
        sy.begin_loop(pairs, {"xa": 16, "xb": 16, "oa": 16, "ob": 16,
                              "wca": 2 * 32, "wcb": 2 * 32,
                              "dv": 2 * N_DV, "ac": 2 * N_AC, "pe": 2 * N_PE})
        with ctx.fori(NT // 2 - 1, [PE, DVE, SP, ACT]) as u:
            sy.iter_start(u)
            tile(u, 0)
            tile(u, 1)
        sy.end_loop(NT // 2 - 1)
        tile(NT // 2 - 1, 0)
        tile(NT // 2 - 1, 1, prefetch=False)
        nc.sync.wait_ge(sy.p["oa"], 16 * NT // 2)
        nc.sync.wait_ge(sy.p["ob"], 16 * NT // 2)
        sy.free()
    ctx.barrier()


def emit_p3(ctx):
    nc, d = ctx.nc, ctx.d
    es = ExitStack()
    sb = lambda n, shp, dt=F32: es.enter_context(nc.sbuf_tensor(n, shp, dt))

    ident = sb("p3_ident", [128, 128], BF16)
    ident32 = sb("p3_id32", [128, 128], F32)
    onesr32 = sb("p3_oner32", [1, 128], F32)
    stat = sb("p3_stat", [128, 8], F32)
    tri = sb("p3_tri", [128, 128], F32)
    es_bc = ExitStack()
    sb2 = lambda n, shp, dt: es_bc.enter_context(nc.sbuf_tensor(n, shp, dt))
    Bfm = sb2("p3_Bfm", [128, T], BF16)
    B_tm = sb2("p3_Btm", [128, T], BF16)
    Cbf = sb2("p3_Cbf", [128, T], BF16)
    Cfm32 = sb2("p3_C32", [128, T], F32)
    dtf = sb2("p3_dtf", [64, T], F32)
    cumf = sb2("p3_cumf", [64, T], F32)
    Acol = sb2("p3_Acol", [64, 2], F32)
    dtb = sb2("p3_dtb", [64, 1], F32)
    zer64 = sb2("p3_zer64", [64, 128], F32)
    convb = sb2("p3_convb", [128, NCC], F32)
    cbst = sb2("p3_cbst", [128, 1], F32)

    x2v = d["x2"].rearrange("(t p) n -> t p n", p=128)
    x3v = d["x3"].rearrange("(t p) n -> t p n", p=128)
    wzv = d["wz_bf"].rearrange("(c p) n -> p c n", p=128)
    wxv = d["wxbc_bf"].rearrange("(c p) n -> p c n", p=128)
    wdtv = d["wdt_bf"].rearrange("(c p) n -> p c n", p=128)
    wov = d["wout_bf"].rearrange("(c p) n -> p c n", p=128)
    cdv = d["convdiag_bf"].rearrange("(j c p) n -> p j c n", c=NCC, p=128)
    zsv = d["z_scr"].rearrange("(t p) n -> t p n", p=128)
    xtv = d["xtm_scr"].rearrange("(t p) n -> t p n", p=128)
    ynv = d["ynorm_scr"].rearrange("(c p) t -> p c t", p=128)
    cumd = d["cum_scr"]

    with es:
        cn = ctx.sem("p3cn")
        ncn = 0
        for dst, src in ((ident[:], d["identity"]), (ident32[:], d["identity"]),
                         (onesr32[:], d["ones_row"]), (tri[:], d["tri01"]),
                         (zer64[:], d["diagmask"][0:64, 1536:1664]),
                         (convb[:], d["conv_b_fm"]),
                         (Acol[:, 0:1], d["A_log"]), (dtb[:], d["dt_bias"])):
            nc.gpsimd.dma_start(out=dst, in_=src).then_inc(cn, 16)
            ncn += 16
        for e in (nc.vector, nc.scalar, nc.tensor, nc.sync, nc.gpsimd):
            e.wait_ge(cn, ncn)

        with nc.sbuf_tensor("p3_ln3", [128, ND * T], BF16) as ln3:
            lnv = ln3[:].rearrange("p (c t) -> p c t", t=T)
            with nc.psum_tensor("p3a_ps0", [128, 1024], BF16) as pstr0, \
                 nc.psum_tensor("p3a_ps1", [128, 1024], BF16) as pstr1, \
                 nc.sbuf_tensor("p3a_lnb", [128, D], BF16) as lnbf, \
                 nc.sbuf_tensor("p3a_xw", [128, D], F32) as xw, \
                 nc.sbuf_tensor("p3a_sq", [128, D], F32) as sqs:
                pstr = [pstr0, pstr1]
                sy = Sync(ctx, "p3a")

                def p3a_tile(u):
                    sy.waitn("ac", "dv")
                    sy.dma("xa", nc.scalar.dma_start(out=xw[:], in_=x2v[u]))
                    sy.waitn("dv", "xa")
                    sy.waitn("ac", "xa")
                    sy.waitn("dv", "dv")
                    sy.done("dv", nc.vector.tensor_reduce(out=stat[:, 0:1], in_=xw[:],
                                                          axis=AX.X, op=OP.add))
                    sy.waitn("ac", "ac")
                    sy.done("ac", nc.scalar.activation(out=sqs[:], in_=xw[:],
                                                       func=AF.Square,
                                                       accum_out=stat[:, 1:2]))
                    sy.waitn("dv", "dv")
                    sy.done("dv", nc.vector.tensor_scalar_mul(stat[:, 2:3], stat[:, 0:1],
                                                              float(1.0 / D)))
                    sy.waitn("dv", "dv")
                    sy.done("dv", nc.vector.tensor_tensor(out=stat[:, 3:4],
                                                          in0=stat[:, 2:3],
                                                          in1=stat[:, 2:3], op=OP.mult))
                    sy.waitn("dv", "dv")
                    sy.waitn("dv", "ac")
                    sy.done("dv", nc.vector.scalar_tensor_tensor(
                        out=stat[:, 4:5], in0=stat[:, 1:2], scalar=float(1.0 / D),
                        in1=stat[:, 3:4], op0=OP.mult, op1=OP.subtract))
                    sy.waitn("ac", "dv")
                    sy.done("ac", nc.scalar.activation(out=stat[:, 5:6], in_=stat[:, 4:5],
                                                       func=AF.Sqrt, bias=f32c(1e-5)))
                    sy.waitn("dv", "ac")
                    sy.done("dv", nc.vector.reciprocal(stat[:, 6:7], stat[:, 5:6]))
                    sy.waitn("dv", "dv")
                    sy.done("dv", nc.vector.tensor_tensor(out=stat[:, 7:8],
                                                          in0=stat[:, 2:3],
                                                          in1=stat[:, 6:7], op=OP.mult))
                    sy.waitn("dv", "dv")
                    sy.waitn("dv", "pe")
                    sy.done("dv", nc.vector.tensor_scalar(out=lnbf[:], in0=xw[:],
                                                          scalar1=stat[:, 6:7],
                                                          scalar2=stat[:, 7:8],
                                                          op0=OP.mult, op1=OP.subtract))
                    for half in range(2):
                        sy.waitn("pe", "dv")
                        sy.waitn("pe", "dv", 8 - half)
                        inst = None
                        for c8 in range(8):
                            c = half * 8 + c8
                            inst = nc.tensor.transpose(
                                pstr[half][:, c8 * 128:(c8 + 1) * 128],
                                lnbf[:, c * 128:(c + 1) * 128], ident[:])
                        sy.done("pe", inst)
                    tsl = bass.ts(u, 128)
                    for half in range(2):
                        sy.waitn("dv", "pe", 1 - half)
                        src2 = pstr[half][:].rearrange("p (c n) -> p c n", n=128)
                        sy.done("dv", nc.vector.tensor_copy(
                            lnv[:, half * 8:(half + 1) * 8, tsl], src2))

                sy.begin_loop([("dv", "xa"), ("ac", "xa"), ("dv", "dv"), ("dv", "ac"),
                               ("ac", "dv"), ("ac", "ac"), ("pe", "dv"), ("dv", "pe")],
                              {"xa": 16, "dv": 9, "ac": 2, "pe": 2})
                with ctx.fori(NT, [PE, DVE, ACT]) as u:
                    sy.iter_start(u)
                    p3a_tile(u)
                sy.end_loop(NT)
                sy.free()
            ctx.barrier()

            with nc.psum_tensor("p3b_pz", [128, 2048], F32) as pzh, \
                 nc.sbuf_tensor("p3b_wz", [128, 16 * 2048], BF16) as wzr, \
                 nc.sbuf_tensor("p3b_ls", [128, 16 * 128], BF16) as lst, \
                 nc.sbuf_tensor("p3b_ze", [128, 2048], BF16) as zevh:
                sy = Sync(ctx, "p3b")

                def p3b_tile(u, half):
                    sy.waitn("dv", "pe")
                    sy.done("dv", nc.vector.tensor_copy(
                        lst[:].rearrange("p (c n) -> p c n", n=128),
                        lnv[:, :, bass.ts(u, 128)]))
                    sy.waitn("pe", "dv")
                    for g in range(4):
                        inst = None
                        for c in range(ND):
                            inst = nc.tensor.matmul(
                                pzh[:, g * 512:(g + 1) * 512],
                                lst[:, c * 128:(c + 1) * 128],
                                wzr[:, c * 2048 + g * 512: c * 2048 + (g + 1) * 512],
                                start=(c == 0), stop=(c == ND - 1))
                        sy.done("pe", inst)
                    sy.waitn("dv", "pe")
                    sy.waitn("dv", "zs")
                    sy.done("dv", nc.vector.tensor_copy(zevh[:], pzh[:]))
                    sy.waitn("ac", "dv")
                    sy.dma("zs", nc.scalar.dma_start(
                        out=zsv[u][:, half * 2048:(half + 1) * 2048], in_=zevh[:]))

                for half in range(2):
                    if half == 1:
                        sy.wait("sp", "pe", sy.n["pe"])
                    sy.dma("wz", nc.sync.dma_start(
                        out=wzr[:].rearrange("p (c n) -> p c n", n=2048),
                        in_=wzv[:, :, half * 2048:(half + 1) * 2048]))
                    sy.wait("pe", "wz", 16 * (half + 1))
                    sy.begin_loop([("pe", "dv"), ("dv", "pe"), ("dv", "zs"),
                                   ("ac", "dv")],
                                  {"pe": 4, "dv": 2, "zs": 16})
                    with ctx.fori(NT, [PE, DVE, ACT]) as u:
                        sy.iter_start(u)
                        p3b_tile(u, half)
                    sy.end_loop(NT)
                nc.sync.wait_ge(sy.p["zs"], sy.n["zs"])
                sy.free()
            ctx.barrier()

            with nc.psum_tensor("p3c_px", [128, 2048], F32) as px, \
                 nc.psum_tensor("p3c_pc", [128, 1024], F32) as pc, \
                 nc.psum_tensor("p3c_ptr", [128, 2048], BF16) as ptr, \
                 nc.sbuf_tensor("p3c_wx", [128, 16 * 128], BF16) as wxs, \
                 nc.sbuf_tensor("p3c_dg", [128, 4 * 128], BF16) as dgb, \
                 nc.sbuf_tensor("p3c_xraw", [128, 2051], BF16) as xraw, \
                 nc.sbuf_tensor("p3c_postc", [128, 2048], BF16) as postc, \
                 nc.sbuf_tensor("p3c_xtsb", [128, 2048], BF16) as xtsb:
                nc.gpsimd.dma_start(out=xraw[:, 0:3],
                                    in_=d["diagmask"][:, 1536:1539]).then_inc(cn, 16)
                ncn += 16
                for e in (nc.vector, nc.tensor):
                    e.wait_ge(cn, ncn)
                sy = Sync(ctx, "p3c")

                def p3c_core(cc):
                    sy.waitn("sp", "pe")
                    sy.dma("wx", nc.sync.dma_start(
                        out=wxs[:].rearrange("p (c n) -> p c n", n=128),
                        in_=wxv[:, :, bass.ts(cc, 128)]))
                    sy.dma("wx", nc.sync.dma_start(
                        out=dgb[:].rearrange("p (j n) -> p j n", n=128),
                        in_=cdv[:, :, cc, :]))
                    sy.waitn("po", "ac")
                    sy.done("po", nc.gpsimd.tensor_copy(cbst[:, 0:1],
                                                        convb[:, bass.ts(cc, 1)]))
                    sy.waitn("pe", "wx")
                    sy.waitn("pe", "dv", 1)
                    for tq in range(4):
                        inst = None
                        for c in range(ND):
                            inst = nc.tensor.matmul(
                                px[:, tq * 512:(tq + 1) * 512],
                                wxs[:, c * 128:(c + 1) * 128],
                                ln3[:, c * T + tq * 512: c * T + (tq + 1) * 512],
                                start=(c == 0), stop=(c == ND - 1))
                        sy.done("pe", inst)
                    sy.waitn("dv", "pe")
                    sy.done("dv", nc.vector.tensor_copy(xraw[:, 3:2051], px[:]))

                def p3c_conv(cc, silu_out):
                    for q in range(2):
                        sy.waitn("pe", "dv")
                        sy.waitn("pe", "ac")
                        sy.waitn("pe", "po")
                        inst = None
                        for j in range(4):
                            for th in range(2):
                                inst = nc.tensor.matmul(
                                    pc[:, th * 512:(th + 1) * 512],
                                    dgb[:, j * 128:(j + 1) * 128],
                                    xraw[:, (2 * q + th) * 512 + j:
                                         (2 * q + th) * 512 + j + 512],
                                    start=(j == 0), stop=(j == 3),
                                    skip_group_check=True)
                        sy.done("pe", inst)
                        sy.waitn("ac", "pe")
                        sy.done("ac", nc.scalar.activation(
                            out=silu_out(q), in_=pc[:], func=AF.Silu,
                            bias=cbst[:, 0:1]))

                def p3c_trans(src):
                    for g in range(2):
                        sy.waitn("pe", "ac")
                        sy.waitn("pe", "dv", 1)
                        inst = None
                        for i8 in range(8):
                            i = g * 8 + i8
                            inst = nc.tensor.transpose(
                                ptr[:, i * 128:(i + 1) * 128],
                                src[:, i * 128:(i + 1) * 128], ident[:])
                        sy.done("pe", inst)
                    sy.waitn("dv", "pe")

                def p3c_body(cc):
                    p3c_core(cc)
                    p3c_conv(cc, lambda q: postc[:, q * 1024:(q + 1) * 1024])
                    p3c_trans(postc)
                    sy.waitn("dv", "xs")
                    sy.done("dv", nc.vector.tensor_copy(xtsb[:], ptr[:]))
                    sy.waitn("ac", "dv")
                    xtp = d["xtm_scr"].rearrange("(t p) n -> p t n", p=128)
                    sy.dma("xs", nc.scalar.dma_start(
                        out=xtp[:, :, bass.ts(cc, 128)],
                        in_=xtsb[:].rearrange("p (t n) -> p t n", n=128)))

                sy.begin_loop([("sp", "pe"), ("po", "ac"), ("pe", "wx"),
                               ("pe", "dv"), ("pe", "ac"), ("pe", "po"), ("dv", "pe"),
                               ("dv", "xs"), ("ac", "pe"), ("ac", "dv")],
                              {"wx": 32, "po": 1, "pe": 8, "dv": 2,
                               "ac": 2, "xs": 16})
                with ctx.fori(32, [PE, DVE, ACT, SP, Pool]) as u:
                    sy.iter_start(u)
                    p3c_body(u)
                sy.end_loop(32)
                p3c_core(32)
                p3c_conv(32, lambda q: Bfm[:, q * 1024:(q + 1) * 1024])
                p3c_trans(Bfm)
                sy.done("dv", nc.vector.tensor_copy(B_tm[:], ptr[:]))
                p3c_core(33)
                p3c_conv(33, lambda q: Cfm32[:, q * 1024:(q + 1) * 1024])
                sy.waitn("dv", "ac")
                sy.done("dv", nc.vector.tensor_copy(Cbf[:], Cfm32[:]))
                nc.sync.wait_ge(sy.p["xs"], sy.n["xs"])
                sy.free()
            ctx.barrier()


            with nc.psum_tensor("p3d_pd", [128, 2048], F32) as pd, \
                 nc.sbuf_tensor("p3d_wdt", [128, 16 * 64], BF16) as wdtres, \
                 nc.sbuf_tensor("p3d_dtA", [64, T], F32) as dtA:
                sy = Sync(ctx, "p3d")
                nc.gpsimd.dma_start(
                    out=wdtres[:].rearrange("p (c n) -> p c n", n=64),
                    in_=wdtv).then_inc(cn, 16)
                ncn += 16
                nc.tensor.wait_ge(cn, ncn)
                sy.done("ac", nc.scalar.activation(out=Acol[:, 1:2], in_=Acol[:, 0:1],
                                                   func=AF.Exp))
                sy.waitn("dv", "ac")
                sy.done("dv", nc.vector.tensor_scalar_mul(Acol[:, 1:2], Acol[:, 1:2],
                                                          -1.0))
                for tq in range(4):
                    inst = None
                    for c in range(ND):
                        inst = nc.tensor.matmul(
                            pd[0:64, tq * 512:(tq + 1) * 512],
                            wdtres[:, c * 64:(c + 1) * 64],
                            ln3[:, c * T + tq * 512: c * T + (tq + 1) * 512],
                            start=(c == 0), stop=(c == ND - 1))
                    sy.done("pe", inst)
                    dsl = dtf[:, tq * 512:(tq + 1) * 512]
                    sy.waitn("dv", "pe")
                    sy.done("dv", nc.vector.tensor_scalar(
                        out=dsl, in0=pd[0:64, tq * 512:(tq + 1) * 512],
                        scalar1=dtb[:, 0:1], scalar2=30.0,
                        op0=OP.add, op1=OP.min))
                    sy.waitn("ac", "dv")
                    sy.done("ac", nc.scalar.activation(out=dsl, in_=dsl,
                                                       func=AF.Exp))
                    sy.waitn("dv", "ac")
                    sy.done("dv", nc.vector.tensor_scalar_add(dsl, dsl, 1.0))
                    sy.waitn("ac", "dv")
                    sy.done("ac", nc.scalar.activation(out=dsl, in_=dsl,
                                                       func=AF.Ln))
                sy.waitn("dv", "ac")
                sy.waitn("dv", "ac")
                sy.done("dv", nc.vector.tensor_scalar_mul(dtA[:], dtf[:],
                                                          Acol[:, 1:2]))
                for ch in range(16):
                    sy.waitn("dv", "dv")
                    sy.done("dv", nc.vector.tensor_tensor_scan(
                        out=cumf[:, ch * 128:(ch + 1) * 128],
                        data0=dtA[:, ch * 128:(ch + 1) * 128],
                        data1=zer64[:], initial=0.0, op0=OP.add, op1=OP.add))
                sy.waitn("sp", "dv")
                sy.dma("cs", nc.sync.dma_start(out=cumd, in_=cumf[:]))
                nc.sync.wait_ge(sy.p["cs"], 16)
                sy.free()
            ctx.barrier()

        es_e = ExitStack()
        sbe = lambda n, shp, dt: es_e.enter_context(nc.sbuf_tensor(n, shp, dt))
        with es_e:
            pbig = es_e.enter_context(nc.psum_tensor("p3e_pa", [128, 2048], F32))
            psB = es_e.enter_context(nc.psum_tensor("p3e_pb", [128, 2048], F32))
            S = sbe("p3e_S", [128, 4096], F32)
            EW = sbe("p3e_EW", [128, 2048], BF16)
            dtx = sbe("p3e_dtx", [128, 4096], BF16)
            dtxd = sbe("p3e_dtxd", [128, 4096], BF16)
            Xch = sbe("p3e_X", [128, 4096], BF16)
            zch = sbe("p3e_z", [128, 4096], BF16)
            ysb = sbe("p3e_y", [128, 4096], F32)
            cum_tm = sbe("p3e_ctm", [128, 64], F32)
            dt_tm = sbe("p3e_dttm", [128, 64], F32)
            expcum = sbe("p3e_ecum", [128, 64], F32)
            decf = sbe("p3e_decf", [128, 64], F32)
            dAl = sbe("p3e_dAl", [128, 64], F32)
            Gsb = sbe("p3e_G", [128, 128], BF16)
            cr0 = sbe("p3e_cr0", [1, 4096], F32)
            cr1 = sbe("p3e_cr1", [1, 4096], F32)
            culast = sbe("p3e_cl", [1, 64], F32)
            elast = sbe("p3e_el", [1, 64], F32)
            cumstg = sbe("p3e_cs64", [64, 128], F32)
            dtstg = sbe("p3e_ds64", [64, 128], F32)
            Bstg = sbe("p3e_Bst", [128, 128], BF16)
            Cstg = sbe("p3e_Cst", [128, 128], F32)
            Btmstg = sbe("p3e_Btst", [128, 128], BF16)
            Dbc = sbe("p3e_Dbc", [128, 4096], BF16)
            nw1 = sbe("p3e_nw", [128, 4096], BF16)
            crh = [cr0, cr1]
            nc.gpsimd.dma_start(out=S[:], in_=d["zeros4096"]).then_inc(cn, 16)
            nc.gpsimd.dma_start(out=Dbc[:], in_=d["D_bc"]).then_inc(cn, 16)
            nc.gpsimd.dma_start(out=nw1[:], in_=d["l1_normw_bc"]).then_inc(cn, 16)
            ncn += 48
            for e in (nc.vector, nc.scalar, nc.tensor, nc.sync):
                e.wait_ge(cn, ncn)
            sy = Sync(ctx, "p3e")

            def p3e_body(ch):
                sy.waitn("sp", "dv")
                sy.dma("xz", nc.sync.dma_start(out=Xch[:], in_=xtv[ch]))
                sy.waitn("sp", "yn")
                sy.dma("xz", nc.sync.dma_start(out=zch[:], in_=zsv[ch]))
                for hf in range(2):
                    sy.dma("cr", nc.sync.dma_start(
                        out=crh[hf][0:1, :].rearrange("o (h t) -> o h t", t=128),
                        in_=cumd[hf * 32:(hf + 1) * 32,
                                 bass.ts(ch, 128)].rearrange("h t -> () h t")))
                sy.waitn("ac", "pe")
                sy.done("ac", nc.scalar.activation(out=cumstg[:], func=AF.Copy,
                                                   in_=cumf[:, bass.ts(ch, 128)]))
                sy.done("ac", nc.scalar.activation(out=dtstg[:], func=AF.Copy,
                                                   in_=dtf[:, bass.ts(ch, 128)]))
                sy.done("ac", nc.scalar.activation(out=Bstg[:], func=AF.Copy,
                                                   in_=Bfm[:, bass.ts(ch, 128)]))
                sy.done("ac", nc.scalar.activation(out=Cstg[:], func=AF.Copy,
                                                   in_=Cfm32[:, bass.ts(ch, 128)]))
                sy.done("ac", nc.scalar.activation(out=Btmstg[:], func=AF.Copy,
                                                   in_=B_tm[:, bass.ts(ch, 128)]))
                sy.waitn("pe", "ac")
                sy.waitn("pe", "dv")
                sy.done("pe", nc.tensor.matmul(pbig[:, 0:128], Bstg[:],
                                               Cbf[:, bass.ts(ch, 128)],
                                               start=True, stop=True))
                sy.done("pe", nc.tensor.transpose(pbig[:, 512:576], cumstg[:],
                                                  ident32[0:64, 0:64]))
                sy.done("pe", nc.tensor.transpose(pbig[:, 1024:1088], dtstg[:],
                                                  ident32[0:64, 0:64]))
                sy.waitn("dv", "pe")
                sy.done("dv", nc.vector.tensor_tensor(out=Gsb[:], in0=pbig[:, 0:128],
                                                      in1=tri[:], op=OP.mult))
                sy.done("dv", nc.vector.tensor_copy(cum_tm[:], pbig[:, 512:576]))
                sy.done("dv", nc.vector.tensor_copy(dt_tm[:], pbig[:, 1024:1088]))
                sy.waitn("pe", "dv")
                sy.done("pe", nc.tensor.matmul(psB[0:1, 64:128], ident32[:, 127:128],
                                               cum_tm[:], start=True, stop=True,
                                               skip_group_check=True))
                sy.waitn("dv", "pe")
                sy.done("dv", nc.vector.tensor_copy(culast[:], psB[0:1, 64:128]))
                sy.waitn("ac", "dv")
                sy.done("ac", nc.scalar.activation(out=expcum[:], in_=cum_tm[:],
                                                   func=AF.Exp))
                sy.done("ac", nc.scalar.activation(out=elast[:], in_=culast[:],
                                                   func=AF.Exp))
                sy.waitn("pe", "ac")
                sy.done("pe", nc.tensor.matmul(pbig[:, 1536:1600], onesr32[:],
                                               culast[:], start=True, stop=True,
                                               skip_group_check=True))
                sy.done("pe", nc.tensor.matmul(psB[:, 0:64], onesr32[:],
                                               elast[:], start=True, stop=True,
                                               skip_group_check=True))
                sy.waitn("dv", "pe")
                sy.done("dv", nc.vector.tensor_tensor(out=decf[:],
                                                      in0=pbig[:, 1536:1600],
                                                      in1=cum_tm[:], op=OP.subtract))
                sy.done("dv", nc.vector.tensor_copy(dAl[:], psB[:, 0:64]))
                sy.waitn("ac", "dv")
                sy.done("ac", nc.scalar.activation(out=decf[:], in_=decf[:],
                                                   func=AF.Exp))
                sy.waitn("dv", "xz")
                sy.waitn("dv", "ac")
                d3 = dtx[:].rearrange("p (h q) -> p h q", q=64)
                x3_ = Xch[:].rearrange("p (h q) -> p h q", q=64)
                sy.done("dv", nc.vector.tensor_tensor(
                    out=d3, in0=x3_,
                    in1=dt_tm[:].unsqueeze(2).to_broadcast([128, 64, 64]),
                    op=OP.mult))
                dd3 = dtxd[:].rearrange("p (h q) -> p h q", q=64)
                sy.waitn("dv", "dv")
                sy.done("dv", nc.vector.tensor_tensor(
                    out=dd3, in0=d3,
                    in1=decf[:].unsqueeze(2).to_broadcast([128, 64, 64]),
                    op=OP.mult))
                for hf in range(2):
                    for qq in range(2):
                        q = 2 * hf + qq
                        sy.waitn("pe", "cr")
                        sy.waitn("pe", "ac")
                        sy.waitn("pe", "dv")
                        inst = None
                        for i in range(4):
                            inst = nc.tensor.matmul(
                                pbig[:, i * 512:(i + 1) * 512], onesr32[:],
                                crh[hf][0:1, qq * 2048 + i * 512:
                                        qq * 2048 + (i + 1) * 512],
                                start=True, stop=True)
                        sy.done("pe", inst)
                        sy.waitn("dv", "pe")
                        p3 = pbig[:].rearrange("p (h t) -> p h t", t=128)
                        sy.done("dv", nc.vector.tensor_tensor(
                            out=p3, in0=p3,
                            in1=cum_tm[:, q * 16:(q + 1) * 16].unsqueeze(2)
                                .to_broadcast([128, 16, 128]),
                            op=OP.subtract))
                        sy.waitn("dv", "dv")
                        sy.done("dv", nc.vector.tensor_scalar_min(pbig[:], pbig[:],
                                                                  0.0))
                        sy.waitn("ac", "dv")
                        sy.waitn("ac", "pe")
                        sy.done("ac", nc.scalar.activation(out=EW[:], in_=pbig[:],
                                                           func=AF.Exp))
                        sy.waitn("dv", "ac")
                        e3 = EW[:].rearrange("p (h t) -> p h t", t=128)
                        sy.done("dv", nc.vector.tensor_tensor(
                            out=e3, in0=e3,
                            in1=Gsb[:].unsqueeze(1).to_broadcast([128, 16, 128]),
                            op=OP.mult))
                        sy.waitn("pe", "dv")
                        inst = None
                        for hl in range(16):
                            h = q * 16 + hl
                            inst = nc.tensor.matmul(
                                psB[:, qq * 1024 + hl * 64:
                                    qq * 1024 + (hl + 1) * 64],
                                EW[:, hl * 128:(hl + 1) * 128],
                                dtx[:, h * 64:(h + 1) * 64],
                                start=True, stop=True, skip_group_check=True)
                        sy.done("pe", inst)
                    sy.waitn("pe", "ac")
                    inst = None
                    for i in range(4):
                        inst = nc.tensor.matmul(
                            pbig[:, i * 512:(i + 1) * 512], Cstg[:],
                            S[:, hf * 2048 + i * 512: hf * 2048 + (i + 1) * 512],
                            start=True, stop=True)
                    sy.done("pe", inst)
                    sy.waitn("dv", "pe")
                    ysl = ysb[:, hf * 2048:(hf + 1) * 2048].rearrange(
                        "p (h q) -> p h q", q=64)
                    pb3 = pbig[:].rearrange("p (h q) -> p h q", q=64)
                    sy.done("dv", nc.vector.tensor_tensor(
                        out=ysl, in0=pb3,
                        in1=expcum[:, hf * 32:(hf + 1) * 32].unsqueeze(2)
                            .to_broadcast([128, 32, 64]),
                        op=OP.mult))
                    sy.waitn("dv", "dv")
                    sy.done("dv", nc.vector.tensor_tensor(
                        out=ysb[:, hf * 2048:(hf + 1) * 2048],
                        in0=ysb[:, hf * 2048:(hf + 1) * 2048],
                        in1=psB[:], op=OP.add))
                for hf in range(2):
                    sy.waitn("dv", "pe")
                    sy.waitn("dv", "dv")
                    sy.done("dv", nc.vector.tensor_tensor(
                        out=EW[:], in0=Xch[:, hf * 2048:(hf + 1) * 2048],
                        in1=Dbc[:, hf * 2048:(hf + 1) * 2048], op=OP.mult))
                    sy.waitn("dv", "dv")
                    sy.done("dv", nc.vector.tensor_tensor(
                        out=ysb[:, hf * 2048:(hf + 1) * 2048],
                        in0=ysb[:, hf * 2048:(hf + 1) * 2048],
                        in1=EW[:], op=OP.add))
                sy.waitn("ac", "dv")
                sy.waitn("ac", "xz")
                sy.done("ac", nc.scalar.activation(out=zch[:], in_=zch[:],
                                                   func=AF.Silu))
                sy.waitn("dv", "ac")
                sy.done("dv", nc.vector.tensor_tensor(out=ysb[:], in0=ysb[:],
                                                      in1=zch[:], op=OP.mult))
                sy.waitn("ac", "dv")
                sy.done("ac", nc.scalar.activation(out=dtx[:], in_=ysb[:],
                                                   func=AF.Square,
                                                   accum_out=stat[:, 0:1]))
                sy.waitn("dv", "ac")
                sy.done("dv", nc.vector.tensor_scalar_mul(stat[:, 1:2], stat[:, 0:1],
                                                          float(1.0 / INTER)))
                sy.waitn("ac", "dv")
                sy.done("ac", nc.scalar.activation(out=stat[:, 2:3], in_=stat[:, 1:2],
                                                   func=AF.Sqrt, bias=f32c(1e-5)))
                sy.waitn("dv", "ac")
                sy.done("dv", nc.vector.reciprocal(stat[:, 3:4], stat[:, 2:3]))
                sy.waitn("dv", "dv")
                sy.done("dv", nc.vector.tensor_scalar_mul(dtx[:], ysb[:],
                                                          stat[:, 3:4]))
                sy.waitn("dv", "dv")
                sy.done("dv", nc.vector.tensor_tensor(out=dtx[:], in0=dtx[:],
                                                      in1=nw1[:], op=OP.mult))
                sy.waitn("pe", "dv")
                pbigB = pbig[:, 0:1024].bitcast(BF16)
                psBB = psB[:, 0:1024].bitcast(BF16)
                for g in range(4):
                    inst = None
                    for i8 in range(8):
                        i = g * 8 + i8
                        pt = pbigB if i < 16 else psBB
                        inst = nc.tensor.transpose(
                            pt[:, (i % 16) * 128:((i % 16) + 1) * 128],
                            dtx[:, i * 128:(i + 1) * 128], ident[:])
                    sy.done("pe", inst)
                sy.waitn("dv", "pe")
                sy.done("dv", nc.vector.tensor_copy(zch[:, 0:2048], pbigB[:]))
                sy.done("dv", nc.vector.tensor_copy(zch[:, 2048:4096], psBB[:]))
                sy.waitn("ac", "dv")
                sy.dma("yn", nc.scalar.dma_start(
                    out=ynv[:, :, bass.ts(ch, 128)],
                    in_=zch[:].rearrange("p (c t) -> p c t", t=128)))
                sy.waitn("pe", "dv")
                inst = None
                for i in range(8):
                    pt = pbig if i < 4 else psB
                    inst = nc.tensor.matmul(
                        pt[:, (i % 4) * 512:((i % 4) + 1) * 512], Btmstg[:],
                        dtxd[:, i * 512:(i + 1) * 512],
                        start=True, stop=True, skip_group_check=True)
                sy.done("pe", inst)
                s3 = S[:].rearrange("p (h q) -> p h q", q=64)
                sy.waitn("dv", "pe")
                sy.done("dv", nc.vector.tensor_tensor(
                    out=s3, in0=s3,
                    in1=dAl[:].unsqueeze(2).to_broadcast([128, 64, 64]),
                    op=OP.mult))
                sy.waitn("dv", "dv")
                sy.done("dv", nc.vector.tensor_tensor(out=S[:, 0:2048],
                                                      in0=S[:, 0:2048],
                                                      in1=pbig[:], op=OP.add))
                sy.done("dv", nc.vector.tensor_tensor(out=S[:, 2048:4096],
                                                      in0=S[:, 2048:4096],
                                                      in1=psB[:], op=OP.add))

            pairs = [("sp", "dv"), ("sp", "yn"),
                     ("pe", "dv"), ("pe", "ac"),
                     ("pe", "cr"), ("dv", "pe"), ("dv", "dv"),
                     ("dv", "ac"), ("dv", "xz"), ("ac", "dv"), ("ac", "pe"),
                     ("ac", "xz")]
            sy.begin_loop(pairs, {"xz": 32, "cr": 32, "yn": 16,
                                  "pe": 21, "dv": 38, "ac": 15})
            with ctx.fori(16, [PE, DVE, ACT, SP]) as ch:
                sy.iter_start(ch)
                p3e_body(ch)
            sy.end_loop(16)
            nc.sync.wait_ge(sy.p["yn"], sy.n["yn"])
            sy.free()
        ctx.barrier()
        es_bc.close()

        with nc.psum_tensor("p3f_p", [128, 2048], F32) as pf, \
             nc.sbuf_tensor("p3f_wo", [128, 32 * 2048], BF16) as wout, \
             nc.sbuf_tensor("p3f_yn", [128, 32 * 128], BF16) as ynst, \
             nc.sbuf_tensor("p3f_x2", [128, 2048], F32) as x2t, \
             nc.sbuf_tensor("p3f_x3", [128, 2048], F32) as x3t:
            nc.gpsimd.dma_start(out=wout[:].rearrange("p (c n) -> p c n", n=2048),
                                in_=wov).then_inc(cn, 16)
            ncn += 16
            for e in (nc.tensor, nc.sync):
                e.wait_ge(cn, ncn)
            sy = Sync(ctx, "p3f")

            def p3f_tile(u):
                sy.waitn("sp", "pe")
                sy.dma("ya", nc.sync.dma_start(
                    out=ynst[:].rearrange("p (c t) -> p c t", t=128),
                    in_=ynv[:, :, bass.ts(u, 128)]))
                sy.waitn("ac", "dv")
                sy.dma("xa", nc.scalar.dma_start(out=x2t[:], in_=x2v[u]))
                sy.waitn("pe", "ya")
                sy.waitn("pe", "dv")
                for dp in range(4):
                    inst = None
                    for c in range(32):
                        inst = nc.tensor.matmul(
                            pf[:, dp * 512:(dp + 1) * 512],
                            ynst[:, c * 128:(c + 1) * 128],
                            wout[:, c * 2048 + dp * 512: c * 2048 + (dp + 1) * 512],
                            start=(c == 0), stop=(c == 31))
                    sy.done("pe", inst)
                sy.waitn("dv", "pe")
                sy.waitn("dv", "xa")
                sy.waitn("dv", "oa")
                sy.done("dv", nc.vector.tensor_tensor(out=x3t[:], in0=pf[:],
                                                      in1=x2t[:], op=OP.add))
                sy.waitn("ac", "dv")
                sy.dma("oa", nc.scalar.dma_start(out=x3v[u], in_=x3t[:]))

            sy.begin_loop([("sp", "pe"), ("pe", "ya"), ("pe", "dv"), ("dv", "pe"),
                           ("dv", "xa"), ("dv", "oa"), ("ac", "dv")],
                          {"ya": 16, "xa": 16, "oa": 16, "pe": 4, "dv": 1})
            with ctx.fori(NT, [PE, DVE, SP, ACT]) as u:
                sy.iter_start(u)
                p3f_tile(u)
            sy.end_loop(NT)
            nc.sync.wait_ge(sy.p["oa"], 16 * NT)
            sy.free()
    ctx.barrier()


def emit_p4(ctx):
    nc, d = ctx.nc, ctx.d
    es = ExitStack()
    sb = lambda n, shp, dt=F32: es.enter_context(nc.sbuf_tensor(n, shp, dt))

    ident = sb("p4_ident", [128, 128], BF16)
    onesr32 = sb("p4_oner", [1, 128], F32)
    onec32 = sb("p4_onec", [128, 1], F32)
    stat = sb("p4_stat", [128, 8], F32)
    srow = sb("p4_srow", [1, 4], F32)
    thrc = sb("p4_thrc", [128, 9], F32)
    invs_all = sb("p4_invs", [128, 16], F32)

    x3v = d["x3"].rearrange("(t p) n -> t p n", p=128)
    outv = d["out"].rearrange("(t p) n -> t p n", p=128)
    wgv = d["wgT"].rearrange("(c p) n -> p c n", p=128)
    wuv = d["wuT"].rearrange("(c p) n -> p c n", p=128)
    wdv = d["wdT"].rearrange("(c p) n -> p c n", p=128)
    gsv = d["g_scr"].rearrange("(t p) n -> t p n", p=128)
    gsp = d["g_scr"].rearrange("(t p) n -> p t n", p=128)
    qgv = d["qg_scr"].rearrange("(c p) t -> p c t", p=128)

    with es:
        cn = ctx.sem("p4cn")
        ncn = 0
        for dst, src in ((ident[:], d["identity"]), (onesr32[:], d["ones_row"]),
                         (onec32[:], d["ones_col"])):
            nc.gpsimd.dma_start(out=dst, in_=src).then_inc(cn, 16)
            ncn += 16
        for e in (nc.vector, nc.scalar, nc.tensor, nc.sync):
            e.wait_ge(cn, ncn)

        with nc.psum_tensor("p4a_ps", [128, 512], F32) as psa, \
             nc.sbuf_tensor("p4a_w0", [128, 8192], F32) as ws0, \
             nc.sbuf_tensor("p4a_w1", [128, 8192], F32) as ws1, \
             nc.sbuf_tensor("p4a_ab", [128, 8192], BF16) as absb:
            ws = [ws0, ws1]
            sy = Sync(ctx, "p4a")
            chunks = ([("g", wgv[:, c, :]) for c in range(16)] +
                      [("u", wuv[:, c, :]) for c in range(16)] +
                      [("d", wdv[:, 4 * q:4 * q + 4, :]) for q in range(13)])
            nmat = {"g": 0, "u": 1, "d": 2}
            first = {"g": True, "u": True, "d": True}
            for i, (mk, src) in enumerate(chunks):
                par = i % 2
                wq = ("wa", "wb")[par]
                width = 6656 if mk in ("g", "u") else 8192
                sy.waitn("sp", "ac", 1)
                if mk == "d":
                    sy.dma(wq, nc.sync.dma_start(
                        out=ws[par][:].rearrange("p (c n) -> p c n", n=2048),
                        in_=src))
                else:
                    sy.dma(wq, nc.sync.dma_start(out=ws[par][:, 0:width], in_=src))
                sy.waitn("ac", wq)
                sy.waitn("ac", "dv")
                sy.done("ac", nc.scalar.activation(out=absb[:, 0:width],
                                                   in_=ws[par][:, 0:width],
                                                   func=AF.Abs,
                                                   accum_out=stat[:, 0:1]))
                sy.waitn("dv", "ac")
                mi = nmat[mk]
                if first[mk]:
                    first[mk] = False
                    sy.done("dv", nc.vector.tensor_copy(stat[:, 1 + mi:2 + mi],
                                                        stat[:, 0:1]))
                else:
                    sy.done("dv", nc.vector.tensor_tensor(
                        out=stat[:, 1 + mi:2 + mi], in0=stat[:, 1 + mi:2 + mi],
                        in1=stat[:, 0:1], op=OP.add))
            for mi in range(3):
                sy.waitn("pe", "dv")
                sy.done("pe", nc.tensor.matmul(psa[0:1, 0:1], stat[:, 1 + mi:2 + mi],
                                               onec32[:], start=True, stop=True,
                                               skip_group_check=True))
                sy.waitn("dv", "pe")
                sy.done("dv", nc.vector.tensor_scalar(
                    out=srow[0:1, mi:mi + 1], in0=psa[0:1, 0:1],
                    scalar1=float(1.0 / (I2 * D)), scalar2=1e-5,
                    op0=OP.mult, op1=OP.max))
                sy.waitn("pe", "dv")
                sy.done("pe", nc.tensor.matmul(psa[:, 64:65], onesr32[:],
                                               srow[0:1, mi:mi + 1],
                                               start=True, stop=True,
                                               skip_group_check=True))
                sy.waitn("dv", "pe")
                sy.done("dv", nc.vector.tensor_scalar_mul(thrc[:, 2 * mi:2 * mi + 1],
                                                          psa[:, 64:65], 0.5))
                sy.waitn("dv", "dv")
                sy.done("dv", nc.vector.tensor_scalar_mul(
                    thrc[:, 2 * mi + 1:2 * mi + 2], psa[:, 64:65], -0.5))
                sy.waitn("dv", "dv")
                sy.done("dv", nc.vector.tensor_copy(thrc[:, 6 + mi:7 + mi],
                                                    psa[:, 64:65]))
            sy.free()
        ctx.barrier()

        es_q = ExitStack()
        qact = es_q.enter_context(nc.sbuf_tensor("p4_qact", [128, 16 * 2048], BF16))
        qview = qact[:].rearrange("p (c t) -> p c t", t=T)
        with nc.psum_tensor("p4b_ptr", [128, 2048], BF16) as ptr, \
             nc.sbuf_tensor("p4b_xw", [128, D], F32) as xw, \
             nc.sbuf_tensor("p4b_sq", [128, D], F32) as sqs, \
             nc.sbuf_tensor("p4b_lnf", [128, D], F32) as lnf, \
             nc.sbuf_tensor("p4b_u", [128, D], F32) as ubuf, \
             nc.sbuf_tensor("p4b_ci", [128, D], I32) as cibuf, \
             nc.sbuf_tensor("p4b_cf", [128, D], F32) as cfbuf, \
             nc.sbuf_tensor("p4b_iv", [128, 1], F32) as ivcol, \
             nc.sbuf_tensor("p4b_q", [128, D], BF16) as lnq:
            sy = Sync(ctx, "p4b")

            def p4b_tile(u):
                sy.waitn("ac", "dv")
                sy.dma("xa", nc.scalar.dma_start(out=xw[:], in_=x3v[u]))
                sy.waitn("dv", "xa")
                sy.waitn("ac", "xa")
                sy.waitn("dv", "dv")
                sy.done("dv", nc.vector.tensor_reduce(out=stat[:, 0:1], in_=xw[:],
                                                      axis=AX.X, op=OP.add))
                sy.waitn("ac", "ac")
                sy.done("ac", nc.scalar.activation(out=sqs[:], in_=xw[:],
                                                   func=AF.Square,
                                                   accum_out=stat[:, 1:2]))
                sy.waitn("dv", "dv")
                sy.done("dv", nc.vector.tensor_scalar_mul(stat[:, 2:3], stat[:, 0:1],
                                                          float(1.0 / D)))
                sy.waitn("dv", "dv")
                sy.done("dv", nc.vector.tensor_tensor(out=stat[:, 3:4],
                                                      in0=stat[:, 2:3],
                                                      in1=stat[:, 2:3], op=OP.mult))
                sy.waitn("dv", "dv")
                sy.waitn("dv", "ac")
                sy.done("dv", nc.vector.scalar_tensor_tensor(
                    out=stat[:, 4:5], in0=stat[:, 1:2], scalar=float(1.0 / D),
                    in1=stat[:, 3:4], op0=OP.mult, op1=OP.subtract))
                sy.waitn("ac", "dv")
                sy.done("ac", nc.scalar.activation(out=stat[:, 5:6], in_=stat[:, 4:5],
                                                   func=AF.Sqrt, bias=f32c(1e-5)))
                sy.waitn("dv", "ac")
                sy.done("dv", nc.vector.reciprocal(stat[:, 6:7], stat[:, 5:6]))
                sy.waitn("dv", "dv")
                sy.done("dv", nc.vector.tensor_tensor(out=stat[:, 7:8],
                                                      in0=stat[:, 2:3],
                                                      in1=stat[:, 6:7], op=OP.mult))
                sy.waitn("dv", "dv")
                sy.done("dv", nc.vector.tensor_scalar(out=lnf[:], in0=xw[:],
                                                      scalar1=stat[:, 6:7],
                                                      scalar2=stat[:, 7:8],
                                                      op0=OP.mult, op1=OP.subtract))
                sy.waitn("dv", "dv")
                sy.done("dv", nc.vector.scalar_tensor_tensor(
                    out=ubuf[:], in0=lnf[:], scalar=-1.0, in1=lnf[:],
                    op0=OP.mult, op1=OP.max))
                sy.waitn("dv", "dv")
                sy.done("dv", nc.vector.tensor_reduce(out=stat[:, 0:1], in_=ubuf[:],
                                                      axis=AX.X, op=OP.max))
                sy.waitn("dv", "dv")
                sy.done("dv", nc.vector.tensor_scalar_max(stat[:, 1:2], stat[:, 0:1],
                                                          1e-5))
                sy.waitn("dv", "dv")
                sy.done("dv", nc.vector.reciprocal(stat[:, 2:3], stat[:, 1:2]))
                sy.waitn("dv", "dv")
                sy.done("dv", nc.vector.tensor_scalar_mul(stat[:, 3:4], stat[:, 2:3],
                                                          127.0))
                sy.waitn("dv", "dv")
                sy.done("dv", nc.vector.tensor_scalar_mul(
                    invs_all[:, bass.ts(u, 1)], stat[:, 1:2], float(1.0 / 127.0)))
                sy.waitn("dv", "dv")
                sy.done("dv", nc.vector.tensor_scalar(out=ubuf[:], in0=lnf[:],
                                                      scalar1=stat[:, 3:4],
                                                      scalar2=0.5,
                                                      op0=OP.mult, op1=OP.add))
                sy.waitn("dv", "dv")
                sy.done("dv", nc.vector.tensor_copy(cibuf[:], ubuf[:]))
                sy.waitn("dv", "dv")
                sy.done("dv", nc.vector.tensor_copy(cfbuf[:], cibuf[:]))
                sy.waitn("dv", "dv")
                sy.done("dv", nc.vector.tensor_tensor(out=ubuf[:], in0=cfbuf[:],
                                                      in1=ubuf[:], op=OP.is_gt))
                sy.waitn("dv", "dv")
                sy.waitn("dv", "pe")
                sy.done("dv", nc.vector.tensor_tensor(out=lnq[:], in0=cfbuf[:],
                                                      in1=ubuf[:], op=OP.subtract))
                sy.waitn("pe", "dv")
                sy.waitn("pe", "dv", 17)
                inst = None
                for c in range(16):
                    inst = nc.tensor.transpose(ptr[:, c * 128:(c + 1) * 128],
                                               lnq[:, c * 128:(c + 1) * 128],
                                               ident[:])
                sy.done("pe", inst)
                sy.waitn("dv", "pe")
                sy.done("dv", nc.vector.tensor_copy(
                    qview[:, :, bass.ts(u, 128)],
                    ptr[:].rearrange("p (c n) -> p c n", n=128)))
                isv = d["invs_scr"].rearrange("(t p) o -> t p o", p=128)
                sy.waitn("dv", "is")
                sy.done("dv", nc.vector.tensor_scalar_mul(ivcol[:], stat[:, 1:2],
                                                          float(1.0 / 127.0)))
                sy.waitn("ac", "dv")
                sy.dma("is", nc.scalar.dma_start(out=isv[u], in_=ivcol[:]))

            sy.begin_loop([("ac", "dv"), ("dv", "xa"), ("ac", "xa"), ("dv", "dv"),
                           ("dv", "ac"), ("ac", "ac"), ("dv", "pe"), ("pe", "dv"),
                           ("dv", "is")],
                          {"xa": 16, "is": 16, "dv": 20, "ac": 2, "pe": 1})
            with ctx.fori(NT, [PE, DVE, ACT]) as u:
                sy.iter_start(u)
                p4b_tile(u)
            sy.end_loop(NT)
            nc.sync.wait_ge(sy.p["is"], 16 * NT)
            sy.free()
        ctx.barrier()

        with nc.psum_tensor("p4g_pg", [128, 2048], F32) as pg, \
             nc.psum_tensor("p4g_pu", [128, 2048], F32) as pu:
            sy = Sync(ctx, "p4g")
            with nc.sbuf_tensor("p4g_ivr", [1, 2048], F32) as ivrow, \
                 nc.sbuf_tensor("p4g_bcg", [128, 2048], F32) as invsg_bc, \
                 nc.sbuf_tensor("p4g_bcu", [128, 2048], F32) as invsu_bc, \
                 nc.sbuf_tensor("p4g_wf", [128, 16 * 512], F32) as wf, \
                 nc.sbuf_tensor("p4g_qwg", [128, 16 * 512], BF16) as qwg, \
                 nc.sbuf_tensor("p4g_qwu", [128, 16 * 512], BF16) as qwu, \
                 nc.sbuf_tensor("p4g_qtmp", [128, 16 * 512], BF16) as qtmp, \
                 nc.sbuf_tensor("p4g_gfm", [128, 2048], BF16) as gfm, \
                 nc.sbuf_tensor("p4g_gtm", [128, 2048], BF16) as gtm, \
                 nc.sbuf_tensor("p4g_sg", [128, 512], F32) as sg:
                sy.dma("iv", nc.sync.dma_start(
                    out=ivrow[:], in_=d["invs_scr"].rearrange("t o -> o t")))
                nc.tensor.wait_ge(sy.p["iv"], 16)
                inst = None
                for i in range(4):
                    inst = nc.tensor.matmul(pg[:, i * 512:(i + 1) * 512], onesr32[:],
                                            ivrow[0:1, i * 512:(i + 1) * 512],
                                            start=True, stop=True,
                                            skip_group_check=True)
                r_bc = sy.done("pe", inst)
                sy.waitn("dv", "pe")
                sy.done("dv", nc.vector.tensor_scalar_mul(invsg_bc[:], pg[:],
                                                          thrc[:, 6:7]))
                sy.waitn("dv", "dv")
                sy.done("dv", nc.vector.tensor_scalar_mul(invsu_bc[:], pg[:],
                                                          thrc[:, 7:8]))

                def g1_body(op):
                    sy.waitn("sp", "dv")
                    sy.dma("wf", nc.sync.dma_start(
                        out=wf[:].rearrange("p (c n) -> p c n", n=512),
                        in_=wgv[:, :, bass.ts(op, 512)]))
                    sy.waitn("dv", "wf")
                    sy.waitn("dv", "pe")
                    sy.done("dv", nc.vector.tensor_scalar(
                        out=qtmp[:], in0=wf[:], scalar1=thrc[:, 1:2], scalar2=0.0,
                        op0=OP.is_le, op1=OP.bypass))
                    sy.waitn("dv", "dv")
                    sy.done("dv", nc.vector.scalar_tensor_tensor(
                        out=qwg[:], in0=wf[:], scalar=thrc[:, 0:1], in1=qtmp[:],
                        op0=OP.is_ge, op1=OP.subtract))
                    sy.waitn("sp", "dv")
                    sy.dma("wf", nc.sync.dma_start(
                        out=wf[:].rearrange("p (c n) -> p c n", n=512),
                        in_=wuv[:, :, bass.ts(op, 512)]))
                    sy.waitn("dv", "wf")
                    sy.done("dv", nc.vector.tensor_scalar(
                        out=qtmp[:], in0=wf[:], scalar1=thrc[:, 3:4], scalar2=0.0,
                        op0=OP.is_le, op1=OP.bypass))
                    sy.waitn("dv", "dv")
                    sy.done("dv", nc.vector.scalar_tensor_tensor(
                        out=qwu[:], in0=wf[:], scalar=thrc[:, 2:3], in1=qtmp[:],
                        op0=OP.is_ge, op1=OP.subtract))
                    for osub in range(4):
                        sy.waitn("pe", "dv")
                        for mat, (qw, ps) in enumerate(((qwg, pg), (qwu, pu))):
                            for tq in range(4):
                                inst = None
                                for c in range(16):
                                    inst = nc.tensor.matmul(
                                        ps[:, tq * 512:(tq + 1) * 512],
                                        qw[:, c * 512 + osub * 128:
                                           c * 512 + osub * 128 + 128],
                                        qact[:, c * 2048 + tq * 512:
                                             c * 2048 + (tq + 1) * 512],
                                        start=(c == 0), stop=(c == 15))
                                sy.done("pe", inst)
                        for tq in range(4):
                            sy.waitn("dv", "pe")
                            sy.waitn("dv", "ac")
                            sy.waitn("dv", "dv")
                            sy.done("dv", nc.vector.tensor_tensor(
                                out=sg[:], in0=pg[:, tq * 512:(tq + 1) * 512],
                                in1=invsg_bc[:, tq * 512:(tq + 1) * 512],
                                op=OP.mult))
                            sy.waitn("ac", "dv")
                            sy.done("ac", nc.scalar.activation(out=sg[:], in_=sg[:],
                                                               func=AF.Silu))
                            sy.waitn("dv", "dv")
                            sy.done("dv", nc.vector.tensor_tensor(
                                out=gfm[:, tq * 512:(tq + 1) * 512],
                                in0=pu[:, tq * 512:(tq + 1) * 512],
                                in1=invsu_bc[:, tq * 512:(tq + 1) * 512],
                                op=OP.mult))
                            sy.waitn("dv", "ac")
                            sy.waitn("dv", "dv")
                            sy.done("dv", nc.vector.tensor_tensor(
                                out=gfm[:, tq * 512:(tq + 1) * 512],
                                in0=gfm[:, tq * 512:(tq + 1) * 512],
                                in1=sg[:], op=OP.mult))
                        ptrg = pg[:, 0:1024].bitcast(BF16)
                        sy.waitn("pe", "dv")
                        inst = None
                        for i in range(16):
                            inst = nc.tensor.transpose(
                                ptrg[:, i * 128:(i + 1) * 128],
                                gfm[:, i * 128:(i + 1) * 128], ident[:])
                        sy.done("pe", inst)
                        sy.waitn("dv", "pe")
                        sy.waitn("dv", "gs")
                        sy.done("dv", nc.vector.tensor_copy(gtm[:], ptrg[:]))
                        sy.waitn("sp", "dv", 0)
                        sy.dma("gs", nc.sync.dma_start(
                            out=gsp[:, :, bass.ts(op, 512)][
                                :, :, osub * 128:(osub + 1) * 128],
                            in_=gtm[:].rearrange("p (t n) -> p t n", n=128)))

                sy.begin_loop([("sp", "dv"), ("dv", "wf"), ("dv", "pe"), ("dv", "dv"),
                               ("dv", "ac"), ("dv", "gs"), ("pe", "dv"),
                               ("ac", "dv")],
                              {"wf": 32, "pe": 36, "dv": 4 + 4 * 13, "ac": 16,
                               "gs": 64})
                with ctx.fori(13, [PE, DVE, ACT, SP]) as op:
                    sy.iter_start(op)
                    g1_body(op)
                sy.end_loop(13)
                nc.sync.wait_ge(sy.p["gs"], 64 * 13)
                sy.free()
        es_q.close()
        ctx.barrier()

        with nc.psum_tensor("p4c_ptr", [128, 2048], BF16) as ptr2, \
             nc.sbuf_tensor("p4c_g", [128, I2P], BF16) as gbuf, \
             nc.sbuf_tensor("p4c_u", [128, I2P], F32) as u2, \
             nc.sbuf_tensor("p4c_ci", [128, I2P], I32) as ci2, \
             nc.sbuf_tensor("p4c_cf", [128, I2P], F32) as cf2, \
             nc.sbuf_tensor("p4c_q", [128, I2P], BF16) as qgb, \
             nc.sbuf_tensor("p4c_qt", [128, 13 * 128], BF16) as qtm:
            sy = Sync(ctx, "p4c")

            def p4c_tile(u):
                sy.waitn("po", "dv")
                sy.dma("gl", nc.gpsimd.dma_start(out=gbuf[:], in_=gsv[u]))
                sy.waitn("dv", "gl")
                sy.waitn("dv", "dv")
                sy.done("dv", nc.vector.scalar_tensor_tensor(
                    out=u2[:], in0=gbuf[:], scalar=-1.0, in1=gbuf[:],
                    op0=OP.mult, op1=OP.max))
                sy.waitn("dv", "dv")
                sy.done("dv", nc.vector.tensor_reduce(out=stat[:, 0:1], in_=u2[:],
                                                      axis=AX.X, op=OP.max))
                sy.waitn("dv", "dv")
                sy.done("dv", nc.vector.tensor_scalar_max(stat[:, 1:2], stat[:, 0:1],
                                                          1e-5))
                sy.waitn("dv", "dv")
                sy.done("dv", nc.vector.reciprocal(stat[:, 2:3], stat[:, 1:2]))
                sy.waitn("dv", "dv")
                sy.done("dv", nc.vector.tensor_scalar_mul(stat[:, 3:4], stat[:, 2:3],
                                                          127.0))
                sy.waitn("dv", "dv")
                sy.done("dv", nc.vector.tensor_scalar_mul(
                    invs_all[:, bass.ts(u, 1)], stat[:, 1:2], float(1.0 / 127.0)))
                sy.waitn("dv", "dv")
                sy.done("dv", nc.vector.tensor_scalar(out=u2[:], in0=gbuf[:],
                                                      scalar1=stat[:, 3:4],
                                                      scalar2=0.5,
                                                      op0=OP.mult, op1=OP.add))
                sy.waitn("dv", "dv")
                sy.done("dv", nc.vector.tensor_copy(ci2[:], u2[:]))
                sy.waitn("dv", "dv")
                sy.done("dv", nc.vector.tensor_copy(cf2[:], ci2[:]))
                sy.waitn("dv", "dv")
                sy.done("dv", nc.vector.tensor_tensor(out=u2[:], in0=cf2[:],
                                                      in1=u2[:], op=OP.is_gt))
                sy.waitn("dv", "dv")
                sy.waitn("dv", "pe")
                sy.done("dv", nc.vector.tensor_tensor(out=qgb[:], in0=cf2[:],
                                                      in1=u2[:], op=OP.subtract))
                for g4 in range(4):
                    sy.waitn("pe", "dv")
                    inst = None
                    for j in range(13):
                        i = g4 * 13 + j
                        inst = nc.tensor.transpose(ptr2[:, j * 128:(j + 1) * 128],
                                                   qgb[:, i * 128:(i + 1) * 128],
                                                   ident[:])
                    sy.done("pe", inst)
                    sy.waitn("dv", "pe")
                    sy.waitn("dv", "qs")
                    sy.done("dv", nc.vector.tensor_copy(qtm[:], ptr2[:, 0:13 * 128]))
                    sy.waitn("po", "dv")
                    sy.dma("qs", nc.gpsimd.dma_start(
                        out=qgv[:, g4 * 13:(g4 + 1) * 13, bass.ts(u, 128)],
                        in_=qtm[:].rearrange("p (c t) -> p c t", t=128)))

            sy.begin_loop([("po", "dv"), ("dv", "gl"), ("dv", "dv"), ("dv", "pe"),
                           ("pe", "dv"), ("dv", "qs")],
                          {"gl": 16, "qs": 64, "dv": 11 + 4, "pe": 4})
            with ctx.fori(NT, [PE, DVE, Pool]) as u:
                sy.iter_start(u)
                p4c_tile(u)
            sy.end_loop(NT)
            sy.waitn("dv", "dv")
            sy.done("dv", nc.vector.tensor_scalar_mul(invs_all[:], invs_all[:],
                                                      thrc[:, 8:9]))
            nc.gpsimd.wait_ge(sy.p["qs"], 64 * NT)
            sy.free()
        ctx.barrier()

        with nc.psum_tensor("p4d_ps", [128, 1024], F32) as pd4, \
             nc.sbuf_tensor("p4d_wf", [128, 13 * 512], F32) as wdf, \
             nc.sbuf_tensor("p4d_qw", [128, 52 * 1024], BF16) as qwd, \
             nc.sbuf_tensor("p4d_qt", [128, 13 * 512], BF16) as qdt, \
             nc.sbuf_tensor("p4d_qg", [128, 52 * 128], BF16) as qgt, \
             nc.sbuf_tensor("p4d_x3", [128, 1024], F32) as x3s, \
             nc.sbuf_tensor("p4d_ob", [128, 1024], F32) as ob, \
             nc.sbuf_tensor("p4d_iv", [128, 1], F32) as iv2:
            sy = Sync(ctx, "p4d")
            for dh in range(2):
                for q8 in range(8):
                    q4, dsl = q8 // 2, 2 * dh + q8 % 2
                    sy.waitn("sp", "dv")
                    if dh > 0 and q8 == 0:
                        sy.waitn("sp", "pe")
                    sy.dma("wf", nc.sync.dma_start(
                        out=wdf[:].rearrange("p (c n) -> p c n", n=512),
                        in_=wdv[:, q4 * 13:(q4 + 1) * 13,
                                dsl * 512:(dsl + 1) * 512]))
                    sy.waitn("dv", "wf")
                    if dh > 0 and q8 == 0:
                        sy.waitn("dv", "pe")
                    sy.done("dv", nc.vector.tensor_scalar(
                        out=qdt[:], in0=wdf[:], scalar1=thrc[:, 5:6], scalar2=0.0,
                        op0=OP.is_le, op1=OP.bypass))
                    sy.waitn("dv", "dv")
                    sy.done("dv", nc.vector.scalar_tensor_tensor(
                        out=qwd[:].rearrange("p (c q) -> p c q", q=1024)[
                            :, q4 * 13:(q4 + 1) * 13,
                            (q8 % 2) * 512:(q8 % 2 + 1) * 512],
                        in0=wdf[:], scalar=thrc[:, 4:5], in1=qdt[:],
                        op0=OP.is_ge, op1=OP.subtract))

                def p4d_tile(u):
                    sy.waitn("ac", "pe")
                    sy.dma("qg", nc.scalar.dma_start(
                        out=qgt[:].rearrange("p (c t) -> p c t", t=128),
                        in_=qgv[:, :, bass.ts(u, 128)]))
                    sy.waitn("po", "dv")
                    sy.dma("xr", nc.gpsimd.dma_start(
                        out=x3s[:], in_=x3v[u][:, dh * 1024:(dh + 1) * 1024]))
                    sy.waitn("dv", "dv")
                    sy.done("dv", nc.vector.tensor_copy(iv2[:],
                                                        invs_all[:, bass.ts(u, 1)]))
                    sy.waitn("pe", "qg")
                    sy.waitn("pe", "dv")
                    for hb in range(2):
                        inst = None
                        for c in range(NOC):
                            inst = nc.tensor.matmul(
                                pd4[:, hb * 512:(hb + 1) * 512],
                                qgt[:, c * 128:(c + 1) * 128],
                                qwd[:, c * 1024 + hb * 512:
                                    c * 1024 + (hb + 1) * 512],
                                start=(c == 0), stop=(c == NOC - 1))
                        sy.done("pe", inst)
                    sy.waitn("dv", "pe")
                    sy.waitn("dv", "xr")
                    sy.waitn("dv", "dv")
                    sy.waitn("dv", "os")
                    sy.done("dv", nc.vector.scalar_tensor_tensor(
                        out=ob[:], in0=pd4[:], scalar=iv2[:, 0:1],
                        in1=x3s[:], op0=OP.mult, op1=OP.add))
                    sy.waitn("ac", "dv")
                    sy.dma("os", nc.scalar.dma_start(
                        out=outv[u][:, dh * 1024:(dh + 1) * 1024], in_=ob[:]))

                sy.begin_loop([("ac", "pe"), ("ac", "dv"), ("po", "dv"),
                               ("pe", "qg"), ("pe", "dv"), ("dv", "pe"),
                               ("dv", "xr"), ("dv", "dv"), ("dv", "os")],
                              {"qg": 16, "xr": 16, "os": 16, "pe": 2, "dv": 2})
                with ctx.fori(NT, [PE, DVE, ACT, Pool]) as u:
                    sy.iter_start(u)
                    p4d_tile(u)
                sy.end_loop(NT)
            nc.sync.wait_ge(sy.p["os"], 16 * NT * 2)
            sy.free()
    ctx.barrier()



PHASE_EMITS = {}


def build_program(phases=("p0", "p1", "p2", "p3", "p4"),
                  external_scratch=(), final_wait=True):
    nc = bass.Bass("TRN2", target_bir_lowering=False)
    es = ExitStack()
    ctx = Ctx(nc, es)
    ctx.d["x0"] = nc.dram_tensor("x0", [T, D], F32, kind="ExternalInput").ap()
    for name, shp in HOST_SHAPES.items():
        ctx.d[name] = nc.dram_tensor(name, list(shp), F32, kind="ExternalInput").ap()
    ctx.d["out"] = nc.dram_tensor("out", [T, D], F32, kind="ExternalOutput").ap()
    for name, (shp, dt) in SCRATCH.items():
        kind = "ExternalOutput" if name in external_scratch else "Internal"
        ctx.d[name] = nc.dram_tensor(name, list(shp), dt, kind=kind).ap()
    emits = {"p0": emit_p0, "p1": emit_p1, "p2": emit_p2, "p3": emit_p3,
             "p4": emit_p4}
    with es:
        for p in phases:
            emits[p](ctx)
    return nc, ctx


def kernel(**inputs):
    Hm = host_prepare(inputs)
    x_all = np.asarray(inputs["L0_attn_hidden_states"], dtype=np.float32)
    nc, ctx = build_program()
    in_maps = []
    for b in range(8):
        m = dict(Hm)
        m["x0"] = np.ascontiguousarray(x_all[b])
        in_maps.append(m)
    res = run_bass_kernel_spmd(nc, in_maps, core_ids=list(range(8)))
    return np.stack([res.results[b]["out"] for b in range(8)], axis=0)


# version 03
# speedup vs baseline: 1.0117x; 1.0117x over previous
import numpy as np
import concourse.bass as bass
import concourse.mybir as mybir
from concourse.bass_utils import run_bass_kernel_spmd
from concourse.alu_op_type import AluOpType as OP
from concourse.expressions import RuntimeValue
from contextlib import ExitStack, contextmanager

F32 = mybir.dt.float32
BF16 = mybir.dt.bfloat16
I32 = mybir.dt.int32
AF = mybir.ActivationFunctionType
AX = mybir.AxisListType
PE = mybir.EngineType.PE
DVE = mybir.EngineType.DVE
Pool = mybir.EngineType.Pool
SP = mybir.EngineType.SP
ACT = mybir.EngineType.Activation

T = 2048
D = 2048
NT = T // 128
ND = D // 128
H = 16
NCC = 34
INTER = 4096
I2 = 6464
I2P = 6656
NOC = I2P // 128
ISC = f32 = None


def f32c(x):
    return np.float32(x)


SC_ATT = 1.0 / np.sqrt(128.0)



def rope_perm():
    perm = np.zeros(128, dtype=np.int64)
    for i in range(32):
        perm[i] = 2 * i
        perm[32 + i] = 64 + 2 * i
        perm[64 + i] = 2 * i + 1
        perm[96 + i] = 64 + 2 * i + 1
    return perm


def host_prepare(inputs):
    f = lambda k: np.ascontiguousarray(np.asarray(inputs[k], dtype=np.float32))
    Hm = {}
    perm = rope_perm()

    WqT = np.ascontiguousarray(f("L0_attn_Wq").T)
    Wq_p = WqT.copy()
    for h in range(H):
        Wq_p[:, h * 128:(h + 1) * 128] = WqT[:, h * 128 + perm]
    Hm["WqT"] = Wq_p
    Hm["WkT"] = np.ascontiguousarray(f("L0_attn_Wk").T[:, perm])
    Hm["WvT"] = np.ascontiguousarray(f("L0_attn_Wv").T)
    Hm["WoT"] = np.ascontiguousarray(f("L0_attn_Wo").T)

    half = 64
    theta = 1.0 / 10000.0 ** (np.arange(0, half, 2, dtype=np.float64) / half)
    hh = np.arange(H, dtype=np.float64)
    ang = theta[:, None] * hh[None, :]
    cos = np.cos(ang)
    sin = np.sin(ang)
    cos128 = np.tile(cos, (4, 1)).astype(np.float32)
    sinH = np.concatenate([-np.tile(sin, (2, 1)), np.tile(sin, (2, 1))],
                          axis=0).astype(np.float32)
    Hm["cosF"] = np.ascontiguousarray(cos128)
    Hm["sinF"] = np.ascontiguousarray(sinH)

    r = np.arange(128)[:, None]
    c = np.arange(512)[None, :]
    Hm["diagmask"] = np.ascontiguousarray(
        np.concatenate([(c >= 128 * j + r).astype(np.float32) for j in range(4)], axis=1))

    Hm["identity"] = np.eye(128, dtype=np.float32)
    Hm["ones_col"] = np.ones((128, 1), dtype=np.float32)
    Hm["ones_row"] = np.ones((1, 128), dtype=np.float32)

    Hm["WcT"] = np.ascontiguousarray(f("L0_ffn_Wc").T)
    Hm["genes"] = f("L0_ffn_genes")
    Hm["WdT"] = np.ascontiguousarray(f("L0_ffn_Wd").T)
    Hm["WuT"] = np.ascontiguousarray(f("L0_ffn_Wu").T)
    Hm["l0_normw_bc"] = np.tile(f("L0_ffn_norm_w")[None, :], (128, 1))
    Hm["l0_temp_bc"] = np.tile(f("L0_ffn_temperature").reshape(1, 1), (128, 1))
    Hm["l0_scale_bc"] = np.tile(f("L0_ffn_scale").reshape(1, 1), (128, 1))

    Hm["W_z"] = f("L1_attn_W_z")
    Hm["W_xBC"] = f("L1_attn_W_xBC")
    Hm["W_dt"] = f("L1_attn_W_dt")
    cw = f("L1_attn_conv_w")
    diag = np.zeros((4, NCC, 128, 128), dtype=np.float32)
    for j in range(4):
        for cc in range(NCC):
            np.fill_diagonal(diag[j, cc], cw[cc * 128:(cc + 1) * 128, j])
    Hm["conv_diag"] = diag.reshape(4 * NCC * 128, 128)
    Hm["conv_b_fm"] = np.ascontiguousarray(f("L1_attn_conv_b").reshape(NCC, 128).T)
    Hm["dt_bias"] = f("L1_attn_dt_bias").reshape(64, 1)
    Hm["A_log"] = f("L1_attn_A_log").reshape(64, 1)
    Hm["D_bc"] = np.tile(np.repeat(f("L1_attn_D"), 64)[None, :], (128, 1))
    Hm["l1_normw_bc"] = np.tile(f("L1_attn_norm_w")[None, :], (128, 1))
    Hm["W_out"] = f("L1_attn_W_out")
    Hm["zeros4096"] = np.zeros((128, 4096), dtype=np.float32)
    tt_ = np.arange(128)
    Hm["tri01"] = (tt_[None, :] >= tt_[:, None]).astype(np.float32)

    wg = np.zeros((D, I2P), dtype=np.float32)
    wg[:, 0:I2] = f("L1_ffn_w_gate").T
    wu = np.zeros((D, I2P), dtype=np.float32)
    wu[:, 0:I2] = f("L1_ffn_w_up").T
    wd = np.zeros((I2P, D), dtype=np.float32)
    wd[0:I2, :] = f("L1_ffn_w_down").T
    Hm["wgT"] = wg
    Hm["wuT"] = wu
    Hm["wdT"] = wd
    return Hm


HOST_SHAPES = {
    "WqT": (D, D), "WkT": (D, 128), "WvT": (D, 128), "WoT": (D, D),
    "cosF": (128, H), "sinF": (128, H), "diagmask": (128, 4 * 512),
    "identity": (128, 128), "ones_col": (128, 1), "ones_row": (1, 128),
    "WcT": (D, 4096), "genes": (4096, 256), "WdT": (256, D), "WuT": (256, D),
    "l0_normw_bc": (128, D), "l0_temp_bc": (128, 1), "l0_scale_bc": (128, 1),
    "W_z": (D, 4096), "W_xBC": (D, 4352), "W_dt": (D, 64),
    "conv_diag": (4 * NCC * 128, 128), "conv_b_fm": (128, NCC),
    "dt_bias": (64, 1), "A_log": (64, 1), "D_bc": (128, 4096),
    "l1_normw_bc": (128, 4096), "W_out": (4096, D),
    "zeros4096": (128, 4096), "tri01": (128, 128),
    "wgT": (D, I2P), "wuT": (D, I2P), "wdT": (I2P, D),
}

SCRATCH = {
    "x1": ((T, D), F32), "x2": ((T, D), F32), "x3": ((T, D), F32),
    "wq_bf": ((D, D + 128), BF16),
    "wk_bf": ((D, 128), BF16), "wv_bf": ((D, 128), BF16), "wo_bf": ((D, D), BF16),
    "wc_bf": ((D, 4096), BF16), "genes_bf": ((4096, 256), BF16),
    "wd_bf": ((256, D), BF16), "wu_bf": ((256, D), BF16),
    "wz_bf": ((D, 4096), BF16), "wxbc_bf": ((D, 4352), BF16),
    "wdt_bf": ((D, 64), BF16), "convdiag_bf": ((4 * NCC * 128, 128), BF16),
    "wout_bf": ((4096, D), BF16),
    "z_scr": ((T, 4096), BF16), "xtm_scr": ((T, 4096), BF16),
    "ynorm_scr": ((4096, T), BF16), "cum_scr": ((64, T), F32),
    "g_scr": ((T, I2P), BF16),
    "qg_scr": ((I2P, T), BF16),
    "invs_scr": ((T, 1), F32),
}


@contextmanager
def _my_fori(nc, trips, regs):
    engines = regs.engines
    loop_name = f"myfori_{nc.next_id()}"
    loop_start = f"{loop_name}_loop"
    loop_end = f"{loop_name}_end"
    nc.regs_mov(regs, 0)
    nc.br(loop_start, engines=engines)
    with nc.body(loop_start, valid_engines=engines):
        yield RuntimeValue(val=regs, min_val=0, max_val=trips - 1)
        nc.regs_alu(regs, regs, 1, op=OP.add)
        nc.br_lt(regs, trips, on_true=loop_start, on_false=loop_end,
                 engines=engines)
    nc.switch_bb(loop_end)


class Ctx:
    def __init__(self, nc, es):
        self.nc = nc
        self.es = es
        self.d = {}
        self.sems = {}
        self.cnt = {}
        self.loop_rh = None

    def fori(self, trips, engines):
        if self.loop_rh is None:
            self.loop_rh = self.nc.alloc_registers(
                "loopreg", engines=[PE, DVE, ACT, SP, Pool])
        RH = type(self.loop_rh)
        sub = RH([self.loop_rh[e] for e in engines])
        return _my_fori(self.nc, trips, sub)

    def sem(self, name):
        if name not in self.sems:
            self.sems[name] = self.es.enter_context(self.nc.semaphore(name))
            self.cnt[name] = 0
        return self.sems[name]

    def barrier(self):
        self.nc.all_engine_barrier()



class Trk:

    def __init__(self, nc, eng_type, eng_obj, sem, name):
        self.nc = nc
        self.eng = eng_obj
        self.sem = sem
        self.rh = nc.alloc_registers(name, engines=[eng_type])
        nc.regs_mov(self.rh, 0)
        self.runC = 0
        self.body_first = False
        self.loop_i = None
        self.coeff = 0

    def begin_body(self, loop_i, coeff):
        self.body_first = True
        self.loop_i = loop_i
        self.coeff = coeff

    def wait(self, C):
        if self.loop_i is not None:
            if not self.body_first and C <= self.runC:
                return
            self.body_first = False
            self.runC = C
            self.nc.regs_alu(self.rh, self.loop_i, self.coeff, op=OP.mult)
            self.nc.regs_alu(self.rh, self.rh, C, op=OP.add)
            self.nc.regs_alu(self.rh, self.rh, 0, op=OP.max)
            self.eng.wait_ge(self.sem, self.rh)
            return
        if C <= 0:
            return
        d = C - self.runC
        if d > 0:
            self.runC = C
            self.nc.regs_alu(self.rh, self.rh, d, op=OP.add)
            self.eng.wait_ge(self.sem, self.rh)

    def free(self):
        for h in self.rh.handles:
            self.nc.free_register(h)


class Sync:
    ET = {"pe": PE, "dv": DVE, "ac": ACT, "sp": SP, "po": Pool}

    def __init__(self, ctx, tag):
        self.ctx = ctx
        self.nc = ctx.nc
        self.eng = {"pe": ctx.nc.tensor, "dv": ctx.nc.vector, "ac": ctx.nc.scalar,
                    "sp": ctx.nc.sync, "po": ctx.nc.gpsimd}
        self.tag = tag
        self.p = {}
        self.n = {}
        self.trk = {}
        self.in_loop = False
        self.coeffs = {}

    def _sem(self, key):
        if key not in self.p:
            self.p[key] = self.ctx.sem(f"{self.tag}_{key}")
            self.n[key] = 0
        return self.p[key]

    def done(self, e, inst):
        inst.then_inc(self._sem(e), 1)
        self.n[e] = self.n.get(e, 0) + 1
        return self.n[e]

    def dma(self, q, inst):
        inst.then_inc(self._sem(q), 16)
        self.n[q] = self.n.get(q, 0) + 16
        return self.n[q]

    def _trk(self, ce, key):
        k = (ce, key)
        if k not in self.trk:
            assert not self.in_loop, f"trk {k} first used inside loop - predeclare"
            self.trk[k] = Trk(self.nc, self.ET[ce], self.eng[ce], self._sem(key),
                              f"{self.tag}_{ce}_{key}"[:24])
        return self.trk[k]

    def wait(self, ce, key, C):
        self._trk(ce, key).wait(C)

    def waitn(self, ce, key, lag=0):
        self._trk(ce, key).wait(self.n.get(key, 0) - lag)

    def begin_loop(self, pairs, coeffs):
        for ce, key in pairs:
            self._trk(ce, key)
        self.coeffs = dict(coeffs)
        self.loop_n0 = dict(self.n)
        self.in_loop = True

    def iter_start(self, i):
        for (ce, key), t in self.trk.items():
            if key in self.coeffs:
                t.begin_body(i, self.coeffs[key])

    def end_loop(self, trips):
        for key, co in self.coeffs.items():
            got = self.n.get(key, 0) - self.loop_n0.get(key, 0)
            assert got == co, f"{self.tag}/{key}: emitted {got} per iter != declared {co}"
            self.n[key] = self.loop_n0.get(key, 0) + co * trips
        for (ce, key), t in self.trk.items():
            if key in self.coeffs:
                if not t.body_first:
                    t.runC += self.coeffs[key] * (trips - 1)
                t.body_first = False
                t.loop_i = None
        self.in_loop = False
        self.coeffs = {}

    def free(self):
        for t in self.trk.values():
            t.free()
        self.trk = {}



P0_PAIRS = [("WqT", "wq_bf"), ("WkT", "wk_bf"), ("WvT", "wv_bf"), ("WoT", "wo_bf"),
            ("WcT", "wc_bf"), ("genes", "genes_bf"), ("WdT", "wd_bf"), ("WuT", "wu_bf"),
            ("W_z", "wz_bf"), ("W_xBC", "wxbc_bf"), ("W_dt", "wdt_bf"),
            ("conv_diag", "convdiag_bf"), ("W_out", "wout_bf")]


def emit_p0(ctx):
    nc, d = ctx.nc, ctx.d
    s = ctx.sem("p0")
    for src, dst in P0_PAIRS:
        if dst == "wq_bf":
            nc.gpsimd.dma_start(out=d[dst][:, 0:D], in_=d[src]).then_inc(s, 16)
            nc.gpsimd.dma_start(out=d[dst][:, D:D + 128], in_=d["WkT"]).then_inc(s, 16)
            ctx.cnt["p0"] = ctx.cnt.get("p0", 0) + 16
        else:
            nc.gpsimd.dma_start(out=d[dst], in_=d[src]).then_inc(s, 16)
        ctx.cnt["p0"] = ctx.cnt.get("p0", 0) + 16
    ctx.p0_total = ctx.cnt["p0"]


def emit_p1(ctx):
    nc, d = ctx.nc, ctx.d
    es = ExitStack()
    sb = lambda n, shp, dt=F32: es.enter_context(nc.sbuf_tensor(n, shp, dt))

    ident = sb("p1_ident", [128, 128], BF16)
    cosF = sb("p1_cosF", [128, H], F32)
    sinF = sb("p1_sinF", [128, H], F32)
    csst = sb("p1_csst", [128, 2], F32)
    onec = sb("p1_onec", [128, 1], BF16)
    oner = sb("p1_oner", [1, 128], BF16)
    oner32 = sb("p1_oner32", [1, 128], F32)
    onec32 = sb("p1_onec32", [128, 1], F32)
    ln1 = sb("p1_ln1", [128, ND * T], BF16)
    ktile = sb("p1_k", [128, T], BF16)
    vtm = sb("p1_v", [128, NT * 128], BF16)
    Ofm = sb("p1_O", [128, H * T], BF16)
    wqh = sb("p1_wqh", [128, ND * 128], BF16)
    qA = sb("p1_qA", [128, T], F32)
    qB = sb("p1_qB", [128, T], F32)
    qbf = sb("p1_qbf", [128, T], BF16)
    rowsbf = sb("p1_rowsbf", [128, T], BF16)
    rows32 = sb("p1_rows32", [128, 1024], F32)
    nrow = rowsbf[0:1, 0:T]
    invz = rows32[0:1, 0:512]
    zsb = rows32[0:1, 512:1024]
    nkmax = sb("p1_nkmax", [1, 1], F32)
    negub = sb("p1_negub", [1, T], BF16)
    xw = [sb(f"p1_xw{i}", [128, D], F32) for i in range(2)]
    qC = xw[0]
    stat = sb("p1_stat", [128, 8], F32)

    x0v = d["x0"].rearrange("(t p) n -> t p n", p=128)
    x1v = d["x1"].rearrange("(t p) n -> t p n", p=128)
    wqv = d["wq_bf"].rearrange("(c p) n -> p c n", p=128)
    wkv = d["wk_bf"].rearrange("(c p) n -> p c n", p=128)
    wvv = d["wv_bf"].rearrange("(c p) n -> p c n", p=128)
    wov = d["wo_bf"].rearrange("(c p) n -> p c n", p=128)

    with es:
        for eng in (nc.sync, nc.gpsimd, nc.scalar):
            eng.wait_ge(ctx.sem("p0"), ctx.p0_total)
        cn = ctx.sem("p1_cn")
        for dst, srcn in ((ident, "identity"), (cosF, "cosF"), (sinF, "sinF"),
                          (onec, "ones_col"), (oner, "ones_row"),
                          (oner32, "ones_row"), (onec32, "ones_col")):
            nc.gpsimd.dma_start(out=dst[:], in_=d[srcn]).then_inc(cn, 16)
        NCON = 7 * 16
        for eng in (nc.vector, nc.scalar, nc.tensor):
            eng.wait_ge(cn, NCON)

        with nc.psum_tensor("p1_pstr0", [128, 1024], BF16) as pstr0, \
             nc.psum_tensor("p1_pstr1", [128, 1024], BF16) as pstr1, \
             nc.sbuf_tensor("p1_lnbf0", [128, D], BF16) as lnbf0, \
             nc.sbuf_tensor("p1_lnbf1", [128, D], BF16) as lnbf1:
            pstr = [pstr0, pstr1]
            lnbf = [lnbf0, lnbf1]
            sy = Sync(ctx, "pa")
            sy.dma("xa", nc.sync.dma_start(out=xw[0][:], in_=x0v[0]))

            def p1a_tile(u, h2):
                xq = ("xa", "xb")[h2]
                sy.waitn("dv", xq)
                sy.waitn("ac", xq)
                if h2 == 0:
                    sy.waitn("sp", "dv", 2)
                    sy.dma("xb", nc.sync.dma_start(out=xw[1][:], in_=x0v[2 * u + 1]))
                elif not (isinstance(u, int) and u == NT // 2 - 1):
                    sy.waitn("sp", "dv", 2)
                    sy.dma("xa", nc.sync.dma_start(out=xw[0][:], in_=x0v[2 * u + 2]))
                sy.waitn("dv", "dv")
                sy.done("dv", nc.vector.tensor_reduce(out=stat[:, 0:1], in_=xw[h2][:],
                                                      axis=AX.X, op=OP.add))
                sy.waitn("ac", "ac")
                sy.done("ac", nc.scalar.activation(out=qA[:, 0:D], in_=xw[h2][:],
                                                   func=AF.Square, accum_out=stat[:, 1:2]))
                sy.waitn("dv", "dv")
                sy.done("dv", nc.vector.tensor_scalar_mul(stat[:, 2:3], stat[:, 0:1],
                                                          float(1.0 / D)))
                sy.waitn("dv", "dv")
                sy.done("dv", nc.vector.tensor_tensor(out=stat[:, 3:4], in0=stat[:, 2:3],
                                                      in1=stat[:, 2:3], op=OP.mult))
                sy.waitn("dv", "dv")
                sy.waitn("dv", "ac")
                sy.done("dv", nc.vector.scalar_tensor_tensor(
                    out=stat[:, 4:5], in0=stat[:, 1:2], scalar=float(1.0 / D),
                    in1=stat[:, 3:4], op0=OP.mult, op1=OP.subtract))
                sy.waitn("ac", "dv")
                sy.done("ac", nc.scalar.activation(out=stat[:, 5:6], in_=stat[:, 4:5],
                                                   func=AF.Sqrt, bias=f32c(1e-5)))
                sy.waitn("dv", "ac")
                sy.done("dv", nc.vector.reciprocal(stat[:, 6:7], stat[:, 5:6]))
                sy.waitn("dv", "dv")
                sy.done("dv", nc.vector.tensor_tensor(out=stat[:, 7:8], in0=stat[:, 2:3],
                                                      in1=stat[:, 6:7], op=OP.mult))
                sy.waitn("dv", "dv")
                sy.waitn("dv", "pe", 2)
                sy.done("dv", nc.vector.tensor_scalar(out=lnbf[h2][:], in0=xw[h2][:],
                                                      scalar1=stat[:, 6:7],
                                                      scalar2=stat[:, 7:8],
                                                      op0=OP.mult, op1=OP.subtract))
                for half in range(2):
                    sy.waitn("pe", "dv")
                    sy.waitn("pe", "dv", 8 - half)
                    inst = None
                    for c8 in range(8):
                        c = half * 8 + c8
                        inst = nc.tensor.transpose(pstr[half][:, c8 * 128:(c8 + 1) * 128],
                                                   lnbf[h2][:, c * 128:(c + 1) * 128],
                                                   ident[:])
                    sy.done("pe", inst)
                tsl = bass.ts(2 * u + h2, 128)
                for half in range(2):
                    sy.waitn("dv", "pe", 1 - half)
                    src2 = pstr[half][:].rearrange("p (c n) -> p c n", n=128)
                    dstv = ln1[:].rearrange("p (c t) -> p c t", t=T)
                    sy.done("dv", nc.vector.tensor_copy(
                        dstv[:, half * 8:(half + 1) * 8, tsl], src2))

            sy.begin_loop([("dv", "xa"), ("dv", "xb"), ("ac", "xa"), ("ac", "xb"),
                           ("sp", "dv"), ("dv", "dv"), ("dv", "ac"), ("ac", "dv"),
                           ("ac", "ac"), ("pe", "dv"), ("dv", "pe")],
                          {"xa": 16, "xb": 16, "dv": 18, "ac": 4, "pe": 4})
            with ctx.fori(NT // 2 - 1, [PE, DVE, SP, ACT]) as u:
                sy.iter_start(u)
                p1a_tile(u, 0)
                p1a_tile(u, 1)
            sy.end_loop(NT // 2 - 1)
            p1a_tile(NT // 2 - 1, 0)
            p1a_tile(NT // 2 - 1, 1)
            sy.free()
        ctx.barrier()

        with nc.psum_tensor("p1_psq", [128, 2048], F32) as psq, \
             nc.psum_tensor("p1_pss0", [128, 512], F32) as pss0, \
             nc.psum_tensor("p1_pss1", [128, 512], F32) as pss1, \
             nc.psum_tensor("p1_pso", [128, 512], F32) as pso, \
             nc.psum_tensor("p1_psz", [1, 512], F32) as psz, \
             nc.sbuf_tensor("p1_pt0", [128, 512], BF16) as pt0, \
             nc.sbuf_tensor("p1_pt1", [128, 512], BF16) as pt1, \
             nc.sbuf_tensor("p1_pt3", [128, 512], BF16) as pt3, \
             nc.sbuf_tensor("p1_invzbc", [128, 512], F32) as invzbc, \
             nc.sbuf_tensor("p1_dmask", [128, 4 * 512], BF16) as dmask:
            pss = [pss0, pss1]
            ptile = [pt0, pt1]

            sy = Sync(ctx, "pb")
            nc.gpsimd.dma_start(out=dmask[:], in_=d["diagmask"]).then_inc(cn, 16)
            nc.tensor.wait_ge(cn, NCON + 16)
            sy.dma("wk", nc.sync.dma_start(
                out=wqh[:].rearrange("p (c n) -> p c n", n=128), in_=wkv))
            sy.dma("wk", nc.sync.dma_start(
                out=qbf[:].rearrange("p (c n) -> p c n", n=128), in_=wvv))
            sy.wait("pe", "wk", 32)
            inst = None
            for tq in range(4):
                for c in range(ND):
                    inst = nc.tensor.matmul(psq[:, tq * 512:(tq + 1) * 512],
                                            wqh[:, c * 128:(c + 1) * 128],
                                            ln1[:, c * T + tq * 512: c * T + (tq + 1) * 512],
                                            start=(c == 0), stop=(c == ND - 1))
            sy.done("pe", inst)
            for tt in range(NT):
                sy.waitn("pe", "dv", 1)
                inst = None
                for c in range(ND):
                    inst = nc.tensor.matmul(pss[tt % 2][:, 0:128],
                                            ln1[:, c * T + tt * 128: c * T + (tt + 1) * 128],
                                            qbf[:, c * 128:(c + 1) * 128],
                                            start=(c == 0), stop=(c == ND - 1))
                sy.done("pe", inst)
                sy.waitn("dv", "pe")
                sy.done("dv", nc.vector.tensor_copy(vtm[:, tt * 128:(tt + 1) * 128],
                                                    pss[tt % 2][:, 0:128]))
            sy.wait("dv", "pe", 1)
            sy.done("dv", nc.vector.tensor_copy(qA[:], psq[:]))
            sy.waitn("dv", "dv")
            sy.done("dv", nc.vector.tensor_copy(ktile[:], qA[:]))
            sy.waitn("ac", "dv")
            sy.done("ac", nc.scalar.activation(out=qB[:], in_=qA[:], func=AF.Square))
            for tq in range(4):
                sy.waitn("pe", "ac")
                sy.done("pe", nc.tensor.matmul(psz[:], onec32[:],
                                               qB[:, tq * 512:(tq + 1) * 512],
                                               start=True, stop=True,
                                               skip_group_check=True))
                sy.waitn("ac", "pe")
                sy.done("ac", nc.scalar.activation(out=nrow[:, tq * 512:(tq + 1) * 512],
                                                   in_=psz[0:1, :], func=AF.Sqrt))
            sy.waitn("dv", "ac")
            sy.done("dv", nc.vector.tensor_reduce(out=nkmax[:], in_=nrow, axis=AX.X,
                                                  op=OP.max))
            sy.waitn("pe", "dv")
            sy.free()
            ctx.barrier()

            sy = Sync(ctx, "pc")
            og = [0, 4, 12, 24]
            wslice = lambda hh: wqv[:, :, bass.ts(hh, 128)]
            sy.dma("wq", nc.sync.dma_start(
                out=wqh[:].rearrange("p (c n) -> p c n", n=128), in_=wslice(0)))

            def s_group(g, o, kt, qc, par, diag):
                sy.waitn("pe", "ac", 1)
                nc.tensor.matmul(pss[par][:], ktile[:, kt * 128:(kt + 1) * 128],
                                 qbf[:, qc * 512:(qc + 1) * 512], start=True, stop=False)
                sy.done("pe", nc.tensor.matmul(pss[par][:], oner[:],
                                               negub[0:1, qc * 512:(qc + 1) * 512],
                                               start=False, stop=True))
                sy.waitn("ac", "pe")
                sy.waitn("ac", "pe", 3)
                sy.done("ac", nc.scalar.activation(out=ptile[par][:], in_=pss[par][:],
                                                   func=AF.Exp, scale=f32c(SC_ATT)))
                if diag is not None:
                    sy.waitn("dv", "ac")
                    sy.done("dv", nc.vector.tensor_tensor(
                        out=ptile[par][:], in0=ptile[par][:],
                        in1=dmask[:, diag * 512:(diag + 1) * 512], op=OP.mult))

            def pv_group(g, o, kt, qc, par, first, diag):
                if diag is not None:
                    sy.waitn("pe", "dv")
                else:
                    sy.waitn("pe", "ac")
                nc.tensor.matmul(pso[:], vtm[:, kt * 128:(kt + 1) * 128], ptile[par][:],
                                 start=first, stop=False, skip_group_check=True)
                sy.done("pe", nc.tensor.matmul(psz[:], onec[:], ptile[par][:],
                                               start=first, stop=False,
                                               skip_group_check=True))

            def head_body(h):
                sy.waitn("pe", "wq")
                sy.waitn("pe", "dv", 38)
                inst = None
                for tq in range(4):
                    for c in range(ND):
                        inst = nc.tensor.matmul(
                            psq[:, tq * 512:(tq + 1) * 512],
                            wqh[:, c * 128:(c + 1) * 128],
                            ln1[:, c * T + tq * 512: c * T + (tq + 1) * 512],
                            start=(c == 0), stop=(c == ND - 1))
                sy.done("pe", inst)
                sy.waitn("sp", "pe")
                sy.dma("wq", nc.sync.dma_start(
                    out=wqh[:].rearrange("p (c n) -> p c n", n=128), in_=wslice(h + 1)))
                sy.waitn("po", "dv", 34)
                hx = bass.ts(h, 1)
                nc.gpsimd.tensor_copy(csst[:, 0:1], cosF[:, hx])
                sy.done("po", nc.gpsimd.tensor_copy(csst[:, 1:2], sinF[:, hx]))
                sy.waitn("dv", "pe")
                sy.waitn("dv", "ac", 44)
                sy.done("dv", nc.vector.tensor_copy(qA[:], psq[:]))
                sy.waitn("dv", "dv")
                sy.done("dv", nc.vector.tensor_copy(qC[0:64, :], qA[64:128, :]))
                sy.done("dv", nc.vector.tensor_copy(qC[64:128, :], qA[0:64, :]))
                sy.waitn("dv", "po")
                sy.done("dv", nc.vector.tensor_scalar_mul(qB[:], qA[:], csst[:, 0:1]))
                sy.waitn("dv", "dv")
                sy.done("dv", nc.vector.scalar_tensor_tensor(out=qB[:], in0=qC[:],
                                                             scalar=csst[:, 1:2], in1=qB[:],
                                                             op0=OP.mult, op1=OP.add))
                sy.waitn("dv", "dv")
                sy.waitn("dv", "pe", 1)
                sy.done("dv", nc.vector.tensor_copy(qbf[:], qB[:]))
                sy.waitn("ac", "dv")
                sy.done("ac", nc.scalar.activation(out=qA[:], in_=qB[:], func=AF.Square))
                for tq in range(4):
                    sy.waitn("pe", "ac")
                    sy.waitn("pe", "dv", 9)
                    sy.done("pe", nc.tensor.matmul(psz[:], onec32[:],
                                                   qA[:, tq * 512:(tq + 1) * 512],
                                                   start=True, stop=True,
                                                   skip_group_check=True))
                    sy.waitn("ac", "pe")
                    sy.done("ac", nc.scalar.activation(
                        out=nrow[:, tq * 512:(tq + 1) * 512], in_=psz[0:1, :],
                        func=AF.Sqrt))
                sy.waitn("dv", "ac")
                sy.waitn("dv", "pe", 5)
                sy.done("dv", nc.vector.tensor_scalar(out=negub[0:1, :], in0=nrow,
                                                      scalar1=nkmax[0:1, 0:1],
                                                      scalar2=-1.0,
                                                      op0=OP.mult, op1=OP.mult))
                for qc in range(4):
                    sy.waitn("pe", "dv")
                    g = 0
                    for kt in range(4 * qc + 4):
                        diag = kt - 4 * qc if kt >= 4 * qc else None
                        o = og[qc] + (4 + kt if diag is None else diag)
                        par = g % 2
                        s_group(g, o, kt, qc, par, diag)
                        pv_group(g, o, kt, qc, par, first=(g == 0), diag=diag)
                        g += 1
                    sy.waitn("dv", "pe")
                    sy.done("dv", nc.vector.tensor_copy(zsb, psz[0:1, :]))
                    sy.waitn("dv", "dv")
                    sy.done("dv", nc.vector.reciprocal(invz, zsb))
                    sy.waitn("pe", "dv")
                    sy.waitn("pe", "ac")
                    sy.done("pe", nc.tensor.matmul(pss[0][:], oner32[:], invz,
                                                   start=True, stop=True,
                                                   skip_group_check=True))
                    sy.waitn("dv", "pe")
                    sy.done("dv", nc.vector.tensor_copy(invzbc[:], pss[0][:]))
                    sy.waitn("dv", "dv")
                    sy.waitn("dv", "po")
                    sy.done("dv", nc.vector.tensor_tensor(out=pt3[:], in0=pso[:],
                                                          in1=invzbc[:], op=OP.mult))
                    sy.waitn("po", "dv")
                    sy.done("po", nc.gpsimd.tensor_copy(
                        Ofm[:, bass.ts(4 * h + qc, 512)], pt3[:]))

            pairs = [("pe", "wq"), ("pe", "dv"), ("pe", "ac"), ("sp", "pe"),
                     ("dv", "pe"), ("dv", "ac"), ("dv", "dv"), ("dv", "po"),
                     ("ac", "dv"), ("ac", "pe"), ("ac", "ac"), ("po", "dv"),
                     ("po", "pe")]
            sy.begin_loop(pairs, {"wq": 16, "pe": 89, "dv": 39, "ac": 45, "po": 5})
            with ctx.fori(H, [PE, ACT, DVE, SP, Pool]) as h:
                sy.iter_start(h)
                head_body(h)
            sy.end_loop(H)
            nc.sync.wait_ge(sy.p["wq"], 16 * (H + 1))
            sy.free()
            ctx.barrier()

            sy = Sync(ctx, "pd")
            sy.dma("wo", nc.sync.dma_start(
                out=ln1[:].rearrange("p (c n) -> p c n", n=2048), in_=wov))
            nc.tensor.wait_ge(sy.p["wo"], 16)

            def p1d_tile(u, par):
                xq = ("xa", "xb")[par]
                oq = ("oa", "ob")[par]
                sy.waitn("sp", "dv", 1)
                sy.dma(xq, nc.sync.dma_start(out=xw[par][:], in_=x0v[2 * u + par]))
                ostg = (qbf, ktile)[par]
                sy.waitn("po", "pe", 4)
                ov = Ofm[:].rearrange("p (g t) -> p g t", t=T)
                sy.done("po", nc.gpsimd.tensor_copy(
                    ostg[:].rearrange("p (g n) -> p g n", n=128),
                    ov[:, :, bass.ts(2 * u + par, 128)]))
                sy.waitn("pe", "po")
                sy.waitn("pe", "dv")
                for dp in range(4):
                    inst = None
                    for hh in range(H):
                        inst = nc.tensor.matmul(
                            psq[:, dp * 512:(dp + 1) * 512],
                            ostg[:, hh * 128:(hh + 1) * 128],
                            ln1[:, hh * 2048 + dp * 512: hh * 2048 + (dp + 1) * 512],
                            start=(hh == 0), stop=(hh == H - 1))
                    sy.done("pe", inst)
                sy.waitn("dv", "pe")
                sy.waitn("dv", xq)
                sy.waitn("dv", oq)
                x1b = (qA, qB)[par]
                sy.done("dv", nc.vector.tensor_tensor(out=x1b[:], in0=psq[:],
                                                      in1=xw[par][:], op=OP.add))
                sy.waitn("ac", "dv")
                sy.dma(oq, nc.scalar.dma_start(out=x1v[2 * u + par], in_=x1b[:]))

            sy.begin_loop([("sp", "dv"), ("po", "pe"), ("pe", "po"), ("pe", "dv"),
                           ("dv", "pe"), ("dv", "xa"), ("dv", "xb"), ("dv", "oa"),
                           ("dv", "ob"), ("ac", "dv")],
                          {"xa": 16, "xb": 16, "oa": 16, "ob": 16,
                           "pe": 8, "dv": 2, "po": 2})
            with ctx.fori(NT // 2, [PE, DVE, SP, ACT, Pool]) as u:
                sy.iter_start(u)
                p1d_tile(u, 0)
                p1d_tile(u, 1)
            sy.end_loop(NT // 2)
            nc.sync.wait_ge(sy.p["oa"], 16 * NT // 2)
            nc.sync.wait_ge(sy.p["ob"], 16 * NT // 2)
            sy.free()
        ctx.barrier()


EPS32 = float(np.finfo(np.float32).eps)


def emit_p2(ctx):
    nc, d = ctx.nc, ctx.d
    es = ExitStack()
    sb = lambda n, shp, dt=F32: es.enter_context(nc.sbuf_tensor(n, shp, dt))

    ident = sb("p2_ident", [128, 128], BF16)
    wc = sb("p2_wcT", [128, 16 * 2048], BF16)
    wcs = [sb(f"p2_wcs{i}", [128, 16 * 512], BF16) for i in range(2)]
    genes = sb("p2_genes", [128, 32 * 256], BF16)
    wd = sb("p2_wd", [128, 2 * 2048], BF16)
    wu = sb("p2_wu", [128, 2 * 2048], BF16)
    normw = sb("p2_normw", [128, D], BF16)
    invtemp = sb("p2_invtemp", [128, 1], F32)
    scalesc = sb("p2_scale", [128, 1], F32)
    stat = sb("p2_stat", [128, 12], F32)
    xw = [sb(f"p2_xw{i}", [128, D], F32) for i in range(2)]
    xn_tm = sb("p2_xntm", [128, D], F32)
    xnbf = sb("p2_xnbf", [128, D], BF16)
    xnfm = sb("p2_xnfm", [128, ND * 128], BF16)
    logits = sb("p2_logits", [128, 4096], F32)
    lrep = sb("p2_lrep", [128, 4096], F32)
    down = lrep[:, 0:2048]
    up = lrep[:, 2048:4096]
    mxall = sb("p2_mx", [128, 32], F32)
    negm = sb("p2_negm", [128, 1], F32)
    zrow = sb("p2_z", [128, 2], F32)
    ebuf = sb("p2_e", [128, 4096], BF16)
    wT = ebuf
    wbuf = lrep[:, 0:2048].bitcast(BF16)
    exprs = sb("p2_expr", [128, 2 * 128], BF16)

    x1v = d["x1"].rearrange("(t p) n -> t p n", p=128)
    x2v = d["x2"].rearrange("(t p) n -> t p n", p=128)
    wcv = d["wc_bf"].rearrange("(c p) n -> p c n", p=128)
    gnv = d["genes_bf"].rearrange("(g p) n -> p g n", p=128)
    wdv = d["wd_bf"].rearrange("(c p) n -> p c n", p=128)
    wuv = d["wu_bf"].rearrange("(c p) n -> p c n", p=128)

    with es, \
         nc.psum_tensor("p2_ptr0", [128, 1024], BF16) as ptr0, \
         nc.psum_tensor("p2_ptr1", [128, 1024], BF16) as ptr1, \
         nc.psum_tensor("p2_plg0", [128, 512], F32) as plg0, \
         nc.psum_tensor("p2_plg1", [128, 512], F32) as plg1, \
         nc.psum_tensor("p2_plg2", [128, 512], F32) as plg2, \
         nc.psum_tensor("p2_plg3", [128, 512], F32) as plg3, \
         nc.psum_tensor("p2_pex", [128, 256], F32) as pex:
        ptr = [ptr0, ptr1]
        plg = [plg0, plg1, plg2, plg3]
        sy = Sync(ctx, "p2")
        cn = ctx.sem("p2_cn")
        nc.gpsimd.dma_start(out=ident[:], in_=d["identity"]).then_inc(cn, 16)
        nc.gpsimd.dma_start(out=wc[:].rearrange("p (c n) -> p c n", n=2048),
                            in_=wcv[:, :, 0:2048]).then_inc(cn, 16)
        nc.gpsimd.dma_start(out=genes[:].rearrange("p (g n) -> p g n", n=256),
                            in_=gnv).then_inc(cn, 16)
        nc.gpsimd.dma_start(out=wd[:].rearrange("p (c n) -> p c n", n=2048),
                            in_=wdv).then_inc(cn, 16)
        nc.gpsimd.dma_start(out=wu[:].rearrange("p (c n) -> p c n", n=2048),
                            in_=wuv).then_inc(cn, 16)
        nc.gpsimd.dma_start(out=normw[:], in_=d["l0_normw_bc"]).then_inc(cn, 16)
        nc.gpsimd.dma_start(out=invtemp[:], in_=d["l0_temp_bc"]).then_inc(cn, 16)
        nc.gpsimd.dma_start(out=scalesc[:], in_=d["l0_scale_bc"]).then_inc(cn, 16)
        NCN = 8 * 16
        for e in (nc.vector, nc.scalar, nc.tensor, nc.sync):
            e.wait_ge(cn, NCN)
        dvp = ctx.sem("p2_pre")
        nc.vector.tensor_scalar_max(invtemp[:], invtemp[:], 0.1).then_inc(dvp, 1)
        nc.vector.wait_ge(dvp, 1)
        nc.vector.reciprocal(invtemp[:], invtemp[:]).then_inc(dvp, 1)
        for e in (nc.scalar, nc.tensor):
            e.wait_ge(dvp, 2)
        sy.dma("xa", nc.sync.dma_start(out=xw[0][:], in_=x1v[0]))

        def tile(u, par, prefetch=True):
            xq = ("xa", "xb")[par]
            oq = ("oa", "ob")[par]
            tt = 2 * u + par
            b = dict(sy.n)
            sy.waitn("pe", "dv")
            sy.waitn("dv", "pe")
            sy.waitn("ac", "dv")
            if prefetch:
                sy.waitn("sp", ("ob", "oa")[par])
                sy.dma(("xb", "xa")[par],
                       nc.sync.dma_start(out=xw[1 - par][:], in_=x1v[tt + 1]))
            sy.waitn("dv", xq)
            sy.waitn("ac", xq)
            sy.waitn("dv", "dv")
            sy.done("dv", nc.vector.tensor_reduce(out=stat[:, 0:1], in_=xw[par][:],
                                                  axis=AX.X, op=OP.add))
            sy.waitn("ac", "ac")
            sy.done("ac", nc.scalar.activation(out=lrep[:, 0:D], in_=xw[par][:],
                                               func=AF.Square, accum_out=stat[:, 1:2]))
            sy.waitn("dv", "dv")
            sy.done("dv", nc.vector.tensor_scalar_mul(stat[:, 2:3], stat[:, 0:1],
                                                      float(1.0 / D)))
            sy.waitn("dv", "dv")
            sy.done("dv", nc.vector.tensor_tensor(out=stat[:, 3:4], in0=stat[:, 2:3],
                                                  in1=stat[:, 2:3], op=OP.mult))
            sy.waitn("dv", "dv")
            sy.waitn("dv", "ac")
            sy.done("dv", nc.vector.scalar_tensor_tensor(out=stat[:, 4:5], in0=stat[:, 1:2],
                                                         scalar=float(1.0 / D),
                                                         in1=stat[:, 3:4],
                                                         op0=OP.mult, op1=OP.subtract))
            sy.waitn("ac", "dv")
            sy.done("ac", nc.scalar.activation(out=stat[:, 5:6], in_=stat[:, 4:5],
                                               func=AF.Sqrt, bias=f32c(1e-5)))
            sy.waitn("dv", "ac")
            sy.done("dv", nc.vector.reciprocal(stat[:, 6:7], stat[:, 5:6]))
            sy.waitn("dv", "dv")
            sy.done("dv", nc.vector.tensor_tensor(out=stat[:, 7:8], in0=stat[:, 4:5],
                                                  in1=stat[:, 6:7], op=OP.mult))
            sy.waitn("dv", "dv")
            sy.done("dv", nc.vector.tensor_tensor(out=stat[:, 8:9], in0=stat[:, 7:8],
                                                  in1=stat[:, 6:7], op=OP.mult))
            sy.waitn("ac", "dv")
            sy.done("ac", nc.scalar.activation(out=stat[:, 9:10], in_=stat[:, 8:9],
                                               func=AF.Sqrt, bias=f32c(EPS32)))
            sy.waitn("dv", "ac")
            sy.done("dv", nc.vector.reciprocal(stat[:, 10:11], stat[:, 9:10]))
            sy.waitn("dv", "dv")
            sy.done("dv", nc.vector.tensor_tensor(out=stat[:, 10:11], in0=stat[:, 10:11],
                                                  in1=stat[:, 6:7], op=OP.mult))
            sy.waitn("dv", "dv")
            sy.done("dv", nc.vector.tensor_tensor(out=stat[:, 11:12], in0=stat[:, 2:3],
                                                  in1=stat[:, 10:11], op=OP.mult))
            sy.waitn("dv", "dv")
            sy.done("dv", nc.vector.tensor_scalar(out=xn_tm[:], in0=xw[par][:],
                                                  scalar1=stat[:, 10:11],
                                                  scalar2=stat[:, 11:12],
                                                  op0=OP.mult, op1=OP.subtract))
            sy.waitn("dv", "dv")
            sy.done("dv", nc.vector.tensor_tensor(out=xn_tm[:], in0=xn_tm[:],
                                                  in1=normw[:], op=OP.mult))
            sy.waitn("dv", "dv")
            r_xnbf = sy.done("dv", nc.vector.tensor_copy(xnbf[:], xn_tm[:]))
            for half in range(2):
                sy.wait("pe", "dv", r_xnbf)
                inst = None
                for c8 in range(8):
                    c = half * 8 + c8
                    inst = nc.tensor.transpose(ptr[half][:, c8 * 128:(c8 + 1) * 128],
                                               xnbf[:, c * 128:(c + 1) * 128], ident[:])
                sy.done("pe", inst)
            r_trev = []
            for half in range(2):
                sy.waitn("dv", "pe")
                src2 = ptr[half][:].rearrange("p (c n) -> p c n", n=128)
                r_trev.append(sy.done("dv", nc.vector.tensor_copy(
                    xnfm[:].rearrange("p (c n) -> p c n", n=128)[:, half * 8:(half + 1) * 8, :],
                    src2)))
            sy.waitn("sp", "pe")
            for gc in (4, 5):
                sy.dma(("wca", "wcb")[gc % 2], nc.sync.dma_start(
                    out=wcs[gc % 2][:].rearrange("p (c n) -> p c n", n=512),
                    in_=wcv[:, :, gc * 512:(gc + 1) * 512]))
            r_lg = []
            r_mm = []
            for gc in range(8):
                sy.waitn("pe", "dv")
                if gc >= 4:
                    sy.wait("pe", "dv", r_lg[gc - 4])
                    sy.waitn("pe", ("wca", "wcb")[gc % 2])
                inst = None
                for c in range(ND):
                    if gc < 4:
                        wsl = wc[:, c * 2048 + gc * 512: c * 2048 + (gc + 1) * 512]
                    else:
                        wsl = wcs[gc % 2][:, c * 512:(c + 1) * 512]
                    inst = nc.tensor.matmul(
                        plg[gc % 4][:], xnfm[:, c * 128:(c + 1) * 128], wsl,
                        start=(c == 0), stop=(c == ND - 1))
                r_mm.append(sy.done("pe", inst))
                if gc in (4, 5):
                    sy.wait("sp", "pe", r_mm[gc])
                    g2 = gc + 2
                    sy.dma(("wca", "wcb")[g2 % 2], nc.sync.dma_start(
                        out=wcs[g2 % 2][:].rearrange("p (c n) -> p c n", n=512),
                        in_=wcv[:, :, g2 * 512:(g2 + 1) * 512]))
                sy.waitn("dv", "pe")
                r_lg.append(sy.done("dv", nc.vector.tensor_scalar_mul(
                    logits[:, gc * 512:(gc + 1) * 512], plg[gc % 4][:], invtemp[:, 0:1])))
            sy.waitn("dv", "dv")
            sy.done("dv", nc.vector.max(out=mxall[:, 0:8], in_=logits[:]))
            sy.waitn("dv", "dv")
            sy.done("dv", nc.vector.match_replace(out=lrep[:], in_to_replace=mxall[:, 0:8],
                                                  in_values=logits[:], imm_value=-1e30))
            for rnd in range(1, 4):
                sy.waitn("dv", "dv")
                sy.done("dv", nc.vector.max(out=mxall[:, 8 * rnd:8 * rnd + 8], in_=lrep[:]))
                if rnd < 3:
                    sy.waitn("dv", "dv")
                    sy.done("dv", nc.vector.match_replace(
                        out=lrep[:], in_to_replace=mxall[:, 8 * rnd:8 * rnd + 8],
                        in_values=lrep[:], imm_value=-1e30))
            sy.waitn("dv", "dv")
            sy.done("dv", nc.vector.tensor_scalar_mul(negm[:, 0:1], mxall[:, 0:1], -1.0))
            sy.waitn("ac", "dv")
            r_exp = sy.done("ac", nc.scalar.activation(out=ebuf[:], in_=logits[:],
                                                       func=AF.Exp, bias=negm[:, 0:1]))
            sy.wait("dv", "ac", r_exp)
            r_w = sy.done("dv", nc.vector.scalar_tensor_tensor(
                out=wbuf, in0=logits[:], scalar=mxall[:, 31:32], in1=ebuf[:],
                op0=OP.is_ge, op1=OP.mult, accum_out=zrow[:, 0:1]))
            sy.waitn("dv", "dv")
            r_invz = sy.done("dv", nc.vector.reciprocal(zrow[:, 1:2], zrow[:, 0:1]))
            r_wtev = []
            for g4 in range(4):
                sy.wait("pe", "dv", r_w)
                if g4 >= 2:
                    sy.wait("pe", "dv", r_wtev[g4 - 2])
                inst = None
                for c8 in range(8):
                    c = g4 * 8 + c8
                    inst = nc.tensor.transpose(ptr[g4 % 2][:, c8 * 128:(c8 + 1) * 128],
                                               wbuf[:, c * 128:(c + 1) * 128], ident[:])
                sy.done("pe", inst)
                sy.waitn("dv", "pe")
                src2 = ptr[g4 % 2][:].rearrange("p (c n) -> p c n", n=128)
                r_wtev.append(sy.done("dv", nc.vector.tensor_copy(
                    wT[:].rearrange("p (c n) -> p c n", n=128)[:, g4 * 8:(g4 + 1) * 8, :],
                    src2)))
            for ch in range(2):
                sy.waitn("pe", "dv")
                inst = None
                for g in range(32):
                    inst = nc.tensor.matmul(
                        pex[:, ch * 128:(ch + 1) * 128],
                        genes[:, g * 256 + ch * 128: g * 256 + (ch + 1) * 128],
                        wT[:, g * 128:(g + 1) * 128],
                        start=(g == 0), stop=(g == 31))
                sy.done("pe", inst)
            sy.waitn("dv", "pe")
            r_expr = sy.done("dv", nc.vector.tensor_copy(exprs[:], pex[:]))
            for dp in range(4):
                sy.wait("pe", "dv", r_expr)
                inst = None
                for ch in range(2):
                    inst = nc.tensor.matmul(
                        plg[dp][:], exprs[:, ch * 128:(ch + 1) * 128],
                        wd[:, ch * 2048 + dp * 512: ch * 2048 + (dp + 1) * 512],
                        start=(ch == 0), stop=(ch == 1))
                sy.done("pe", inst)
                sy.waitn("ac", "pe")
                sy.wait("ac", "dv", r_invz)
                sy.done("ac", nc.scalar.activation(out=down[:, dp * 512:(dp + 1) * 512],
                                                   in_=plg[dp][:], func=AF.Tanh,
                                                   scale=zrow[:, 1:2]))
            for dp in range(4):
                sy.waitn("pe", "ac")
                inst = None
                for ch in range(2):
                    inst = nc.tensor.matmul(
                        plg[dp][:], exprs[:, ch * 128:(ch + 1) * 128],
                        wu[:, ch * 2048 + dp * 512: ch * 2048 + (dp + 1) * 512],
                        start=(ch == 0), stop=(ch == 1))
                sy.done("pe", inst)
                sy.waitn("dv", "pe")
                sy.done("dv", nc.vector.tensor_scalar_mul(
                    up[:, dp * 512:(dp + 1) * 512], plg[dp][:], zrow[:, 1:2]))
            sy.waitn("dv", "dv")
            sy.waitn("dv", "ac")
            sy.done("dv", nc.vector.tensor_tensor(out=down[:], in0=down[:], in1=xn_tm[:],
                                                  op=OP.mult))
            sy.waitn("dv", "dv")
            sy.done("dv", nc.vector.tensor_tensor(out=down[:], in0=down[:], in1=up[:],
                                                  op=OP.mult))
            sy.waitn("dv", "dv")
            r_x2 = sy.done("dv", nc.vector.scalar_tensor_tensor(
                out=xw[par][:], in0=down[:], scalar=scalesc[:, 0:1], in1=xw[par][:],
                op0=OP.mult, op1=OP.add))
            sy.wait("ac", "dv", r_x2)
            sy.dma(oq, nc.scalar.dma_start(out=x2v[tt], in_=xw[par][:]))

        N_DV, N_AC, N_PE = 45, 8, 24
        pairs = [("sp", "oa"), ("sp", "ob"), ("sp", "pe"),
                 ("dv", "xa"), ("dv", "xb"), ("ac", "xa"), ("ac", "xb"),
                 ("dv", "dv"), ("dv", "ac"), ("dv", "pe"),
                 ("ac", "dv"), ("ac", "ac"), ("ac", "pe"), ("pe", "dv"), ("pe", "ac"),
                 ("pe", "wca"), ("pe", "wcb")]
        sy.begin_loop(pairs, {"xa": 16, "xb": 16, "oa": 16, "ob": 16,
                              "wca": 2 * 32, "wcb": 2 * 32,
                              "dv": 2 * N_DV, "ac": 2 * N_AC, "pe": 2 * N_PE})
        with ctx.fori(NT // 2 - 1, [PE, DVE, SP, ACT]) as u:
            sy.iter_start(u)
            tile(u, 0)
            tile(u, 1)
        sy.end_loop(NT // 2 - 1)
        tile(NT // 2 - 1, 0)
        tile(NT // 2 - 1, 1, prefetch=False)
        nc.sync.wait_ge(sy.p["oa"], 16 * NT // 2)
        nc.sync.wait_ge(sy.p["ob"], 16 * NT // 2)
        sy.free()
    ctx.barrier()


def emit_p3(ctx):
    nc, d = ctx.nc, ctx.d
    es = ExitStack()
    sb = lambda n, shp, dt=F32: es.enter_context(nc.sbuf_tensor(n, shp, dt))

    ident = sb("p3_ident", [128, 128], BF16)
    ident32 = sb("p3_id32", [128, 128], F32)
    onesr32 = sb("p3_oner32", [1, 128], F32)
    stat = sb("p3_stat", [128, 8], F32)
    tri = sb("p3_tri", [128, 128], F32)
    es_bc = ExitStack()
    sb2 = lambda n, shp, dt: es_bc.enter_context(nc.sbuf_tensor(n, shp, dt))
    Bfm = sb2("p3_Bfm", [128, T], BF16)
    B_tm = sb2("p3_Btm", [128, T], BF16)
    Cbf = sb2("p3_Cbf", [128, T], BF16)
    Cfm32 = sb2("p3_C32", [128, T], F32)
    dtf = sb2("p3_dtf", [64, T], F32)
    cumf = sb2("p3_cumf", [64, T], F32)
    Acol = sb2("p3_Acol", [64, 2], F32)
    dtb = sb2("p3_dtb", [64, 1], F32)
    zer64 = sb2("p3_zer64", [64, 128], F32)
    convb = sb2("p3_convb", [128, NCC], F32)
    cbst = sb2("p3_cbst", [128, 1], F32)

    x2v = d["x2"].rearrange("(t p) n -> t p n", p=128)
    x3v = d["x3"].rearrange("(t p) n -> t p n", p=128)
    wzv = d["wz_bf"].rearrange("(c p) n -> p c n", p=128)
    wxv = d["wxbc_bf"].rearrange("(c p) n -> p c n", p=128)
    wdtv = d["wdt_bf"].rearrange("(c p) n -> p c n", p=128)
    wov = d["wout_bf"].rearrange("(c p) n -> p c n", p=128)
    cdv = d["convdiag_bf"].rearrange("(j c p) n -> p j c n", c=NCC, p=128)
    zsv = d["z_scr"].rearrange("(t p) n -> t p n", p=128)
    xtv = d["xtm_scr"].rearrange("(t p) n -> t p n", p=128)
    ynv = d["ynorm_scr"].rearrange("(c p) t -> p c t", p=128)
    cumd = d["cum_scr"]

    with es:
        cn = ctx.sem("p3cn")
        ncn = 0
        for dst, src in ((ident[:], d["identity"]), (ident32[:], d["identity"]),
                         (onesr32[:], d["ones_row"]), (tri[:], d["tri01"]),
                         (zer64[:], d["diagmask"][0:64, 1536:1664]),
                         (convb[:], d["conv_b_fm"]),
                         (Acol[:, 0:1], d["A_log"]), (dtb[:], d["dt_bias"])):
            nc.gpsimd.dma_start(out=dst, in_=src).then_inc(cn, 16)
            ncn += 16
        for e in (nc.vector, nc.scalar, nc.tensor, nc.sync, nc.gpsimd):
            e.wait_ge(cn, ncn)

        with nc.sbuf_tensor("p3_ln3", [128, ND * T], BF16) as ln3:
            lnv = ln3[:].rearrange("p (c t) -> p c t", t=T)
            with nc.psum_tensor("p3a_ps0", [128, 1024], BF16) as pstr0, \
                 nc.psum_tensor("p3a_ps1", [128, 1024], BF16) as pstr1, \
                 nc.sbuf_tensor("p3a_lnb", [128, D], BF16) as lnbf, \
                 nc.sbuf_tensor("p3a_xw", [128, D], F32) as xw, \
                 nc.sbuf_tensor("p3a_sq", [128, D], F32) as sqs:
                pstr = [pstr0, pstr1]
                sy = Sync(ctx, "p3a")

                def p3a_tile(u):
                    sy.waitn("ac", "dv")
                    sy.dma("xa", nc.scalar.dma_start(out=xw[:], in_=x2v[u]))
                    sy.waitn("dv", "xa")
                    sy.waitn("ac", "xa")
                    sy.waitn("dv", "dv")
                    sy.done("dv", nc.vector.tensor_reduce(out=stat[:, 0:1], in_=xw[:],
                                                          axis=AX.X, op=OP.add))
                    sy.waitn("ac", "ac")
                    sy.done("ac", nc.scalar.activation(out=sqs[:], in_=xw[:],
                                                       func=AF.Square,
                                                       accum_out=stat[:, 1:2]))
                    sy.waitn("dv", "dv")
                    sy.done("dv", nc.vector.tensor_scalar_mul(stat[:, 2:3], stat[:, 0:1],
                                                              float(1.0 / D)))
                    sy.waitn("dv", "dv")
                    sy.done("dv", nc.vector.tensor_tensor(out=stat[:, 3:4],
                                                          in0=stat[:, 2:3],
                                                          in1=stat[:, 2:3], op=OP.mult))
                    sy.waitn("dv", "dv")
                    sy.waitn("dv", "ac")
                    sy.done("dv", nc.vector.scalar_tensor_tensor(
                        out=stat[:, 4:5], in0=stat[:, 1:2], scalar=float(1.0 / D),
                        in1=stat[:, 3:4], op0=OP.mult, op1=OP.subtract))
                    sy.waitn("ac", "dv")
                    sy.done("ac", nc.scalar.activation(out=stat[:, 5:6], in_=stat[:, 4:5],
                                                       func=AF.Sqrt, bias=f32c(1e-5)))
                    sy.waitn("dv", "ac")
                    sy.done("dv", nc.vector.reciprocal(stat[:, 6:7], stat[:, 5:6]))
                    sy.waitn("dv", "dv")
                    sy.done("dv", nc.vector.tensor_tensor(out=stat[:, 7:8],
                                                          in0=stat[:, 2:3],
                                                          in1=stat[:, 6:7], op=OP.mult))
                    sy.waitn("dv", "dv")
                    sy.waitn("dv", "pe")
                    sy.done("dv", nc.vector.tensor_scalar(out=lnbf[:], in0=xw[:],
                                                          scalar1=stat[:, 6:7],
                                                          scalar2=stat[:, 7:8],
                                                          op0=OP.mult, op1=OP.subtract))
                    for half in range(2):
                        sy.waitn("pe", "dv")
                        sy.waitn("pe", "dv", 8 - half)
                        inst = None
                        for c8 in range(8):
                            c = half * 8 + c8
                            inst = nc.tensor.transpose(
                                pstr[half][:, c8 * 128:(c8 + 1) * 128],
                                lnbf[:, c * 128:(c + 1) * 128], ident[:])
                        sy.done("pe", inst)
                    tsl = bass.ts(u, 128)
                    for half in range(2):
                        sy.waitn("dv", "pe", 1 - half)
                        src2 = pstr[half][:].rearrange("p (c n) -> p c n", n=128)
                        sy.done("dv", nc.vector.tensor_copy(
                            lnv[:, half * 8:(half + 1) * 8, tsl], src2))

                sy.begin_loop([("dv", "xa"), ("ac", "xa"), ("dv", "dv"), ("dv", "ac"),
                               ("ac", "dv"), ("ac", "ac"), ("pe", "dv"), ("dv", "pe")],
                              {"xa": 16, "dv": 9, "ac": 2, "pe": 2})
                with ctx.fori(NT, [PE, DVE, ACT]) as u:
                    sy.iter_start(u)
                    p3a_tile(u)
                sy.end_loop(NT)
                sy.free()
            ctx.barrier()

            with nc.psum_tensor("p3b_pz", [128, 2048], F32) as pzh, \
                 nc.sbuf_tensor("p3b_wz", [128, 16 * 2048], BF16) as wzr, \
                 nc.sbuf_tensor("p3b_ls", [128, 16 * 128], BF16) as lst, \
                 nc.sbuf_tensor("p3b_ze", [128, 2048], BF16) as zevh:
                sy = Sync(ctx, "p3b")

                def p3b_tile(u, half):
                    sy.waitn("dv", "pe")
                    sy.done("dv", nc.vector.tensor_copy(
                        lst[:].rearrange("p (c n) -> p c n", n=128),
                        lnv[:, :, bass.ts(u, 128)]))
                    sy.waitn("pe", "dv")
                    for g in range(4):
                        inst = None
                        for c in range(ND):
                            inst = nc.tensor.matmul(
                                pzh[:, g * 512:(g + 1) * 512],
                                lst[:, c * 128:(c + 1) * 128],
                                wzr[:, c * 2048 + g * 512: c * 2048 + (g + 1) * 512],
                                start=(c == 0), stop=(c == ND - 1))
                        sy.done("pe", inst)
                    sy.waitn("dv", "pe")
                    sy.waitn("dv", "zs")
                    sy.done("dv", nc.vector.tensor_copy(zevh[:], pzh[:]))
                    sy.waitn("ac", "dv")
                    sy.dma("zs", nc.scalar.dma_start(
                        out=zsv[u][:, half * 2048:(half + 1) * 2048], in_=zevh[:]))

                for half in range(2):
                    if half == 1:
                        sy.wait("sp", "pe", sy.n["pe"])
                    sy.dma("wz", nc.sync.dma_start(
                        out=wzr[:].rearrange("p (c n) -> p c n", n=2048),
                        in_=wzv[:, :, half * 2048:(half + 1) * 2048]))
                    sy.wait("pe", "wz", 16 * (half + 1))
                    sy.begin_loop([("pe", "dv"), ("dv", "pe"), ("dv", "zs"),
                                   ("ac", "dv")],
                                  {"pe": 4, "dv": 2, "zs": 16})
                    with ctx.fori(NT, [PE, DVE, ACT]) as u:
                        sy.iter_start(u)
                        p3b_tile(u, half)
                    sy.end_loop(NT)
                nc.sync.wait_ge(sy.p["zs"], sy.n["zs"])
                sy.free()
            ctx.barrier()

            with nc.psum_tensor("p3c_px", [128, 2048], F32) as px, \
                 nc.psum_tensor("p3c_pc", [128, 1024], F32) as pc, \
                 nc.psum_tensor("p3c_ptr", [128, 2048], BF16) as ptr, \
                 nc.sbuf_tensor("p3c_wx", [128, 16 * 128], BF16) as wxs, \
                 nc.sbuf_tensor("p3c_dg", [128, 4 * 128], BF16) as dgb, \
                 nc.sbuf_tensor("p3c_xraw", [128, 2051], BF16) as xraw, \
                 nc.sbuf_tensor("p3c_postc", [128, 2048], BF16) as postc, \
                 nc.sbuf_tensor("p3c_xtsb", [128, 2048], BF16) as xtsb:
                nc.gpsimd.dma_start(out=xraw[:, 0:3],
                                    in_=d["diagmask"][:, 1536:1539]).then_inc(cn, 16)
                ncn += 16
                for e in (nc.vector, nc.tensor):
                    e.wait_ge(cn, ncn)
                sy = Sync(ctx, "p3c")

                def p3c_core(cc):
                    sy.waitn("sp", "pe")
                    sy.dma("wx", nc.sync.dma_start(
                        out=wxs[:].rearrange("p (c n) -> p c n", n=128),
                        in_=wxv[:, :, bass.ts(cc, 128)]))
                    sy.dma("wx", nc.sync.dma_start(
                        out=dgb[:].rearrange("p (j n) -> p j n", n=128),
                        in_=cdv[:, :, cc, :]))
                    sy.waitn("po", "ac")
                    sy.done("po", nc.gpsimd.tensor_copy(cbst[:, 0:1],
                                                        convb[:, bass.ts(cc, 1)]))
                    sy.waitn("pe", "wx")
                    sy.waitn("pe", "dv", 1)
                    for tq in range(4):
                        inst = None
                        for c in range(ND):
                            inst = nc.tensor.matmul(
                                px[:, tq * 512:(tq + 1) * 512],
                                wxs[:, c * 128:(c + 1) * 128],
                                ln3[:, c * T + tq * 512: c * T + (tq + 1) * 512],
                                start=(c == 0), stop=(c == ND - 1))
                        sy.done("pe", inst)
                    sy.waitn("dv", "pe")
                    sy.done("dv", nc.vector.tensor_copy(xraw[:, 3:2051], px[:]))

                def p3c_conv(cc, silu_out):
                    for q in range(2):
                        sy.waitn("pe", "dv")
                        sy.waitn("pe", "ac")
                        sy.waitn("pe", "po")
                        inst = None
                        for j in range(4):
                            for th in range(2):
                                inst = nc.tensor.matmul(
                                    pc[:, th * 512:(th + 1) * 512],
                                    dgb[:, j * 128:(j + 1) * 128],
                                    xraw[:, (2 * q + th) * 512 + j:
                                         (2 * q + th) * 512 + j + 512],
                                    start=(j == 0), stop=(j == 3),
                                    skip_group_check=True)
                        sy.done("pe", inst)
                        sy.waitn("ac", "pe")
                        sy.done("ac", nc.scalar.activation(
                            out=silu_out(q), in_=pc[:], func=AF.Silu,
                            bias=cbst[:, 0:1]))

                def p3c_trans(src):
                    for g in range(2):
                        sy.waitn("pe", "ac")
                        sy.waitn("pe", "dv", 1)
                        inst = None
                        for i8 in range(8):
                            i = g * 8 + i8
                            inst = nc.tensor.transpose(
                                ptr[:, i * 128:(i + 1) * 128],
                                src[:, i * 128:(i + 1) * 128], ident[:])
                        sy.done("pe", inst)
                    sy.waitn("dv", "pe")

                def p3c_body(cc):
                    p3c_core(cc)
                    p3c_conv(cc, lambda q: postc[:, q * 1024:(q + 1) * 1024])
                    p3c_trans(postc)
                    sy.waitn("dv", "xs")
                    sy.done("dv", nc.vector.tensor_copy(xtsb[:], ptr[:]))
                    sy.waitn("ac", "dv")
                    xtp = d["xtm_scr"].rearrange("(t p) n -> p t n", p=128)
                    sy.dma("xs", nc.scalar.dma_start(
                        out=xtp[:, :, bass.ts(cc, 128)],
                        in_=xtsb[:].rearrange("p (t n) -> p t n", n=128)))

                sy.begin_loop([("sp", "pe"), ("po", "ac"), ("pe", "wx"),
                               ("pe", "dv"), ("pe", "ac"), ("pe", "po"), ("dv", "pe"),
                               ("dv", "xs"), ("ac", "pe"), ("ac", "dv")],
                              {"wx": 32, "po": 1, "pe": 8, "dv": 2,
                               "ac": 2, "xs": 16})
                with ctx.fori(32, [PE, DVE, ACT, SP, Pool]) as u:
                    sy.iter_start(u)
                    p3c_body(u)
                sy.end_loop(32)
                p3c_core(32)
                p3c_conv(32, lambda q: Bfm[:, q * 1024:(q + 1) * 1024])
                p3c_trans(Bfm)
                sy.done("dv", nc.vector.tensor_copy(B_tm[:], ptr[:]))
                p3c_core(33)
                p3c_conv(33, lambda q: Cfm32[:, q * 1024:(q + 1) * 1024])
                sy.waitn("dv", "ac")
                sy.done("dv", nc.vector.tensor_copy(Cbf[:], Cfm32[:]))
                nc.sync.wait_ge(sy.p["xs"], sy.n["xs"])
                sy.free()
            ctx.barrier()


            with nc.psum_tensor("p3d_pd", [128, 2048], F32) as pd, \
                 nc.sbuf_tensor("p3d_wdt", [128, 16 * 64], BF16) as wdtres, \
                 nc.sbuf_tensor("p3d_dtA", [64, T], F32) as dtA:
                sy = Sync(ctx, "p3d")
                nc.gpsimd.dma_start(
                    out=wdtres[:].rearrange("p (c n) -> p c n", n=64),
                    in_=wdtv).then_inc(cn, 16)
                ncn += 16
                nc.tensor.wait_ge(cn, ncn)
                sy.done("ac", nc.scalar.activation(out=Acol[:, 1:2], in_=Acol[:, 0:1],
                                                   func=AF.Exp))
                sy.waitn("dv", "ac")
                sy.done("dv", nc.vector.tensor_scalar_mul(Acol[:, 1:2], Acol[:, 1:2],
                                                          -1.0))
                for tq in range(4):
                    inst = None
                    for c in range(ND):
                        inst = nc.tensor.matmul(
                            pd[0:64, tq * 512:(tq + 1) * 512],
                            wdtres[:, c * 64:(c + 1) * 64],
                            ln3[:, c * T + tq * 512: c * T + (tq + 1) * 512],
                            start=(c == 0), stop=(c == ND - 1))
                    sy.done("pe", inst)
                    dsl = dtf[:, tq * 512:(tq + 1) * 512]
                    sy.waitn("dv", "pe")
                    sy.done("dv", nc.vector.tensor_scalar(
                        out=dsl, in0=pd[0:64, tq * 512:(tq + 1) * 512],
                        scalar1=dtb[:, 0:1], scalar2=30.0,
                        op0=OP.add, op1=OP.min))
                    sy.waitn("ac", "dv")
                    sy.done("ac", nc.scalar.activation(out=dsl, in_=dsl,
                                                       func=AF.Exp))
                    sy.waitn("dv", "ac")
                    sy.done("dv", nc.vector.tensor_scalar_add(dsl, dsl, 1.0))
                    sy.waitn("ac", "dv")
                    sy.done("ac", nc.scalar.activation(out=dsl, in_=dsl,
                                                       func=AF.Ln))
                sy.waitn("dv", "ac")
                sy.waitn("dv", "ac")
                sy.done("dv", nc.vector.tensor_scalar_mul(dtA[:], dtf[:],
                                                          Acol[:, 1:2]))
                for ch in range(16):
                    sy.waitn("dv", "dv")
                    sy.done("dv", nc.vector.tensor_tensor_scan(
                        out=cumf[:, ch * 128:(ch + 1) * 128],
                        data0=dtA[:, ch * 128:(ch + 1) * 128],
                        data1=zer64[:], initial=0.0, op0=OP.add, op1=OP.add))
                sy.waitn("sp", "dv")
                sy.dma("cs", nc.sync.dma_start(out=cumd, in_=cumf[:]))
                nc.sync.wait_ge(sy.p["cs"], 16)
                sy.free()
            ctx.barrier()

        es_e = ExitStack()
        sbe = lambda n, shp, dt: es_e.enter_context(nc.sbuf_tensor(n, shp, dt))
        with es_e:
            pbig = es_e.enter_context(nc.psum_tensor("p3e_pa", [128, 2048], F32))
            psB = es_e.enter_context(nc.psum_tensor("p3e_pb", [128, 2048], F32))
            S = sbe("p3e_S", [128, 4096], F32)
            EW = sbe("p3e_EW", [128, 2048], BF16)
            dtx = sbe("p3e_dtx", [128, 4096], BF16)
            dtxd = sbe("p3e_dtxd", [128, 4096], BF16)
            Xch = sbe("p3e_X", [128, 4096], BF16)
            zch = sbe("p3e_z", [128, 4096], BF16)
            ysb = sbe("p3e_y", [128, 4096], F32)
            cum_tm = sbe("p3e_ctm", [128, 64], F32)
            dt_tm = sbe("p3e_dttm", [128, 64], F32)
            expcum = sbe("p3e_ecum", [128, 64], F32)
            decf = sbe("p3e_decf", [128, 64], F32)
            dAl = sbe("p3e_dAl", [128, 64], F32)
            Gsb = sbe("p3e_G", [128, 128], BF16)
            cr0 = sbe("p3e_cr0", [1, 4096], F32)
            cr1 = sbe("p3e_cr1", [1, 4096], F32)
            culast = sbe("p3e_cl", [1, 64], F32)
            elast = sbe("p3e_el", [1, 64], F32)
            cumstg = sbe("p3e_cs64", [64, 128], F32)
            dtstg = sbe("p3e_ds64", [64, 128], F32)
            Bstg = sbe("p3e_Bst", [128, 128], BF16)
            Cstg = sbe("p3e_Cst", [128, 128], F32)
            Btmstg = sbe("p3e_Btst", [128, 128], BF16)
            Dbc = sbe("p3e_Dbc", [128, 4096], BF16)
            nw1 = sbe("p3e_nw", [128, 4096], BF16)
            crh = [cr0, cr1]
            nc.gpsimd.dma_start(out=S[:], in_=d["zeros4096"]).then_inc(cn, 16)
            nc.gpsimd.dma_start(out=Dbc[:], in_=d["D_bc"]).then_inc(cn, 16)
            nc.gpsimd.dma_start(out=nw1[:], in_=d["l1_normw_bc"]).then_inc(cn, 16)
            ncn += 48
            for e in (nc.vector, nc.scalar, nc.tensor, nc.sync):
                e.wait_ge(cn, ncn)
            sy = Sync(ctx, "p3e")

            def p3e_body(ch):
                sy.waitn("sp", "dv", 10)
                sy.dma("xz", nc.sync.dma_start(out=Xch[:], in_=xtv[ch]))
                sy.waitn("sp", "yn")
                sy.dma("xz", nc.sync.dma_start(out=zch[:], in_=zsv[ch]))
                for hf in range(2):
                    sy.dma("cr", nc.sync.dma_start(
                        out=crh[hf][0:1, :].rearrange("o (h t) -> o h t", t=128),
                        in_=cumd[hf * 32:(hf + 1) * 32,
                                 bass.ts(ch, 128)].rearrange("h t -> () h t")))
                sy.waitn("ac", "pe")
                sy.done("ac", nc.scalar.activation(out=cumstg[:], func=AF.Copy,
                                                   in_=cumf[:, bass.ts(ch, 128)]))
                sy.done("ac", nc.scalar.activation(out=dtstg[:], func=AF.Copy,
                                                   in_=dtf[:, bass.ts(ch, 128)]))
                sy.done("ac", nc.scalar.activation(out=Bstg[:], func=AF.Copy,
                                                   in_=Bfm[:, bass.ts(ch, 128)]))
                sy.done("ac", nc.scalar.activation(out=Cstg[:], func=AF.Copy,
                                                   in_=Cfm32[:, bass.ts(ch, 128)]))
                sy.done("ac", nc.scalar.activation(out=Btmstg[:], func=AF.Copy,
                                                   in_=B_tm[:, bass.ts(ch, 128)]))
                sy.waitn("pe", "ac")
                sy.waitn("pe", "dv")
                sy.done("pe", nc.tensor.matmul(pbig[:, 0:128], Bstg[:],
                                               Cbf[:, bass.ts(ch, 128)],
                                               start=True, stop=True))
                sy.done("pe", nc.tensor.transpose(pbig[:, 512:576], cumstg[:],
                                                  ident32[0:64, 0:64]))
                sy.done("pe", nc.tensor.transpose(pbig[:, 1024:1088], dtstg[:],
                                                  ident32[0:64, 0:64]))
                sy.waitn("dv", "pe")
                sy.done("dv", nc.vector.tensor_tensor(out=Gsb[:], in0=pbig[:, 0:128],
                                                      in1=tri[:], op=OP.mult))
                sy.done("dv", nc.vector.tensor_copy(cum_tm[:], pbig[:, 512:576]))
                sy.done("dv", nc.vector.tensor_copy(dt_tm[:], pbig[:, 1024:1088]))
                sy.waitn("pe", "dv")
                sy.done("pe", nc.tensor.matmul(psB[0:1, 64:128], ident32[:, 127:128],
                                               cum_tm[:], start=True, stop=True,
                                               skip_group_check=True))
                sy.waitn("dv", "pe")
                sy.done("dv", nc.vector.tensor_copy(culast[:], psB[0:1, 64:128]))
                sy.waitn("ac", "dv")
                sy.done("ac", nc.scalar.activation(out=expcum[:], in_=cum_tm[:],
                                                   func=AF.Exp))
                sy.done("ac", nc.scalar.activation(out=elast[:], in_=culast[:],
                                                   func=AF.Exp))
                sy.waitn("pe", "ac")
                sy.done("pe", nc.tensor.matmul(pbig[:, 1536:1600], onesr32[:],
                                               culast[:], start=True, stop=True,
                                               skip_group_check=True))
                sy.done("pe", nc.tensor.matmul(psB[:, 0:64], onesr32[:],
                                               elast[:], start=True, stop=True,
                                               skip_group_check=True))
                sy.waitn("dv", "pe")
                sy.done("dv", nc.vector.tensor_tensor(out=decf[:],
                                                      in0=pbig[:, 1536:1600],
                                                      in1=cum_tm[:], op=OP.subtract))
                sy.done("dv", nc.vector.tensor_copy(dAl[:], psB[:, 0:64]))
                sy.waitn("ac", "dv")
                sy.done("ac", nc.scalar.activation(out=decf[:], in_=decf[:],
                                                   func=AF.Exp))
                sy.waitn("dv", "xz")
                sy.waitn("dv", "ac")
                d3 = dtx[:].rearrange("p (h q) -> p h q", q=64)
                x3_ = Xch[:].rearrange("p (h q) -> p h q", q=64)
                sy.done("dv", nc.vector.tensor_tensor(
                    out=d3, in0=x3_,
                    in1=dt_tm[:].unsqueeze(2).to_broadcast([128, 64, 64]),
                    op=OP.mult))
                dd3 = dtxd[:].rearrange("p (h q) -> p h q", q=64)
                sy.waitn("dv", "dv")
                sy.done("dv", nc.vector.tensor_tensor(
                    out=dd3, in0=d3,
                    in1=decf[:].unsqueeze(2).to_broadcast([128, 64, 64]),
                    op=OP.mult))
                for hf in range(2):
                    for qq in range(2):
                        q = 2 * hf + qq
                        sy.waitn("pe", "cr")
                        sy.waitn("pe", "ac")
                        sy.waitn("pe", "dv")
                        inst = None
                        for i in range(4):
                            inst = nc.tensor.matmul(
                                pbig[:, i * 512:(i + 1) * 512], onesr32[:],
                                crh[hf][0:1, qq * 2048 + i * 512:
                                        qq * 2048 + (i + 1) * 512],
                                start=True, stop=True)
                        sy.done("pe", inst)
                        sy.waitn("dv", "pe")
                        p3 = pbig[:].rearrange("p (h t) -> p h t", t=128)
                        sy.done("dv", nc.vector.tensor_tensor(
                            out=p3, in0=p3,
                            in1=cum_tm[:, q * 16:(q + 1) * 16].unsqueeze(2)
                                .to_broadcast([128, 16, 128]),
                            op=OP.subtract))
                        sy.waitn("dv", "dv")
                        sy.done("dv", nc.vector.tensor_scalar_min(pbig[:], pbig[:],
                                                                  0.0))
                        sy.waitn("ac", "dv")
                        sy.waitn("ac", "pe")
                        sy.done("ac", nc.scalar.activation(out=EW[:], in_=pbig[:],
                                                           func=AF.Exp))
                        sy.waitn("dv", "ac")
                        e3 = EW[:].rearrange("p (h t) -> p h t", t=128)
                        sy.done("dv", nc.vector.tensor_tensor(
                            out=e3, in0=e3,
                            in1=Gsb[:].unsqueeze(1).to_broadcast([128, 16, 128]),
                            op=OP.mult))
                        sy.waitn("pe", "dv")
                        inst = None
                        for hl in range(16):
                            h = q * 16 + hl
                            inst = nc.tensor.matmul(
                                psB[:, qq * 1024 + hl * 64:
                                    qq * 1024 + (hl + 1) * 64],
                                EW[:, hl * 128:(hl + 1) * 128],
                                dtx[:, h * 64:(h + 1) * 64],
                                start=True, stop=True, skip_group_check=True)
                        sy.done("pe", inst)
                    sy.waitn("pe", "ac")
                    inst = None
                    for i in range(4):
                        inst = nc.tensor.matmul(
                            pbig[:, i * 512:(i + 1) * 512], Cstg[:],
                            S[:, hf * 2048 + i * 512: hf * 2048 + (i + 1) * 512],
                            start=True, stop=True)
                    sy.done("pe", inst)
                    sy.waitn("dv", "pe")
                    ysl = ysb[:, hf * 2048:(hf + 1) * 2048].rearrange(
                        "p (h q) -> p h q", q=64)
                    pb3 = pbig[:].rearrange("p (h q) -> p h q", q=64)
                    sy.done("dv", nc.vector.tensor_tensor(
                        out=ysl, in0=pb3,
                        in1=expcum[:, hf * 32:(hf + 1) * 32].unsqueeze(2)
                            .to_broadcast([128, 32, 64]),
                        op=OP.mult))
                    sy.waitn("dv", "dv")
                    sy.done("dv", nc.vector.tensor_tensor(
                        out=ysb[:, hf * 2048:(hf + 1) * 2048],
                        in0=ysb[:, hf * 2048:(hf + 1) * 2048],
                        in1=psB[:], op=OP.add))
                for hf in range(2):
                    sy.waitn("dv", "pe")
                    sy.waitn("dv", "dv")
                    sy.done("dv", nc.vector.tensor_tensor(
                        out=EW[:], in0=Xch[:, hf * 2048:(hf + 1) * 2048],
                        in1=Dbc[:, hf * 2048:(hf + 1) * 2048], op=OP.mult))
                    sy.waitn("dv", "dv")
                    sy.done("dv", nc.vector.tensor_tensor(
                        out=ysb[:, hf * 2048:(hf + 1) * 2048],
                        in0=ysb[:, hf * 2048:(hf + 1) * 2048],
                        in1=EW[:], op=OP.add))
                sy.waitn("ac", "dv")
                sy.waitn("ac", "xz")
                sy.done("ac", nc.scalar.activation(out=zch[:], in_=zch[:],
                                                   func=AF.Silu))
                sy.waitn("dv", "ac")
                sy.done("dv", nc.vector.tensor_tensor(out=ysb[:], in0=ysb[:],
                                                      in1=zch[:], op=OP.mult))
                sy.waitn("ac", "dv")
                sy.done("ac", nc.scalar.activation(out=dtx[:], in_=ysb[:],
                                                   func=AF.Square,
                                                   accum_out=stat[:, 0:1]))
                sy.waitn("dv", "ac")
                sy.done("dv", nc.vector.tensor_scalar_mul(stat[:, 1:2], stat[:, 0:1],
                                                          float(1.0 / INTER)))
                sy.waitn("ac", "dv")
                sy.done("ac", nc.scalar.activation(out=stat[:, 2:3], in_=stat[:, 1:2],
                                                   func=AF.Sqrt, bias=f32c(1e-5)))
                sy.waitn("dv", "ac")
                sy.done("dv", nc.vector.reciprocal(stat[:, 3:4], stat[:, 2:3]))
                sy.waitn("dv", "dv")
                sy.done("dv", nc.vector.scalar_tensor_tensor(
                    out=dtx[:], in0=ysb[:], scalar=stat[:, 3:4], in1=nw1[:],
                    op0=OP.mult, op1=OP.mult))
                sy.waitn("pe", "dv")
                pbigB = pbig[:, 0:1024].bitcast(BF16)
                psBB = psB[:, 0:1024].bitcast(BF16)
                for g in range(4):
                    inst = None
                    for i8 in range(8):
                        i = g * 8 + i8
                        pt = pbigB if i < 16 else psBB
                        inst = nc.tensor.transpose(
                            pt[:, (i % 16) * 128:((i % 16) + 1) * 128],
                            dtx[:, i * 128:(i + 1) * 128], ident[:])
                    sy.done("pe", inst)
                sy.waitn("dv", "pe")
                sy.done("dv", nc.vector.tensor_copy(zch[:, 0:2048], pbigB[:]))
                sy.done("dv", nc.vector.tensor_copy(zch[:, 2048:4096], psBB[:]))
                sy.waitn("ac", "dv")
                sy.dma("yn", nc.scalar.dma_start(
                    out=ynv[:, :, bass.ts(ch, 128)],
                    in_=zch[:].rearrange("p (c t) -> p c t", t=128)))
                sy.waitn("pe", "dv")
                inst = None
                for i in range(8):
                    pt = pbig if i < 4 else psB
                    inst = nc.tensor.matmul(
                        pt[:, (i % 4) * 512:((i % 4) + 1) * 512], Btmstg[:],
                        dtxd[:, i * 512:(i + 1) * 512],
                        start=True, stop=True, skip_group_check=True)
                sy.done("pe", inst)
                s3 = S[:].rearrange("p (h q) -> p h q", q=64)
                sy.waitn("dv", "pe")
                sy.done("dv", nc.vector.tensor_tensor(
                    out=s3, in0=s3,
                    in1=dAl[:].unsqueeze(2).to_broadcast([128, 64, 64]),
                    op=OP.mult))
                sy.waitn("dv", "dv")
                sy.done("dv", nc.vector.tensor_tensor(out=S[:, 0:2048],
                                                      in0=S[:, 0:2048],
                                                      in1=pbig[:], op=OP.add))
                sy.done("dv", nc.vector.tensor_tensor(out=S[:, 2048:4096],
                                                      in0=S[:, 2048:4096],
                                                      in1=psB[:], op=OP.add))

            pairs = [("sp", "dv"), ("sp", "yn"),
                     ("pe", "dv"), ("pe", "ac"),
                     ("pe", "cr"), ("dv", "pe"), ("dv", "dv"),
                     ("dv", "ac"), ("dv", "xz"), ("ac", "dv"), ("ac", "pe"),
                     ("ac", "xz")]
            sy.begin_loop(pairs, {"xz": 32, "cr": 32, "yn": 16,
                                  "pe": 21, "dv": 37, "ac": 15})
            with ctx.fori(16, [PE, DVE, ACT, SP]) as ch:
                sy.iter_start(ch)
                p3e_body(ch)
            sy.end_loop(16)
            nc.sync.wait_ge(sy.p["yn"], sy.n["yn"])
            sy.free()
        ctx.barrier()
        es_bc.close()

        with nc.psum_tensor("p3f_p", [128, 2048], F32) as pf, \
             nc.sbuf_tensor("p3f_wo", [128, 32 * 2048], BF16) as wout, \
             nc.sbuf_tensor("p3f_yn", [128, 32 * 128], BF16) as ynst, \
             nc.sbuf_tensor("p3f_x2", [128, 2048], F32) as x2t, \
             nc.sbuf_tensor("p3f_x3", [128, 2048], F32) as x3t:
            nc.gpsimd.dma_start(out=wout[:].rearrange("p (c n) -> p c n", n=2048),
                                in_=wov).then_inc(cn, 16)
            ncn += 16
            for e in (nc.tensor, nc.sync):
                e.wait_ge(cn, ncn)
            sy = Sync(ctx, "p3f")

            def p3f_tile(u):
                sy.waitn("sp", "pe")
                sy.dma("ya", nc.sync.dma_start(
                    out=ynst[:].rearrange("p (c t) -> p c t", t=128),
                    in_=ynv[:, :, bass.ts(u, 128)]))
                sy.waitn("ac", "dv")
                sy.dma("xa", nc.scalar.dma_start(out=x2t[:], in_=x2v[u]))
                sy.waitn("pe", "ya")
                sy.waitn("pe", "dv")
                for dp in range(4):
                    inst = None
                    for c in range(32):
                        inst = nc.tensor.matmul(
                            pf[:, dp * 512:(dp + 1) * 512],
                            ynst[:, c * 128:(c + 1) * 128],
                            wout[:, c * 2048 + dp * 512: c * 2048 + (dp + 1) * 512],
                            start=(c == 0), stop=(c == 31))
                    sy.done("pe", inst)
                sy.waitn("dv", "pe")
                sy.waitn("dv", "xa")
                sy.waitn("dv", "oa")
                sy.done("dv", nc.vector.tensor_tensor(out=x3t[:], in0=pf[:],
                                                      in1=x2t[:], op=OP.add))
                sy.waitn("ac", "dv")
                sy.dma("oa", nc.scalar.dma_start(out=x3v[u], in_=x3t[:]))

            sy.begin_loop([("sp", "pe"), ("pe", "ya"), ("pe", "dv"), ("dv", "pe"),
                           ("dv", "xa"), ("dv", "oa"), ("ac", "dv")],
                          {"ya": 16, "xa": 16, "oa": 16, "pe": 4, "dv": 1})
            with ctx.fori(NT, [PE, DVE, SP, ACT]) as u:
                sy.iter_start(u)
                p3f_tile(u)
            sy.end_loop(NT)
            nc.sync.wait_ge(sy.p["oa"], 16 * NT)
            sy.free()
    ctx.barrier()


def emit_p4(ctx):
    nc, d = ctx.nc, ctx.d
    es = ExitStack()
    sb = lambda n, shp, dt=F32: es.enter_context(nc.sbuf_tensor(n, shp, dt))

    ident = sb("p4_ident", [128, 128], BF16)
    onesr32 = sb("p4_oner", [1, 128], F32)
    onec32 = sb("p4_onec", [128, 1], F32)
    stat = sb("p4_stat", [128, 8], F32)
    srow = sb("p4_srow", [1, 4], F32)
    thrc = sb("p4_thrc", [128, 9], F32)
    invs_all = sb("p4_invs", [128, 16], F32)

    x3v = d["x3"].rearrange("(t p) n -> t p n", p=128)
    outv = d["out"].rearrange("(t p) n -> t p n", p=128)
    wgv = d["wgT"].rearrange("(c p) n -> p c n", p=128)
    wuv = d["wuT"].rearrange("(c p) n -> p c n", p=128)
    wdv = d["wdT"].rearrange("(c p) n -> p c n", p=128)
    gsv = d["g_scr"].rearrange("(t p) n -> t p n", p=128)
    gsp = d["g_scr"].rearrange("(t p) n -> p t n", p=128)
    qgv = d["qg_scr"].rearrange("(c p) t -> p c t", p=128)

    with es:
        cn = ctx.sem("p4cn")
        ncn = 0
        for dst, src in ((ident[:], d["identity"]), (onesr32[:], d["ones_row"]),
                         (onec32[:], d["ones_col"])):
            nc.gpsimd.dma_start(out=dst, in_=src).then_inc(cn, 16)
            ncn += 16
        for e in (nc.vector, nc.scalar, nc.tensor, nc.sync):
            e.wait_ge(cn, ncn)

        with nc.psum_tensor("p4a_ps", [128, 512], F32) as psa, \
             nc.sbuf_tensor("p4a_w0", [128, 8192], F32) as ws0, \
             nc.sbuf_tensor("p4a_w1", [128, 8192], F32) as ws1, \
             nc.sbuf_tensor("p4a_ab", [128, 8192], BF16) as absb:
            ws = [ws0, ws1]
            sy = Sync(ctx, "p4a")
            chunks = ([("g", wgv[:, c, :]) for c in range(16)] +
                      [("u", wuv[:, c, :]) for c in range(16)] +
                      [("d", wdv[:, 4 * q:4 * q + 4, :]) for q in range(13)])
            nmat = {"g": 0, "u": 1, "d": 2}
            first = {"g": True, "u": True, "d": True}
            for i, (mk, src) in enumerate(chunks):
                par = i % 2
                wq = ("wa", "wb")[par]
                width = 6656 if mk in ("g", "u") else 8192
                sy.waitn("sp", "ac", 1)
                if mk == "d":
                    sy.dma(wq, nc.sync.dma_start(
                        out=ws[par][:].rearrange("p (c n) -> p c n", n=2048),
                        in_=src))
                else:
                    sy.dma(wq, nc.sync.dma_start(out=ws[par][:, 0:width], in_=src))
                sy.waitn("ac", wq)
                sy.waitn("ac", "dv")
                sy.done("ac", nc.scalar.activation(out=absb[:, 0:width],
                                                   in_=ws[par][:, 0:width],
                                                   func=AF.Abs,
                                                   accum_out=stat[:, 0:1]))
                sy.waitn("dv", "ac")
                mi = nmat[mk]
                if first[mk]:
                    first[mk] = False
                    sy.done("dv", nc.vector.tensor_copy(stat[:, 1 + mi:2 + mi],
                                                        stat[:, 0:1]))
                else:
                    sy.done("dv", nc.vector.tensor_tensor(
                        out=stat[:, 1 + mi:2 + mi], in0=stat[:, 1 + mi:2 + mi],
                        in1=stat[:, 0:1], op=OP.add))
            for mi in range(3):
                sy.waitn("pe", "dv")
                sy.done("pe", nc.tensor.matmul(psa[0:1, 0:1], stat[:, 1 + mi:2 + mi],
                                               onec32[:], start=True, stop=True,
                                               skip_group_check=True))
                sy.waitn("dv", "pe")
                sy.done("dv", nc.vector.tensor_scalar(
                    out=srow[0:1, mi:mi + 1], in0=psa[0:1, 0:1],
                    scalar1=float(1.0 / (I2 * D)), scalar2=1e-5,
                    op0=OP.mult, op1=OP.max))
                sy.waitn("pe", "dv")
                sy.done("pe", nc.tensor.matmul(psa[:, 64:65], onesr32[:],
                                               srow[0:1, mi:mi + 1],
                                               start=True, stop=True,
                                               skip_group_check=True))
                sy.waitn("dv", "pe")
                sy.done("dv", nc.vector.tensor_scalar_mul(thrc[:, 2 * mi:2 * mi + 1],
                                                          psa[:, 64:65], 0.5))
                sy.waitn("dv", "dv")
                sy.done("dv", nc.vector.tensor_scalar_mul(
                    thrc[:, 2 * mi + 1:2 * mi + 2], psa[:, 64:65], -0.5))
                sy.waitn("dv", "dv")
                sy.done("dv", nc.vector.tensor_copy(thrc[:, 6 + mi:7 + mi],
                                                    psa[:, 64:65]))
            sy.free()
        ctx.barrier()

        es_q = ExitStack()
        qact = es_q.enter_context(nc.sbuf_tensor("p4_qact", [128, 16 * 2048], BF16))
        qview = qact[:].rearrange("p (c t) -> p c t", t=T)
        with nc.psum_tensor("p4b_ptr", [128, 2048], BF16) as ptr, \
             nc.sbuf_tensor("p4b_xw", [128, D], F32) as xw, \
             nc.sbuf_tensor("p4b_sq", [128, D], F32) as sqs, \
             nc.sbuf_tensor("p4b_lnf", [128, D], F32) as lnf, \
             nc.sbuf_tensor("p4b_u", [128, D], F32) as ubuf, \
             nc.sbuf_tensor("p4b_ci", [128, D], I32) as cibuf, \
             nc.sbuf_tensor("p4b_cf", [128, D], F32) as cfbuf, \
             nc.sbuf_tensor("p4b_iv", [128, 1], F32) as ivcol, \
             nc.sbuf_tensor("p4b_q", [128, D], BF16) as lnq:
            sy = Sync(ctx, "p4b")

            def p4b_tile(u):
                sy.waitn("ac", "dv")
                sy.dma("xa", nc.scalar.dma_start(out=xw[:], in_=x3v[u]))
                sy.waitn("dv", "xa")
                sy.waitn("ac", "xa")
                sy.waitn("dv", "dv")
                sy.done("dv", nc.vector.tensor_reduce(out=stat[:, 0:1], in_=xw[:],
                                                      axis=AX.X, op=OP.add))
                sy.waitn("ac", "ac")
                sy.done("ac", nc.scalar.activation(out=sqs[:], in_=xw[:],
                                                   func=AF.Square,
                                                   accum_out=stat[:, 1:2]))
                sy.waitn("dv", "dv")
                sy.done("dv", nc.vector.tensor_scalar_mul(stat[:, 2:3], stat[:, 0:1],
                                                          float(1.0 / D)))
                sy.waitn("dv", "dv")
                sy.done("dv", nc.vector.tensor_tensor(out=stat[:, 3:4],
                                                      in0=stat[:, 2:3],
                                                      in1=stat[:, 2:3], op=OP.mult))
                sy.waitn("dv", "dv")
                sy.waitn("dv", "ac")
                sy.done("dv", nc.vector.scalar_tensor_tensor(
                    out=stat[:, 4:5], in0=stat[:, 1:2], scalar=float(1.0 / D),
                    in1=stat[:, 3:4], op0=OP.mult, op1=OP.subtract))
                sy.waitn("ac", "dv")
                sy.done("ac", nc.scalar.activation(out=stat[:, 5:6], in_=stat[:, 4:5],
                                                   func=AF.Sqrt, bias=f32c(1e-5)))
                sy.waitn("dv", "ac")
                sy.done("dv", nc.vector.reciprocal(stat[:, 6:7], stat[:, 5:6]))
                sy.waitn("dv", "dv")
                sy.done("dv", nc.vector.tensor_tensor(out=stat[:, 7:8],
                                                      in0=stat[:, 2:3],
                                                      in1=stat[:, 6:7], op=OP.mult))
                sy.waitn("dv", "dv")
                sy.done("dv", nc.vector.tensor_scalar(out=lnf[:], in0=xw[:],
                                                      scalar1=stat[:, 6:7],
                                                      scalar2=stat[:, 7:8],
                                                      op0=OP.mult, op1=OP.subtract))
                sy.waitn("dv", "dv")
                sy.done("dv", nc.vector.scalar_tensor_tensor(
                    out=ubuf[:], in0=lnf[:], scalar=-1.0, in1=lnf[:],
                    op0=OP.mult, op1=OP.max))
                sy.waitn("dv", "dv")
                sy.done("dv", nc.vector.tensor_reduce(out=stat[:, 0:1], in_=ubuf[:],
                                                      axis=AX.X, op=OP.max))
                sy.waitn("dv", "dv")
                sy.done("dv", nc.vector.tensor_scalar_max(stat[:, 1:2], stat[:, 0:1],
                                                          1e-5))
                sy.waitn("dv", "dv")
                sy.done("dv", nc.vector.reciprocal(stat[:, 2:3], stat[:, 1:2]))
                sy.waitn("dv", "dv")
                sy.done("dv", nc.vector.tensor_scalar_mul(stat[:, 3:4], stat[:, 2:3],
                                                          127.0))
                sy.waitn("dv", "dv")
                sy.done("dv", nc.vector.tensor_scalar_mul(
                    invs_all[:, bass.ts(u, 1)], stat[:, 1:2], float(1.0 / 127.0)))
                sy.waitn("dv", "dv")
                sy.done("dv", nc.vector.tensor_scalar(out=ubuf[:], in0=lnf[:],
                                                      scalar1=stat[:, 3:4],
                                                      scalar2=0.5,
                                                      op0=OP.mult, op1=OP.add))
                sy.waitn("dv", "dv")
                sy.done("dv", nc.vector.tensor_copy(cibuf[:], ubuf[:]))
                sy.waitn("dv", "dv")
                sy.done("dv", nc.vector.tensor_copy(cfbuf[:], cibuf[:]))
                sy.waitn("dv", "dv")
                sy.done("dv", nc.vector.tensor_tensor(out=ubuf[:], in0=cfbuf[:],
                                                      in1=ubuf[:], op=OP.is_gt))
                sy.waitn("dv", "dv")
                sy.waitn("dv", "pe")
                sy.done("dv", nc.vector.tensor_tensor(out=lnq[:], in0=cfbuf[:],
                                                      in1=ubuf[:], op=OP.subtract))
                sy.waitn("pe", "dv")
                sy.waitn("pe", "dv", 17)
                inst = None
                for c in range(16):
                    inst = nc.tensor.transpose(ptr[:, c * 128:(c + 1) * 128],
                                               lnq[:, c * 128:(c + 1) * 128],
                                               ident[:])
                sy.done("pe", inst)
                sy.waitn("dv", "pe")
                sy.done("dv", nc.vector.tensor_copy(
                    qview[:, :, bass.ts(u, 128)],
                    ptr[:].rearrange("p (c n) -> p c n", n=128)))
                isv = d["invs_scr"].rearrange("(t p) o -> t p o", p=128)
                sy.waitn("dv", "is")
                sy.done("dv", nc.vector.tensor_scalar_mul(ivcol[:], stat[:, 1:2],
                                                          float(1.0 / 127.0)))
                sy.waitn("ac", "dv")
                sy.dma("is", nc.scalar.dma_start(out=isv[u], in_=ivcol[:]))

            sy.begin_loop([("ac", "dv"), ("dv", "xa"), ("ac", "xa"), ("dv", "dv"),
                           ("dv", "ac"), ("ac", "ac"), ("dv", "pe"), ("pe", "dv"),
                           ("dv", "is")],
                          {"xa": 16, "is": 16, "dv": 20, "ac": 2, "pe": 1})
            with ctx.fori(NT, [PE, DVE, ACT]) as u:
                sy.iter_start(u)
                p4b_tile(u)
            sy.end_loop(NT)
            nc.sync.wait_ge(sy.p["is"], 16 * NT)
            sy.free()
        ctx.barrier()

        with nc.psum_tensor("p4g_pg", [128, 2048], F32) as pg, \
             nc.psum_tensor("p4g_pu", [128, 2048], F32) as pu:
            sy = Sync(ctx, "p4g")
            with nc.sbuf_tensor("p4g_ivr", [1, 2048], F32) as ivrow, \
                 nc.sbuf_tensor("p4g_bcg", [128, 2048], F32) as invsg_bc, \
                 nc.sbuf_tensor("p4g_bcu", [128, 2048], F32) as invsu_bc, \
                 nc.sbuf_tensor("p4g_wf", [128, 16 * 512], F32) as wf, \
                 nc.sbuf_tensor("p4g_qwg", [128, 16 * 512], BF16) as qwg, \
                 nc.sbuf_tensor("p4g_qwu", [128, 16 * 512], BF16) as qwu, \
                 nc.sbuf_tensor("p4g_qtmp", [128, 16 * 512], BF16) as qtmp, \
                 nc.sbuf_tensor("p4g_gfm", [128, 2048], BF16) as gfm, \
                 nc.sbuf_tensor("p4g_gtm", [128, 2048], BF16) as gtm, \
                 nc.sbuf_tensor("p4g_sgb", [128, 4 * 2048], BF16) as sgbuf, \
                 nc.sbuf_tensor("p4g_sg", [128, 512], F32) as sg:
                sy.dma("iv", nc.sync.dma_start(
                    out=ivrow[:], in_=d["invs_scr"].rearrange("t o -> o t")))
                nc.tensor.wait_ge(sy.p["iv"], 16)
                inst = None
                for i in range(4):
                    inst = nc.tensor.matmul(pg[:, i * 512:(i + 1) * 512], onesr32[:],
                                            ivrow[0:1, i * 512:(i + 1) * 512],
                                            start=True, stop=True,
                                            skip_group_check=True)
                r_bc = sy.done("pe", inst)
                sy.waitn("dv", "pe")
                sy.done("dv", nc.vector.tensor_scalar_mul(invsg_bc[:], pg[:],
                                                          thrc[:, 6:7]))
                sy.waitn("dv", "dv")
                sy.done("dv", nc.vector.tensor_scalar_mul(invsu_bc[:], pg[:],
                                                          thrc[:, 7:8]))

                def g1_body(op):
                    sy.waitn("sp", "dv")
                    sy.dma("wf", nc.sync.dma_start(
                        out=wf[:].rearrange("p (c n) -> p c n", n=512),
                        in_=wgv[:, :, bass.ts(op, 512)]))
                    sy.waitn("dv", "wf")
                    sy.waitn("dv", "pe")
                    sy.done("dv", nc.vector.tensor_scalar(
                        out=qtmp[:], in0=wf[:], scalar1=thrc[:, 1:2], scalar2=0.0,
                        op0=OP.is_le, op1=OP.bypass))
                    sy.waitn("dv", "dv")
                    sy.done("dv", nc.vector.scalar_tensor_tensor(
                        out=qwg[:], in0=wf[:], scalar=thrc[:, 0:1], in1=qtmp[:],
                        op0=OP.is_ge, op1=OP.subtract))
                    sy.waitn("sp", "dv")
                    sy.dma("wf", nc.sync.dma_start(
                        out=wf[:].rearrange("p (c n) -> p c n", n=512),
                        in_=wuv[:, :, bass.ts(op, 512)]))
                    for osub in range(4):
                        sy.waitn("pe", "dv")
                        for tq in range(4):
                            inst = None
                            for c in range(16):
                                inst = nc.tensor.matmul(
                                    pg[:, tq * 512:(tq + 1) * 512],
                                    qwg[:, c * 512 + osub * 128:
                                        c * 512 + osub * 128 + 128],
                                    qact[:, c * 2048 + tq * 512:
                                         c * 2048 + (tq + 1) * 512],
                                    start=(c == 0), stop=(c == 15))
                            sy.done("pe", inst)
                        for tq in range(4):
                            sy.waitn("dv", "pe")
                            sy.waitn("dv", "ac")
                            sy.done("dv", nc.vector.tensor_tensor(
                                out=sg[:], in0=pg[:, tq * 512:(tq + 1) * 512],
                                in1=invsg_bc[:, tq * 512:(tq + 1) * 512],
                                op=OP.mult))
                            sy.waitn("ac", "dv")
                            sy.done("ac", nc.scalar.activation(
                                out=sgbuf[:, osub * 2048 + tq * 512:
                                          osub * 2048 + (tq + 1) * 512],
                                in_=sg[:], func=AF.Silu))
                    sy.waitn("dv", "wf")
                    sy.waitn("dv", "pe")
                    sy.done("dv", nc.vector.tensor_scalar(
                        out=qtmp[:], in0=wf[:], scalar1=thrc[:, 3:4], scalar2=0.0,
                        op0=OP.is_le, op1=OP.bypass))
                    sy.waitn("dv", "dv")
                    sy.done("dv", nc.vector.scalar_tensor_tensor(
                        out=qwu[:], in0=wf[:], scalar=thrc[:, 2:3], in1=qtmp[:],
                        op0=OP.is_ge, op1=OP.subtract))
                    for osub in range(4):
                        sy.waitn("pe", "dv")
                        for tq in range(4):
                            inst = None
                            for c in range(16):
                                inst = nc.tensor.matmul(
                                    pu[:, tq * 512:(tq + 1) * 512],
                                    qwu[:, c * 512 + osub * 128:
                                        c * 512 + osub * 128 + 128],
                                    qact[:, c * 2048 + tq * 512:
                                         c * 2048 + (tq + 1) * 512],
                                    start=(c == 0), stop=(c == 15))
                            sy.done("pe", inst)
                        for tq in range(4):
                            sy.waitn("dv", "pe")
                            sy.waitn("dv", "dv")
                            sy.done("dv", nc.vector.tensor_tensor(
                                out=gfm[:, tq * 512:(tq + 1) * 512],
                                in0=pu[:, tq * 512:(tq + 1) * 512],
                                in1=invsu_bc[:, tq * 512:(tq + 1) * 512],
                                op=OP.mult))
                            sy.waitn("dv", "ac")
                            sy.waitn("dv", "dv")
                            sy.done("dv", nc.vector.tensor_tensor(
                                out=gfm[:, tq * 512:(tq + 1) * 512],
                                in0=gfm[:, tq * 512:(tq + 1) * 512],
                                in1=sgbuf[:, osub * 2048 + tq * 512:
                                          osub * 2048 + (tq + 1) * 512],
                                op=OP.mult))
                        ptrg = pg[:, 0:1024].bitcast(BF16)
                        sy.waitn("pe", "dv")
                        inst = None
                        for i in range(16):
                            inst = nc.tensor.transpose(
                                ptrg[:, i * 128:(i + 1) * 128],
                                gfm[:, i * 128:(i + 1) * 128], ident[:])
                        sy.done("pe", inst)
                        sy.waitn("dv", "pe")
                        sy.waitn("dv", "gs")
                        sy.done("dv", nc.vector.tensor_copy(gtm[:], ptrg[:]))
                        sy.waitn("sp", "dv", 0)
                        sy.dma("gs", nc.sync.dma_start(
                            out=gsp[:, :, bass.ts(op, 512)][
                                :, :, osub * 128:(osub + 1) * 128],
                            in_=gtm[:].rearrange("p (t n) -> p t n", n=128)))

                sy.begin_loop([("sp", "dv"), ("dv", "wf"), ("dv", "pe"), ("dv", "dv"),
                               ("dv", "ac"), ("dv", "gs"), ("pe", "dv"),
                               ("ac", "dv")],
                              {"wf": 32, "pe": 36, "dv": 4 + 4 * 13, "ac": 16,
                               "gs": 64})
                with ctx.fori(13, [PE, DVE, ACT, SP]) as op:
                    sy.iter_start(op)
                    g1_body(op)
                sy.end_loop(13)
                nc.sync.wait_ge(sy.p["gs"], 64 * 13)
                sy.free()
        es_q.close()
        ctx.barrier()

        with nc.psum_tensor("p4c_ptr", [128, 2048], BF16) as ptr2, \
             nc.sbuf_tensor("p4c_g", [128, I2P], BF16) as gbuf, \
             nc.sbuf_tensor("p4c_u", [128, I2P], F32) as u2, \
             nc.sbuf_tensor("p4c_ci", [128, I2P], I32) as ci2, \
             nc.sbuf_tensor("p4c_cf", [128, I2P], F32) as cf2, \
             nc.sbuf_tensor("p4c_q", [128, I2P], BF16) as qgb, \
             nc.sbuf_tensor("p4c_qt", [128, 13 * 128], BF16) as qtm:
            sy = Sync(ctx, "p4c")

            def p4c_tile(u):
                sy.waitn("po", "dv")
                sy.dma("gl", nc.gpsimd.dma_start(out=gbuf[:], in_=gsv[u]))
                sy.waitn("dv", "gl")
                sy.waitn("dv", "dv")
                sy.done("dv", nc.vector.scalar_tensor_tensor(
                    out=u2[:], in0=gbuf[:], scalar=-1.0, in1=gbuf[:],
                    op0=OP.mult, op1=OP.max))
                sy.waitn("dv", "dv")
                sy.done("dv", nc.vector.tensor_reduce(out=stat[:, 0:1], in_=u2[:],
                                                      axis=AX.X, op=OP.max))
                sy.waitn("dv", "dv")
                sy.done("dv", nc.vector.tensor_scalar_max(stat[:, 1:2], stat[:, 0:1],
                                                          1e-5))
                sy.waitn("dv", "dv")
                sy.done("dv", nc.vector.reciprocal(stat[:, 2:3], stat[:, 1:2]))
                sy.waitn("dv", "dv")
                sy.done("dv", nc.vector.tensor_scalar_mul(stat[:, 3:4], stat[:, 2:3],
                                                          127.0))
                sy.waitn("dv", "dv")
                sy.done("dv", nc.vector.tensor_scalar_mul(
                    invs_all[:, bass.ts(u, 1)], stat[:, 1:2], float(1.0 / 127.0)))
                sy.waitn("dv", "dv")
                sy.done("dv", nc.vector.tensor_scalar(out=u2[:], in0=gbuf[:],
                                                      scalar1=stat[:, 3:4],
                                                      scalar2=0.5,
                                                      op0=OP.mult, op1=OP.add))
                sy.waitn("dv", "dv")
                sy.done("dv", nc.vector.tensor_copy(ci2[:], u2[:]))
                sy.waitn("dv", "dv")
                sy.done("dv", nc.vector.tensor_copy(cf2[:], ci2[:]))
                sy.waitn("dv", "dv")
                sy.done("dv", nc.vector.tensor_tensor(out=u2[:], in0=cf2[:],
                                                      in1=u2[:], op=OP.is_gt))
                sy.waitn("dv", "dv")
                sy.waitn("dv", "pe")
                sy.done("dv", nc.vector.tensor_tensor(out=qgb[:], in0=cf2[:],
                                                      in1=u2[:], op=OP.subtract))
                for g4 in range(4):
                    sy.waitn("pe", "dv")
                    inst = None
                    for j in range(13):
                        i = g4 * 13 + j
                        inst = nc.tensor.transpose(ptr2[:, j * 128:(j + 1) * 128],
                                                   qgb[:, i * 128:(i + 1) * 128],
                                                   ident[:])
                    sy.done("pe", inst)
                    sy.waitn("dv", "pe")
                    sy.waitn("dv", "qs")
                    sy.done("dv", nc.vector.tensor_copy(qtm[:], ptr2[:, 0:13 * 128]))
                    sy.waitn("po", "dv")
                    sy.dma("qs", nc.gpsimd.dma_start(
                        out=qgv[:, g4 * 13:(g4 + 1) * 13, bass.ts(u, 128)],
                        in_=qtm[:].rearrange("p (c t) -> p c t", t=128)))

            sy.begin_loop([("po", "dv"), ("dv", "gl"), ("dv", "dv"), ("dv", "pe"),
                           ("pe", "dv"), ("dv", "qs")],
                          {"gl": 16, "qs": 64, "dv": 11 + 4, "pe": 4})
            with ctx.fori(NT, [PE, DVE, Pool]) as u:
                sy.iter_start(u)
                p4c_tile(u)
            sy.end_loop(NT)
            sy.waitn("dv", "dv")
            sy.done("dv", nc.vector.tensor_scalar_mul(invs_all[:], invs_all[:],
                                                      thrc[:, 8:9]))
            nc.gpsimd.wait_ge(sy.p["qs"], 64 * NT)
            sy.free()
        ctx.barrier()

        with nc.psum_tensor("p4d_ps", [128, 1024], F32) as pd4, \
             nc.sbuf_tensor("p4d_wf0", [128, 13 * 512], F32) as wdf0, \
             nc.sbuf_tensor("p4d_wf1", [128, 13 * 512], F32) as wdf1, \
             nc.sbuf_tensor("p4d_qw", [128, 52 * 1024], BF16) as qwd, \
             nc.sbuf_tensor("p4d_qt", [128, 13 * 512], BF16) as qdt, \
             nc.sbuf_tensor("p4d_qg", [128, 52 * 128], BF16) as qgt, \
             nc.sbuf_tensor("p4d_x3", [128, 1024], F32) as x3s, \
             nc.sbuf_tensor("p4d_ob", [128, 1024], F32) as ob, \
             nc.sbuf_tensor("p4d_iv", [128, 1], F32) as iv2:
            sy = Sync(ctx, "p4d")
            wdfs = [wdf0, wdf1]
            for dh in range(2):
                for q8 in range(8):
                    q4, dsl = q8 // 2, 2 * dh + q8 % 2
                    par = q8 % 2
                    sy.waitn("sp", "dv", 1)
                    if dh > 0 and q8 < 2:
                        sy.waitn("sp", "pe")
                    sy.dma("wf", nc.sync.dma_start(
                        out=wdfs[par][:].rearrange("p (c n) -> p c n", n=512),
                        in_=wdv[:, q4 * 13:(q4 + 1) * 13,
                                dsl * 512:(dsl + 1) * 512]))
                    sy.waitn("dv", "wf")
                    sy.waitn("dv", "dv")
                    if dh > 0 and q8 == 0:
                        sy.waitn("dv", "pe")
                    sy.done("dv", nc.vector.tensor_scalar(
                        out=qdt[:], in0=wdfs[par][:], scalar1=thrc[:, 5:6],
                        scalar2=0.0, op0=OP.is_le, op1=OP.bypass))
                    sy.waitn("dv", "dv")
                    sy.done("dv", nc.vector.scalar_tensor_tensor(
                        out=qwd[:].rearrange("p (c q) -> p c q", q=1024)[
                            :, q4 * 13:(q4 + 1) * 13,
                            (q8 % 2) * 512:(q8 % 2 + 1) * 512],
                        in0=wdfs[par][:], scalar=thrc[:, 4:5], in1=qdt[:],
                        op0=OP.is_ge, op1=OP.subtract))

                def p4d_tile(u):
                    sy.waitn("ac", "pe")
                    sy.dma("qg", nc.scalar.dma_start(
                        out=qgt[:].rearrange("p (c t) -> p c t", t=128),
                        in_=qgv[:, :, bass.ts(u, 128)]))
                    sy.waitn("po", "dv")
                    sy.dma("xr", nc.gpsimd.dma_start(
                        out=x3s[:], in_=x3v[u][:, dh * 1024:(dh + 1) * 1024]))
                    sy.waitn("dv", "dv")
                    sy.done("dv", nc.vector.tensor_copy(iv2[:],
                                                        invs_all[:, bass.ts(u, 1)]))
                    sy.waitn("pe", "qg")
                    sy.waitn("pe", "dv")
                    for hb in range(2):
                        inst = None
                        for c in range(NOC):
                            inst = nc.tensor.matmul(
                                pd4[:, hb * 512:(hb + 1) * 512],
                                qgt[:, c * 128:(c + 1) * 128],
                                qwd[:, c * 1024 + hb * 512:
                                    c * 1024 + (hb + 1) * 512],
                                start=(c == 0), stop=(c == NOC - 1))
                        sy.done("pe", inst)
                    sy.waitn("dv", "pe")
                    sy.waitn("dv", "xr")
                    sy.waitn("dv", "dv")
                    sy.waitn("dv", "os")
                    sy.done("dv", nc.vector.scalar_tensor_tensor(
                        out=ob[:], in0=pd4[:], scalar=iv2[:, 0:1],
                        in1=x3s[:], op0=OP.mult, op1=OP.add))
                    sy.waitn("ac", "dv")
                    sy.dma("os", nc.scalar.dma_start(
                        out=outv[u][:, dh * 1024:(dh + 1) * 1024], in_=ob[:]))

                sy.begin_loop([("ac", "pe"), ("ac", "dv"), ("po", "dv"),
                               ("pe", "qg"), ("pe", "dv"), ("dv", "pe"),
                               ("dv", "xr"), ("dv", "dv"), ("dv", "os")],
                              {"qg": 16, "xr": 16, "os": 16, "pe": 2, "dv": 2})
                with ctx.fori(NT, [PE, DVE, ACT, Pool]) as u:
                    sy.iter_start(u)
                    p4d_tile(u)
                sy.end_loop(NT)
            nc.sync.wait_ge(sy.p["os"], 16 * NT * 2)
            sy.free()
    ctx.barrier()



PHASE_EMITS = {}


def build_program(phases=("p0", "p1", "p2", "p3", "p4"),
                  external_scratch=(), final_wait=True):
    nc = bass.Bass("TRN2", target_bir_lowering=False)
    es = ExitStack()
    ctx = Ctx(nc, es)
    ctx.d["x0"] = nc.dram_tensor("x0", [T, D], F32, kind="ExternalInput").ap()
    for name, shp in HOST_SHAPES.items():
        ctx.d[name] = nc.dram_tensor(name, list(shp), F32, kind="ExternalInput").ap()
    ctx.d["out"] = nc.dram_tensor("out", [T, D], F32, kind="ExternalOutput").ap()
    for name, (shp, dt) in SCRATCH.items():
        kind = "ExternalOutput" if name in external_scratch else "Internal"
        ctx.d[name] = nc.dram_tensor(name, list(shp), dt, kind=kind).ap()
    emits = {"p0": emit_p0, "p1": emit_p1, "p2": emit_p2, "p3": emit_p3,
             "p4": emit_p4}
    with es:
        for p in phases:
            emits[p](ctx)
    return nc, ctx


def kernel(**inputs):
    Hm = host_prepare(inputs)
    x_all = np.asarray(inputs["L0_attn_hidden_states"], dtype=np.float32)
    nc, ctx = build_program()
    in_maps = []
    for b in range(8):
        m = dict(Hm)
        m["x0"] = np.ascontiguousarray(x_all[b])
        in_maps.append(m)
    res = run_bass_kernel_spmd(nc, in_maps, core_ids=list(range(8)))
    return np.stack([res.results[b]["out"] for b in range(8)], axis=0)


# version 04
# speedup vs baseline: 1.0141x; 1.0024x over previous
import numpy as np
import concourse.bass as bass
import concourse.mybir as mybir
from concourse.bass_utils import run_bass_kernel_spmd
from concourse.alu_op_type import AluOpType as OP
from concourse.expressions import RuntimeValue
from contextlib import ExitStack, contextmanager

F32 = mybir.dt.float32
BF16 = mybir.dt.bfloat16
I32 = mybir.dt.int32
AF = mybir.ActivationFunctionType
AX = mybir.AxisListType
PE = mybir.EngineType.PE
DVE = mybir.EngineType.DVE
Pool = mybir.EngineType.Pool
SP = mybir.EngineType.SP
ACT = mybir.EngineType.Activation

T = 2048
D = 2048
NT = T // 128
ND = D // 128
H = 16
NCC = 34
INTER = 4096
I2 = 6464
I2P = 6656
NOC = I2P // 128
ISC = f32 = None


def f32c(x):
    return np.float32(x)


SC_ATT = 1.0 / np.sqrt(128.0)



def rope_perm():
    perm = np.zeros(128, dtype=np.int64)
    for i in range(32):
        perm[i] = 2 * i
        perm[32 + i] = 64 + 2 * i
        perm[64 + i] = 2 * i + 1
        perm[96 + i] = 64 + 2 * i + 1
    return perm


def host_prepare(inputs):
    f = lambda k: np.ascontiguousarray(np.asarray(inputs[k], dtype=np.float32))
    Hm = {}
    perm = rope_perm()

    WqT = np.ascontiguousarray(f("L0_attn_Wq").T)
    Wq_p = WqT.copy()
    for h in range(H):
        Wq_p[:, h * 128:(h + 1) * 128] = WqT[:, h * 128 + perm]
    Hm["WqT"] = Wq_p
    Hm["WkT"] = np.ascontiguousarray(f("L0_attn_Wk").T[:, perm])
    Hm["WvT"] = np.ascontiguousarray(f("L0_attn_Wv").T)
    Hm["WoT"] = np.ascontiguousarray(f("L0_attn_Wo").T)

    half = 64
    theta = 1.0 / 10000.0 ** (np.arange(0, half, 2, dtype=np.float64) / half)
    hh = np.arange(H, dtype=np.float64)
    ang = theta[:, None] * hh[None, :]
    cos = np.cos(ang)
    sin = np.sin(ang)
    cos128 = np.tile(cos, (4, 1)).astype(np.float32)
    sinH = np.concatenate([-np.tile(sin, (2, 1)), np.tile(sin, (2, 1))],
                          axis=0).astype(np.float32)
    Hm["cosF"] = np.ascontiguousarray(cos128)
    Hm["sinF"] = np.ascontiguousarray(sinH)

    r = np.arange(128)[:, None]
    c = np.arange(512)[None, :]
    Hm["diagmask"] = np.ascontiguousarray(
        np.concatenate([(c >= 128 * j + r).astype(np.float32) for j in range(4)], axis=1))

    Hm["identity"] = np.eye(128, dtype=np.float32)
    Hm["ones_col"] = np.ones((128, 1), dtype=np.float32)
    Hm["ones_row"] = np.ones((1, 128), dtype=np.float32)

    Hm["WcT"] = np.ascontiguousarray(f("L0_ffn_Wc").T)
    Hm["genes"] = f("L0_ffn_genes")
    Hm["WdT"] = np.ascontiguousarray(f("L0_ffn_Wd").T)
    Hm["WuT"] = np.ascontiguousarray(f("L0_ffn_Wu").T)
    Hm["l0_normw_bc"] = np.tile(f("L0_ffn_norm_w")[None, :], (128, 1))
    Hm["l0_temp_bc"] = np.tile(f("L0_ffn_temperature").reshape(1, 1), (128, 1))
    Hm["l0_scale_bc"] = np.tile(f("L0_ffn_scale").reshape(1, 1), (128, 1))

    Hm["W_z"] = f("L1_attn_W_z")
    Hm["W_xBC"] = f("L1_attn_W_xBC")
    Hm["W_dt"] = f("L1_attn_W_dt")
    cw = f("L1_attn_conv_w")
    diag = np.zeros((4, NCC, 128, 128), dtype=np.float32)
    for j in range(4):
        for cc in range(NCC):
            np.fill_diagonal(diag[j, cc], cw[cc * 128:(cc + 1) * 128, j])
    Hm["conv_diag"] = diag.reshape(4 * NCC * 128, 128)
    Hm["conv_b_fm"] = np.ascontiguousarray(f("L1_attn_conv_b").reshape(NCC, 128).T)
    Hm["dt_bias"] = f("L1_attn_dt_bias").reshape(64, 1)
    Hm["A_log"] = f("L1_attn_A_log").reshape(64, 1)
    Hm["D_bc"] = np.tile(np.repeat(f("L1_attn_D"), 64)[None, :], (128, 1))
    Hm["l1_normw_bc"] = np.tile(f("L1_attn_norm_w")[None, :], (128, 1))
    Hm["W_out"] = f("L1_attn_W_out")
    Hm["zeros4096"] = np.zeros((128, 4096), dtype=np.float32)
    tt_ = np.arange(128)
    Hm["tri01"] = (tt_[None, :] >= tt_[:, None]).astype(np.float32)

    wg = np.zeros((D, I2P), dtype=np.float32)
    wg[:, 0:I2] = f("L1_ffn_w_gate").T
    wu = np.zeros((D, I2P), dtype=np.float32)
    wu[:, 0:I2] = f("L1_ffn_w_up").T
    wd = np.zeros((I2P, D), dtype=np.float32)
    wd[0:I2, :] = f("L1_ffn_w_down").T
    Hm["wgT"] = wg
    Hm["wuT"] = wu
    Hm["wdT"] = wd
    return Hm


HOST_SHAPES = {
    "WqT": (D, D), "WkT": (D, 128), "WvT": (D, 128), "WoT": (D, D),
    "cosF": (128, H), "sinF": (128, H), "diagmask": (128, 4 * 512),
    "identity": (128, 128), "ones_col": (128, 1), "ones_row": (1, 128),
    "WcT": (D, 4096), "genes": (4096, 256), "WdT": (256, D), "WuT": (256, D),
    "l0_normw_bc": (128, D), "l0_temp_bc": (128, 1), "l0_scale_bc": (128, 1),
    "W_z": (D, 4096), "W_xBC": (D, 4352), "W_dt": (D, 64),
    "conv_diag": (4 * NCC * 128, 128), "conv_b_fm": (128, NCC),
    "dt_bias": (64, 1), "A_log": (64, 1), "D_bc": (128, 4096),
    "l1_normw_bc": (128, 4096), "W_out": (4096, D),
    "zeros4096": (128, 4096), "tri01": (128, 128),
    "wgT": (D, I2P), "wuT": (D, I2P), "wdT": (I2P, D),
}

SCRATCH = {
    "x1": ((T, D), F32), "x2": ((T, D), F32), "x3": ((T, D), F32),
    "wq_bf": ((D, D + 128), BF16),
    "wk_bf": ((D, 128), BF16), "wv_bf": ((D, 128), BF16), "wo_bf": ((D, D), BF16),
    "wc_bf": ((D, 4096), BF16), "genes_bf": ((4096, 256), BF16),
    "wd_bf": ((256, D), BF16), "wu_bf": ((256, D), BF16),
    "wz_bf": ((D, 4096), BF16), "wxbc_bf": ((D, 4352), BF16),
    "wdt_bf": ((D, 64), BF16), "convdiag_bf": ((4 * NCC * 128, 128), BF16),
    "wout_bf": ((4096, D), BF16),
    "z_scr": ((T, 4096), BF16), "xtm_scr": ((T, 4096), BF16),
    "ynorm_scr": ((4096, T), BF16), "cum_scr": ((64, T), F32),
    "g_scr": ((T, I2P), BF16),
    "qg_scr": ((I2P, T), BF16),
    "invs_scr": ((T, 1), F32),
}


@contextmanager
def _my_fori(nc, trips, regs):
    engines = regs.engines
    loop_name = f"myfori_{nc.next_id()}"
    loop_start = f"{loop_name}_loop"
    loop_end = f"{loop_name}_end"
    nc.regs_mov(regs, 0)
    nc.br(loop_start, engines=engines)
    with nc.body(loop_start, valid_engines=engines):
        yield RuntimeValue(val=regs, min_val=0, max_val=trips - 1)
        nc.regs_alu(regs, regs, 1, op=OP.add)
        nc.br_lt(regs, trips, on_true=loop_start, on_false=loop_end,
                 engines=engines)
    nc.switch_bb(loop_end)


class Ctx:
    def __init__(self, nc, es):
        self.nc = nc
        self.es = es
        self.d = {}
        self.sems = {}
        self.cnt = {}
        self.loop_rh = None

    def fori(self, trips, engines):
        if self.loop_rh is None:
            self.loop_rh = self.nc.alloc_registers(
                "loopreg", engines=[PE, DVE, ACT, SP, Pool])
        RH = type(self.loop_rh)
        sub = RH([self.loop_rh[e] for e in engines])
        return _my_fori(self.nc, trips, sub)

    def sem(self, name):
        if name not in self.sems:
            self.sems[name] = self.es.enter_context(self.nc.semaphore(name))
            self.cnt[name] = 0
        return self.sems[name]

    def barrier(self):
        self.nc.all_engine_barrier()



class Trk:

    def __init__(self, nc, eng_type, eng_obj, sem, name):
        self.nc = nc
        self.eng = eng_obj
        self.sem = sem
        self.rh = nc.alloc_registers(name, engines=[eng_type])
        nc.regs_mov(self.rh, 0)
        self.runC = 0
        self.body_first = False
        self.loop_i = None
        self.coeff = 0

    def begin_body(self, loop_i, coeff):
        self.body_first = True
        self.loop_i = loop_i
        self.coeff = coeff

    def wait(self, C):
        if self.loop_i is not None:
            if not self.body_first and C <= self.runC:
                return
            self.body_first = False
            self.runC = C
            self.nc.regs_alu(self.rh, self.loop_i, self.coeff, op=OP.mult)
            self.nc.regs_alu(self.rh, self.rh, C, op=OP.add)
            self.nc.regs_alu(self.rh, self.rh, 0, op=OP.max)
            self.eng.wait_ge(self.sem, self.rh)
            return
        if C <= 0:
            return
        d = C - self.runC
        if d > 0:
            self.runC = C
            self.nc.regs_alu(self.rh, self.rh, d, op=OP.add)
            self.eng.wait_ge(self.sem, self.rh)

    def free(self):
        for h in self.rh.handles:
            self.nc.free_register(h)


class Sync:
    ET = {"pe": PE, "dv": DVE, "ac": ACT, "sp": SP, "po": Pool}

    def __init__(self, ctx, tag):
        self.ctx = ctx
        self.nc = ctx.nc
        self.eng = {"pe": ctx.nc.tensor, "dv": ctx.nc.vector, "ac": ctx.nc.scalar,
                    "sp": ctx.nc.sync, "po": ctx.nc.gpsimd}
        self.tag = tag
        self.p = {}
        self.n = {}
        self.trk = {}
        self.in_loop = False
        self.coeffs = {}

    def _sem(self, key):
        if key not in self.p:
            self.p[key] = self.ctx.sem(f"{self.tag}_{key}")
            self.n[key] = 0
        return self.p[key]

    def done(self, e, inst):
        inst.then_inc(self._sem(e), 1)
        self.n[e] = self.n.get(e, 0) + 1
        return self.n[e]

    def dma(self, q, inst):
        inst.then_inc(self._sem(q), 16)
        self.n[q] = self.n.get(q, 0) + 16
        return self.n[q]

    def _trk(self, ce, key):
        k = (ce, key)
        if k not in self.trk:
            assert not self.in_loop, f"trk {k} first used inside loop - predeclare"
            self.trk[k] = Trk(self.nc, self.ET[ce], self.eng[ce], self._sem(key),
                              f"{self.tag}_{ce}_{key}"[:24])
        return self.trk[k]

    def wait(self, ce, key, C):
        self._trk(ce, key).wait(C)

    def waitn(self, ce, key, lag=0):
        self._trk(ce, key).wait(self.n.get(key, 0) - lag)

    def begin_loop(self, pairs, coeffs):
        for ce, key in pairs:
            self._trk(ce, key)
        self.coeffs = dict(coeffs)
        self.loop_n0 = dict(self.n)
        self.in_loop = True

    def iter_start(self, i):
        for (ce, key), t in self.trk.items():
            if key in self.coeffs:
                t.begin_body(i, self.coeffs[key])

    def end_loop(self, trips):
        for key, co in self.coeffs.items():
            got = self.n.get(key, 0) - self.loop_n0.get(key, 0)
            assert got == co, f"{self.tag}/{key}: emitted {got} per iter != declared {co}"
            self.n[key] = self.loop_n0.get(key, 0) + co * trips
        for (ce, key), t in self.trk.items():
            if key in self.coeffs:
                if not t.body_first:
                    t.runC += self.coeffs[key] * (trips - 1)
                t.body_first = False
                t.loop_i = None
        self.in_loop = False
        self.coeffs = {}

    def free(self):
        for t in self.trk.values():
            t.free()
        self.trk = {}



P0_PAIRS = [("WqT", "wq_bf"), ("WkT", "wk_bf"), ("WvT", "wv_bf"), ("WoT", "wo_bf"),
            ("WcT", "wc_bf"), ("genes", "genes_bf"), ("WdT", "wd_bf"), ("WuT", "wu_bf"),
            ("W_z", "wz_bf"), ("W_xBC", "wxbc_bf"), ("W_dt", "wdt_bf"),
            ("conv_diag", "convdiag_bf"), ("W_out", "wout_bf")]


def emit_p0(ctx):
    nc, d = ctx.nc, ctx.d
    s = ctx.sem("p0")
    for src, dst in P0_PAIRS:
        if dst == "wq_bf":
            nc.gpsimd.dma_start(out=d[dst][:, 0:D], in_=d[src]).then_inc(s, 16)
            nc.gpsimd.dma_start(out=d[dst][:, D:D + 128], in_=d["WkT"]).then_inc(s, 16)
            ctx.cnt["p0"] = ctx.cnt.get("p0", 0) + 16
        else:
            nc.gpsimd.dma_start(out=d[dst], in_=d[src]).then_inc(s, 16)
        ctx.cnt["p0"] = ctx.cnt.get("p0", 0) + 16
    ctx.p0_total = ctx.cnt["p0"]


def emit_p1(ctx):
    nc, d = ctx.nc, ctx.d
    es = ExitStack()
    sb = lambda n, shp, dt=F32: es.enter_context(nc.sbuf_tensor(n, shp, dt))

    ident = sb("p1_ident", [128, 128], BF16)
    cosF = sb("p1_cosF", [128, H], F32)
    sinF = sb("p1_sinF", [128, H], F32)
    csst = sb("p1_csst", [128, 2], F32)
    onec = sb("p1_onec", [128, 1], BF16)
    oner = sb("p1_oner", [1, 128], BF16)
    oner32 = sb("p1_oner32", [1, 128], F32)
    onec32 = sb("p1_onec32", [128, 1], F32)
    ln1 = sb("p1_ln1", [128, ND * T], BF16)
    ktile = sb("p1_k", [128, T], BF16)
    vtm = sb("p1_v", [128, NT * 128], BF16)
    Ofm = sb("p1_O", [128, H * T], BF16)
    wqh = sb("p1_wqh", [128, ND * 128], BF16)
    qA = sb("p1_qA", [128, T], F32)
    qB = sb("p1_qB", [128, T], F32)
    qbf = sb("p1_qbf", [128, T], BF16)
    rowsbf = sb("p1_rowsbf", [128, T], BF16)
    rows32 = sb("p1_rows32", [128, 1024], F32)
    nrow = rowsbf[0:1, 0:T]
    invz = rows32[0:1, 0:512]
    zsb = rows32[0:1, 512:1024]
    nkmax = sb("p1_nkmax", [1, 1], F32)
    negub = sb("p1_negub", [1, T], BF16)
    xw = [sb(f"p1_xw{i}", [128, D], F32) for i in range(2)]
    qC = xw[0]
    stat = sb("p1_stat", [128, 8], F32)

    x0v = d["x0"].rearrange("(t p) n -> t p n", p=128)
    x1v = d["x1"].rearrange("(t p) n -> t p n", p=128)
    wqv = d["wq_bf"].rearrange("(c p) n -> p c n", p=128)
    wkv = d["wk_bf"].rearrange("(c p) n -> p c n", p=128)
    wvv = d["wv_bf"].rearrange("(c p) n -> p c n", p=128)
    wov = d["wo_bf"].rearrange("(c p) n -> p c n", p=128)

    with es:
        for eng in (nc.sync, nc.gpsimd, nc.scalar):
            eng.wait_ge(ctx.sem("p0"), ctx.p0_total)
        cn = ctx.sem("p1_cn")
        for dst, srcn in ((ident, "identity"), (cosF, "cosF"), (sinF, "sinF"),
                          (onec, "ones_col"), (oner, "ones_row"),
                          (oner32, "ones_row"), (onec32, "ones_col")):
            nc.gpsimd.dma_start(out=dst[:], in_=d[srcn]).then_inc(cn, 16)
        NCON = 7 * 16
        for eng in (nc.vector, nc.scalar, nc.tensor):
            eng.wait_ge(cn, NCON)

        with nc.psum_tensor("p1_pstr0", [128, 1024], BF16) as pstr0, \
             nc.psum_tensor("p1_pstr1", [128, 1024], BF16) as pstr1, \
             nc.sbuf_tensor("p1_lnbf0", [128, D], BF16) as lnbf0, \
             nc.sbuf_tensor("p1_lnbf1", [128, D], BF16) as lnbf1:
            pstr = [pstr0, pstr1]
            lnbf = [lnbf0, lnbf1]
            sy = Sync(ctx, "pa")
            sy.dma("xa", nc.sync.dma_start(out=xw[0][:], in_=x0v[0]))

            def p1a_tile(u, h2):
                xq = ("xa", "xb")[h2]
                sy.waitn("dv", xq)
                sy.waitn("ac", xq)
                if h2 == 0:
                    sy.waitn("sp", "dv", 2)
                    sy.dma("xb", nc.sync.dma_start(out=xw[1][:], in_=x0v[2 * u + 1]))
                elif not (isinstance(u, int) and u == NT // 2 - 1):
                    sy.waitn("sp", "dv", 2)
                    sy.dma("xa", nc.sync.dma_start(out=xw[0][:], in_=x0v[2 * u + 2]))
                sy.waitn("dv", "dv")
                sy.done("dv", nc.vector.tensor_reduce(out=stat[:, 0:1], in_=xw[h2][:],
                                                      axis=AX.X, op=OP.add))
                sy.waitn("ac", "ac")
                sy.done("ac", nc.scalar.activation(out=qA[:, 0:D], in_=xw[h2][:],
                                                   func=AF.Square, accum_out=stat[:, 1:2]))
                sy.waitn("dv", "dv")
                sy.done("dv", nc.vector.tensor_scalar_mul(stat[:, 2:3], stat[:, 0:1],
                                                          float(1.0 / D)))
                sy.waitn("dv", "dv")
                sy.done("dv", nc.vector.tensor_tensor(out=stat[:, 3:4], in0=stat[:, 2:3],
                                                      in1=stat[:, 2:3], op=OP.mult))
                sy.waitn("dv", "dv")
                sy.waitn("dv", "ac")
                sy.done("dv", nc.vector.scalar_tensor_tensor(
                    out=stat[:, 4:5], in0=stat[:, 1:2], scalar=float(1.0 / D),
                    in1=stat[:, 3:4], op0=OP.mult, op1=OP.subtract))
                sy.waitn("ac", "dv")
                sy.done("ac", nc.scalar.activation(out=stat[:, 5:6], in_=stat[:, 4:5],
                                                   func=AF.Sqrt, bias=f32c(1e-5)))
                sy.waitn("dv", "ac")
                sy.done("dv", nc.vector.reciprocal(stat[:, 6:7], stat[:, 5:6]))
                sy.waitn("dv", "dv")
                sy.done("dv", nc.vector.tensor_tensor(out=stat[:, 7:8], in0=stat[:, 2:3],
                                                      in1=stat[:, 6:7], op=OP.mult))
                sy.waitn("dv", "dv")
                sy.waitn("dv", "pe", 2)
                sy.done("dv", nc.vector.tensor_scalar(out=lnbf[h2][:], in0=xw[h2][:],
                                                      scalar1=stat[:, 6:7],
                                                      scalar2=stat[:, 7:8],
                                                      op0=OP.mult, op1=OP.subtract))
                for half in range(2):
                    sy.waitn("pe", "dv")
                    sy.waitn("pe", "dv", 8 - half)
                    inst = None
                    for c8 in range(8):
                        c = half * 8 + c8
                        inst = nc.tensor.transpose(pstr[half][:, c8 * 128:(c8 + 1) * 128],
                                                   lnbf[h2][:, c * 128:(c + 1) * 128],
                                                   ident[:])
                    sy.done("pe", inst)
                tsl = bass.ts(2 * u + h2, 128)
                for half in range(2):
                    sy.waitn("dv", "pe", 1 - half)
                    src2 = pstr[half][:].rearrange("p (c n) -> p c n", n=128)
                    dstv = ln1[:].rearrange("p (c t) -> p c t", t=T)
                    sy.done("dv", nc.vector.tensor_copy(
                        dstv[:, half * 8:(half + 1) * 8, tsl], src2))

            sy.begin_loop([("dv", "xa"), ("dv", "xb"), ("ac", "xa"), ("ac", "xb"),
                           ("sp", "dv"), ("dv", "dv"), ("dv", "ac"), ("ac", "dv"),
                           ("ac", "ac"), ("pe", "dv"), ("dv", "pe")],
                          {"xa": 16, "xb": 16, "dv": 18, "ac": 4, "pe": 4})
            with ctx.fori(NT // 2 - 1, [PE, DVE, SP, ACT]) as u:
                sy.iter_start(u)
                p1a_tile(u, 0)
                p1a_tile(u, 1)
            sy.end_loop(NT // 2 - 1)
            p1a_tile(NT // 2 - 1, 0)
            p1a_tile(NT // 2 - 1, 1)
            sy.free()
        ctx.barrier()

        with nc.psum_tensor("p1_psq", [128, 2048], F32) as psq, \
             nc.psum_tensor("p1_pss0", [128, 512], F32) as pss0, \
             nc.psum_tensor("p1_pss1", [128, 512], F32) as pss1, \
             nc.psum_tensor("p1_pso", [128, 512], F32) as pso, \
             nc.psum_tensor("p1_psz", [1, 512], F32) as psz, \
             nc.sbuf_tensor("p1_pt0", [128, 512], BF16) as pt0, \
             nc.sbuf_tensor("p1_pt1", [128, 512], BF16) as pt1, \
             nc.sbuf_tensor("p1_pt3", [128, 512], BF16) as pt3, \
             nc.sbuf_tensor("p1_invzbc", [128, 512], F32) as invzbc, \
             nc.sbuf_tensor("p1_dmask", [128, 4 * 512], BF16) as dmask:
            pss = [pss0, pss1]
            ptile = [pt0, pt1]

            sy = Sync(ctx, "pb")
            nc.gpsimd.dma_start(out=dmask[:], in_=d["diagmask"]).then_inc(cn, 16)
            nc.tensor.wait_ge(cn, NCON + 16)
            sy.dma("wk", nc.sync.dma_start(
                out=wqh[:].rearrange("p (c n) -> p c n", n=128), in_=wkv))
            sy.dma("wk", nc.sync.dma_start(
                out=qbf[:].rearrange("p (c n) -> p c n", n=128), in_=wvv))
            sy.wait("pe", "wk", 32)
            inst = None
            for tq in range(4):
                for c in range(ND):
                    inst = nc.tensor.matmul(psq[:, tq * 512:(tq + 1) * 512],
                                            wqh[:, c * 128:(c + 1) * 128],
                                            ln1[:, c * T + tq * 512: c * T + (tq + 1) * 512],
                                            start=(c == 0), stop=(c == ND - 1))
            sy.done("pe", inst)
            for tt in range(NT):
                sy.waitn("pe", "dv", 1)
                inst = None
                for c in range(ND):
                    inst = nc.tensor.matmul(pss[tt % 2][:, 0:128],
                                            ln1[:, c * T + tt * 128: c * T + (tt + 1) * 128],
                                            qbf[:, c * 128:(c + 1) * 128],
                                            start=(c == 0), stop=(c == ND - 1))
                sy.done("pe", inst)
                sy.waitn("dv", "pe")
                sy.done("dv", nc.vector.tensor_copy(vtm[:, tt * 128:(tt + 1) * 128],
                                                    pss[tt % 2][:, 0:128]))
            sy.wait("dv", "pe", 1)
            sy.done("dv", nc.vector.tensor_copy(qA[:], psq[:]))
            sy.waitn("dv", "dv")
            sy.done("dv", nc.vector.tensor_copy(ktile[:], qA[:]))
            sy.waitn("ac", "dv")
            sy.done("ac", nc.scalar.activation(out=qB[:], in_=qA[:], func=AF.Square))
            for tq in range(4):
                sy.waitn("pe", "ac")
                sy.done("pe", nc.tensor.matmul(psz[:], onec32[:],
                                               qB[:, tq * 512:(tq + 1) * 512],
                                               start=True, stop=True,
                                               skip_group_check=True))
                sy.waitn("ac", "pe")
                sy.done("ac", nc.scalar.activation(out=nrow[:, tq * 512:(tq + 1) * 512],
                                                   in_=psz[0:1, :], func=AF.Sqrt))
            sy.waitn("dv", "ac")
            sy.done("dv", nc.vector.tensor_reduce(out=nkmax[:], in_=nrow, axis=AX.X,
                                                  op=OP.max))
            sy.waitn("pe", "dv")
            sy.free()
            ctx.barrier()

            sy = Sync(ctx, "pc")
            og = [0, 4, 12, 24]
            wslice = lambda hh: wqv[:, :, bass.ts(hh, 128)]
            sy.dma("wq", nc.sync.dma_start(
                out=wqh[:].rearrange("p (c n) -> p c n", n=128), in_=wslice(0)))

            def s_group(g, o, kt, qc, par, diag):
                sy.waitn("pe", "ac", 1)
                nc.tensor.matmul(pss[par][:], ktile[:, kt * 128:(kt + 1) * 128],
                                 qbf[:, qc * 512:(qc + 1) * 512], start=True, stop=False)
                sy.done("pe", nc.tensor.matmul(pss[par][:], oner[:],
                                               negub[0:1, qc * 512:(qc + 1) * 512],
                                               start=False, stop=True))
                sy.waitn("ac", "pe")
                sy.waitn("ac", "pe", 3)
                sy.done("ac", nc.scalar.activation(out=ptile[par][:], in_=pss[par][:],
                                                   func=AF.Exp, scale=f32c(SC_ATT)))
                if diag is not None:
                    sy.waitn("dv", "ac")
                    sy.done("dv", nc.vector.tensor_tensor(
                        out=ptile[par][:], in0=ptile[par][:],
                        in1=dmask[:, diag * 512:(diag + 1) * 512], op=OP.mult))

            def pv_group(g, o, kt, qc, par, first, diag):
                if diag is not None:
                    sy.waitn("pe", "dv")
                else:
                    sy.waitn("pe", "ac")
                nc.tensor.matmul(pso[:], vtm[:, kt * 128:(kt + 1) * 128], ptile[par][:],
                                 start=first, stop=False, skip_group_check=True)
                sy.done("pe", nc.tensor.matmul(psz[:], onec[:], ptile[par][:],
                                               start=first, stop=False,
                                               skip_group_check=True))

            def head_body(h):
                sy.waitn("pe", "wq")
                sy.waitn("pe", "dv", 38)
                inst = None
                for tq in range(4):
                    for c in range(ND):
                        inst = nc.tensor.matmul(
                            psq[:, tq * 512:(tq + 1) * 512],
                            wqh[:, c * 128:(c + 1) * 128],
                            ln1[:, c * T + tq * 512: c * T + (tq + 1) * 512],
                            start=(c == 0), stop=(c == ND - 1))
                sy.done("pe", inst)
                sy.waitn("sp", "pe")
                sy.dma("wq", nc.sync.dma_start(
                    out=wqh[:].rearrange("p (c n) -> p c n", n=128), in_=wslice(h + 1)))
                sy.waitn("po", "dv", 34)
                hx = bass.ts(h, 1)
                nc.gpsimd.tensor_copy(csst[:, 0:1], cosF[:, hx])
                sy.done("po", nc.gpsimd.tensor_copy(csst[:, 1:2], sinF[:, hx]))
                sy.waitn("dv", "pe")
                sy.waitn("dv", "ac", 44)
                sy.done("dv", nc.vector.tensor_copy(qA[:], psq[:]))
                sy.waitn("pe", "dv")
                sy.done("pe", nc.tensor.matmul(pso[0:1, 0:1], onec[0:1, 0:1],
                                               onec[0:1, 0:1], start=True, stop=True,
                                               skip_group_check=True))

                sy.waitn("dv", "dv")
                sy.done("dv", nc.vector.tensor_copy(qC[0:64, :], qA[64:128, :]))
                sy.done("dv", nc.vector.tensor_copy(qC[64:128, :], qA[0:64, :]))
                sy.waitn("dv", "po")
                sy.done("dv", nc.vector.tensor_scalar_mul(qB[:], qA[:], csst[:, 0:1]))
                sy.waitn("dv", "dv")
                sy.done("dv", nc.vector.scalar_tensor_tensor(out=qB[:], in0=qC[:],
                                                             scalar=csst[:, 1:2], in1=qB[:],
                                                             op0=OP.mult, op1=OP.add))
                sy.waitn("pe", "dv")
                sy.done("pe", nc.tensor.matmul(pso[0:1, 0:1], onec[0:1, 0:1],
                                               onec[0:1, 0:1], start=True, stop=True,
                                               skip_group_check=True))

                sy.waitn("dv", "dv")
                sy.waitn("dv", "pe", 1)
                sy.done("dv", nc.vector.tensor_copy(qbf[:], qB[:]))
                sy.waitn("ac", "dv")
                sy.done("ac", nc.scalar.activation(out=qA[:], in_=qB[:], func=AF.Square))
                sy.waitn("pe", "dv")
                sy.done("pe", nc.tensor.matmul(pso[0:1, 0:1], onec[0:1, 0:1],
                                               onec[0:1, 0:1], start=True, stop=True,
                                               skip_group_check=True))

                for tq in range(4):
                    sy.waitn("pe", "ac")
                    sy.waitn("pe", "dv", 9)
                    sy.done("pe", nc.tensor.matmul(psz[:], onec32[:],
                                                   qA[:, tq * 512:(tq + 1) * 512],
                                                   start=True, stop=True,
                                                   skip_group_check=True))
                    sy.waitn("ac", "pe")
                    sy.done("ac", nc.scalar.activation(
                        out=nrow[:, tq * 512:(tq + 1) * 512], in_=psz[0:1, :],
                        func=AF.Sqrt))
                sy.waitn("pe", "dv")
                sy.done("pe", nc.tensor.matmul(pso[0:1, 0:1], onec[0:1, 0:1],
                                               onec[0:1, 0:1], start=True, stop=True,
                                               skip_group_check=True))
                sy.waitn("dv", "ac")
                sy.waitn("dv", "pe", 5)
                sy.done("dv", nc.vector.tensor_scalar(out=negub[0:1, :], in0=nrow,
                                                      scalar1=nkmax[0:1, 0:1],
                                                      scalar2=-1.0,
                                                      op0=OP.mult, op1=OP.mult))
                for qc in range(4):
                    sy.waitn("pe", "dv")
                    g = 0
                    for kt in range(4 * qc + 4):
                        diag = kt - 4 * qc if kt >= 4 * qc else None
                        o = og[qc] + (4 + kt if diag is None else diag)
                        par = g % 2
                        s_group(g, o, kt, qc, par, diag)
                        pv_group(g, o, kt, qc, par, first=(g == 0), diag=diag)
                        g += 1
                    sy.waitn("dv", "pe")
                    sy.done("dv", nc.vector.tensor_copy(zsb, psz[0:1, :]))
                    sy.waitn("dv", "dv")
                    sy.done("dv", nc.vector.reciprocal(invz, zsb))
                    sy.waitn("pe", "dv")
                    sy.waitn("pe", "ac")
                    sy.done("pe", nc.tensor.matmul(pss[0][:], oner32[:], invz,
                                                   start=True, stop=True,
                                                   skip_group_check=True))
                    sy.waitn("dv", "pe")
                    sy.done("dv", nc.vector.tensor_copy(invzbc[:], pss[0][:]))
                    sy.waitn("dv", "dv")
                    sy.waitn("dv", "po")
                    sy.done("dv", nc.vector.tensor_tensor(out=pt3[:], in0=pso[:],
                                                          in1=invzbc[:], op=OP.mult))
                    sy.waitn("po", "dv")
                    sy.done("po", nc.gpsimd.tensor_copy(
                        Ofm[:, bass.ts(4 * h + qc, 512)], pt3[:]))

            pairs = [("pe", "wq"), ("pe", "dv"), ("pe", "ac"), ("sp", "pe"),
                     ("dv", "pe"), ("dv", "ac"), ("dv", "dv"), ("dv", "po"),
                     ("ac", "dv"), ("ac", "pe"), ("ac", "ac"), ("po", "dv"),
                     ("po", "pe")]
            sy.begin_loop(pairs, {"wq": 16, "pe": 93, "dv": 39, "ac": 45, "po": 5})
            with ctx.fori(H, [PE, ACT, DVE, SP, Pool]) as h:
                sy.iter_start(h)
                head_body(h)
            sy.end_loop(H)
            nc.sync.wait_ge(sy.p["wq"], 16 * (H + 1))
            sy.free()
            ctx.barrier()

            sy = Sync(ctx, "pd")
            sy.dma("wo", nc.sync.dma_start(
                out=ln1[:].rearrange("p (c n) -> p c n", n=2048), in_=wov))
            nc.tensor.wait_ge(sy.p["wo"], 16)

            def p1d_tile(u, par):
                xq = ("xa", "xb")[par]
                oq = ("oa", "ob")[par]
                sy.waitn("sp", "dv", 1)
                sy.dma(xq, nc.sync.dma_start(out=xw[par][:], in_=x0v[2 * u + par]))
                ostg = (qbf, ktile)[par]
                sy.waitn("po", "pe", 4)
                ov = Ofm[:].rearrange("p (g t) -> p g t", t=T)
                sy.done("po", nc.gpsimd.tensor_copy(
                    ostg[:].rearrange("p (g n) -> p g n", n=128),
                    ov[:, :, bass.ts(2 * u + par, 128)]))
                sy.waitn("pe", "po")
                sy.waitn("pe", "dv")
                for dp in range(4):
                    inst = None
                    for hh in range(H):
                        inst = nc.tensor.matmul(
                            psq[:, dp * 512:(dp + 1) * 512],
                            ostg[:, hh * 128:(hh + 1) * 128],
                            ln1[:, hh * 2048 + dp * 512: hh * 2048 + (dp + 1) * 512],
                            start=(hh == 0), stop=(hh == H - 1))
                    sy.done("pe", inst)
                sy.waitn("dv", "pe")
                sy.waitn("dv", xq)
                sy.waitn("dv", oq)
                x1b = (qA, qB)[par]
                sy.done("dv", nc.vector.tensor_tensor(out=x1b[:], in0=psq[:],
                                                      in1=xw[par][:], op=OP.add))
                sy.waitn("ac", "dv")
                sy.dma(oq, nc.scalar.dma_start(out=x1v[2 * u + par], in_=x1b[:]))

            sy.begin_loop([("sp", "dv"), ("po", "pe"), ("pe", "po"), ("pe", "dv"),
                           ("dv", "pe"), ("dv", "xa"), ("dv", "xb"), ("dv", "oa"),
                           ("dv", "ob"), ("ac", "dv")],
                          {"xa": 16, "xb": 16, "oa": 16, "ob": 16,
                           "pe": 8, "dv": 2, "po": 2})
            with ctx.fori(NT // 2, [PE, DVE, SP, ACT, Pool]) as u:
                sy.iter_start(u)
                p1d_tile(u, 0)
                p1d_tile(u, 1)
            sy.end_loop(NT // 2)
            nc.sync.wait_ge(sy.p["oa"], 16 * NT // 2)
            nc.sync.wait_ge(sy.p["ob"], 16 * NT // 2)
            sy.free()
        ctx.barrier()


EPS32 = float(np.finfo(np.float32).eps)


def emit_p2(ctx):
    nc, d = ctx.nc, ctx.d
    es = ExitStack()
    sb = lambda n, shp, dt=F32: es.enter_context(nc.sbuf_tensor(n, shp, dt))

    ident = sb("p2_ident", [128, 128], BF16)
    wc = sb("p2_wcT", [128, 16 * 2048], BF16)
    wcs = [sb(f"p2_wcs{i}", [128, 16 * 512], BF16) for i in range(2)]
    genes = sb("p2_genes", [128, 32 * 256], BF16)
    wd = sb("p2_wd", [128, 2 * 2048], BF16)
    wu = sb("p2_wu", [128, 2 * 2048], BF16)
    normw = sb("p2_normw", [128, D], BF16)
    invtemp = sb("p2_invtemp", [128, 1], F32)
    scalesc = sb("p2_scale", [128, 1], F32)
    stat = sb("p2_stat", [128, 12], F32)
    xw = [sb(f"p2_xw{i}", [128, D], F32) for i in range(2)]
    xn_tm = sb("p2_xntm", [128, D], F32)
    xnbf = sb("p2_xnbf", [128, D], BF16)
    xnfm = sb("p2_xnfm", [128, ND * 128], BF16)
    logits = sb("p2_logits", [128, 4096], F32)
    lrep = sb("p2_lrep", [128, 4096], F32)
    down = lrep[:, 0:2048]
    up = lrep[:, 2048:4096]
    mxall = sb("p2_mx", [128, 32], F32)
    negm = sb("p2_negm", [128, 1], F32)
    zrow = sb("p2_z", [128, 2], F32)
    ebuf = sb("p2_e", [128, 4096], BF16)
    wT = ebuf
    wbuf = lrep[:, 0:2048].bitcast(BF16)
    exprs = sb("p2_expr", [128, 2 * 128], BF16)

    x1v = d["x1"].rearrange("(t p) n -> t p n", p=128)
    x2v = d["x2"].rearrange("(t p) n -> t p n", p=128)
    wcv = d["wc_bf"].rearrange("(c p) n -> p c n", p=128)
    gnv = d["genes_bf"].rearrange("(g p) n -> p g n", p=128)
    wdv = d["wd_bf"].rearrange("(c p) n -> p c n", p=128)
    wuv = d["wu_bf"].rearrange("(c p) n -> p c n", p=128)

    with es, \
         nc.psum_tensor("p2_ptr0", [128, 1024], BF16) as ptr0, \
         nc.psum_tensor("p2_ptr1", [128, 1024], BF16) as ptr1, \
         nc.psum_tensor("p2_plg0", [128, 512], F32) as plg0, \
         nc.psum_tensor("p2_plg1", [128, 512], F32) as plg1, \
         nc.psum_tensor("p2_plg2", [128, 512], F32) as plg2, \
         nc.psum_tensor("p2_plg3", [128, 512], F32) as plg3, \
         nc.psum_tensor("p2_pex", [128, 256], F32) as pex, \
         nc.psum_tensor("p2_dum", [1, 8], F32) as pdum:
        ptr = [ptr0, ptr1]
        plg = [plg0, plg1, plg2, plg3]
        sy = Sync(ctx, "p2")
        cn = ctx.sem("p2_cn")
        nc.gpsimd.dma_start(out=ident[:], in_=d["identity"]).then_inc(cn, 16)
        nc.gpsimd.dma_start(out=wc[:].rearrange("p (c n) -> p c n", n=2048),
                            in_=wcv[:, :, 0:2048]).then_inc(cn, 16)
        nc.gpsimd.dma_start(out=genes[:].rearrange("p (g n) -> p g n", n=256),
                            in_=gnv).then_inc(cn, 16)
        nc.gpsimd.dma_start(out=wd[:].rearrange("p (c n) -> p c n", n=2048),
                            in_=wdv).then_inc(cn, 16)
        nc.gpsimd.dma_start(out=wu[:].rearrange("p (c n) -> p c n", n=2048),
                            in_=wuv).then_inc(cn, 16)
        nc.gpsimd.dma_start(out=normw[:], in_=d["l0_normw_bc"]).then_inc(cn, 16)
        nc.gpsimd.dma_start(out=invtemp[:], in_=d["l0_temp_bc"]).then_inc(cn, 16)
        nc.gpsimd.dma_start(out=scalesc[:], in_=d["l0_scale_bc"]).then_inc(cn, 16)
        NCN = 8 * 16
        for e in (nc.vector, nc.scalar, nc.tensor, nc.sync):
            e.wait_ge(cn, NCN)
        dvp = ctx.sem("p2_pre")
        nc.vector.tensor_scalar_max(invtemp[:], invtemp[:], 0.1).then_inc(dvp, 1)
        nc.vector.wait_ge(dvp, 1)
        nc.vector.reciprocal(invtemp[:], invtemp[:]).then_inc(dvp, 1)
        for e in (nc.scalar, nc.tensor):
            e.wait_ge(dvp, 2)
        sy.dma("xa", nc.sync.dma_start(out=xw[0][:], in_=x1v[0]))

        def tile(u, par, prefetch=True):
            xq = ("xa", "xb")[par]
            oq = ("oa", "ob")[par]
            tt = 2 * u + par
            b = dict(sy.n)
            sy.waitn("pe", "dv")
            sy.waitn("dv", "pe")
            sy.waitn("ac", "dv")
            if prefetch:
                sy.waitn("sp", ("ob", "oa")[par])
                sy.dma(("xb", "xa")[par],
                       nc.sync.dma_start(out=xw[1 - par][:], in_=x1v[tt + 1]))
            sy.waitn("dv", xq)
            sy.waitn("ac", xq)
            sy.waitn("dv", "dv")
            sy.done("dv", nc.vector.tensor_reduce(out=stat[:, 0:1], in_=xw[par][:],
                                                  axis=AX.X, op=OP.add))
            sy.waitn("ac", "ac")
            sy.done("ac", nc.scalar.activation(out=lrep[:, 0:D], in_=xw[par][:],
                                               func=AF.Square, accum_out=stat[:, 1:2]))
            sy.waitn("dv", "dv")
            sy.done("dv", nc.vector.tensor_scalar_mul(stat[:, 2:3], stat[:, 0:1],
                                                      float(1.0 / D)))
            sy.waitn("dv", "dv")
            sy.done("dv", nc.vector.tensor_tensor(out=stat[:, 3:4], in0=stat[:, 2:3],
                                                  in1=stat[:, 2:3], op=OP.mult))
            sy.waitn("pe", "dv")
            sy.done("pe", nc.tensor.matmul(pdum[0:1, 0:1], ident[0:1, 0:1],
                                           ident[0:1, 0:1], start=True, stop=True,
                                           skip_group_check=True))
            sy.waitn("dv", "dv")
            sy.waitn("dv", "ac")
            sy.done("dv", nc.vector.scalar_tensor_tensor(out=stat[:, 4:5], in0=stat[:, 1:2],
                                                         scalar=float(1.0 / D),
                                                         in1=stat[:, 3:4],
                                                         op0=OP.mult, op1=OP.subtract))
            sy.waitn("ac", "dv")
            sy.done("ac", nc.scalar.activation(out=stat[:, 5:6], in_=stat[:, 4:5],
                                               func=AF.Sqrt, bias=f32c(1e-5)))
            sy.waitn("dv", "ac")
            sy.done("dv", nc.vector.reciprocal(stat[:, 6:7], stat[:, 5:6]))
            sy.waitn("dv", "dv")
            sy.done("dv", nc.vector.tensor_tensor(out=stat[:, 7:8], in0=stat[:, 4:5],
                                                  in1=stat[:, 6:7], op=OP.mult))
            sy.waitn("dv", "dv")
            sy.done("dv", nc.vector.tensor_tensor(out=stat[:, 8:9], in0=stat[:, 7:8],
                                                  in1=stat[:, 6:7], op=OP.mult))
            sy.waitn("ac", "dv")
            sy.done("ac", nc.scalar.activation(out=stat[:, 9:10], in_=stat[:, 8:9],
                                               func=AF.Sqrt, bias=f32c(EPS32)))
            sy.waitn("dv", "ac")
            sy.done("dv", nc.vector.reciprocal(stat[:, 10:11], stat[:, 9:10]))
            sy.waitn("pe", "dv")
            sy.done("pe", nc.tensor.matmul(pdum[0:1, 0:1], ident[0:1, 0:1],
                                           ident[0:1, 0:1], start=True, stop=True,
                                           skip_group_check=True))
            sy.waitn("dv", "dv")
            sy.done("dv", nc.vector.tensor_tensor(out=stat[:, 10:11], in0=stat[:, 10:11],
                                                  in1=stat[:, 6:7], op=OP.mult))
            sy.waitn("dv", "dv")
            sy.done("dv", nc.vector.tensor_tensor(out=stat[:, 11:12], in0=stat[:, 2:3],
                                                  in1=stat[:, 10:11], op=OP.mult))
            sy.waitn("dv", "dv")
            sy.done("dv", nc.vector.tensor_scalar(out=xn_tm[:], in0=xw[par][:],
                                                  scalar1=stat[:, 10:11],
                                                  scalar2=stat[:, 11:12],
                                                  op0=OP.mult, op1=OP.subtract))
            sy.waitn("dv", "dv")
            sy.done("dv", nc.vector.tensor_tensor(out=xn_tm[:], in0=xn_tm[:],
                                                  in1=normw[:], op=OP.mult))
            sy.waitn("dv", "dv")
            r_xnbf = sy.done("dv", nc.vector.tensor_copy(xnbf[:], xn_tm[:]))
            sy.waitn("pe", "dv")
            sy.done("pe", nc.tensor.matmul(pdum[0:1, 0:1], ident[0:1, 0:1],
                                           ident[0:1, 0:1], start=True, stop=True,
                                           skip_group_check=True))

            for half in range(2):
                sy.wait("pe", "dv", r_xnbf)
                inst = None
                for c8 in range(8):
                    c = half * 8 + c8
                    inst = nc.tensor.transpose(ptr[half][:, c8 * 128:(c8 + 1) * 128],
                                               xnbf[:, c * 128:(c + 1) * 128], ident[:])
                sy.done("pe", inst)
            r_trev = []
            for half in range(2):
                sy.waitn("dv", "pe")
                src2 = ptr[half][:].rearrange("p (c n) -> p c n", n=128)
                r_trev.append(sy.done("dv", nc.vector.tensor_copy(
                    xnfm[:].rearrange("p (c n) -> p c n", n=128)[:, half * 8:(half + 1) * 8, :],
                    src2)))
            sy.waitn("sp", "pe")
            for gc in (4, 5):
                sy.dma(("wca", "wcb")[gc % 2], nc.sync.dma_start(
                    out=wcs[gc % 2][:].rearrange("p (c n) -> p c n", n=512),
                    in_=wcv[:, :, gc * 512:(gc + 1) * 512]))
            r_lg = []
            r_mm = []
            for gc in range(8):
                sy.waitn("pe", "dv")
                if gc >= 4:
                    sy.wait("pe", "dv", r_lg[gc - 4])
                    sy.waitn("pe", ("wca", "wcb")[gc % 2])
                inst = None
                for c in range(ND):
                    if gc < 4:
                        wsl = wc[:, c * 2048 + gc * 512: c * 2048 + (gc + 1) * 512]
                    else:
                        wsl = wcs[gc % 2][:, c * 512:(c + 1) * 512]
                    inst = nc.tensor.matmul(
                        plg[gc % 4][:], xnfm[:, c * 128:(c + 1) * 128], wsl,
                        start=(c == 0), stop=(c == ND - 1))
                r_mm.append(sy.done("pe", inst))
                if gc in (4, 5):
                    sy.wait("sp", "pe", r_mm[gc])
                    g2 = gc + 2
                    sy.dma(("wca", "wcb")[g2 % 2], nc.sync.dma_start(
                        out=wcs[g2 % 2][:].rearrange("p (c n) -> p c n", n=512),
                        in_=wcv[:, :, g2 * 512:(g2 + 1) * 512]))
                sy.waitn("dv", "pe")
                r_lg.append(sy.done("dv", nc.vector.tensor_scalar_mul(
                    logits[:, gc * 512:(gc + 1) * 512], plg[gc % 4][:], invtemp[:, 0:1])))
            sy.waitn("dv", "dv")
            sy.done("dv", nc.vector.max(out=mxall[:, 0:8], in_=logits[:]))
            sy.waitn("pe", "dv")
            sy.done("pe", nc.tensor.matmul(pdum[0:1, 0:1], ident[0:1, 0:1],
                                           ident[0:1, 0:1], start=True, stop=True,
                                           skip_group_check=True))

            sy.waitn("dv", "dv")
            sy.done("dv", nc.vector.match_replace(out=lrep[:], in_to_replace=mxall[:, 0:8],
                                                  in_values=logits[:], imm_value=-1e30))
            for rnd in range(1, 4):
                sy.waitn("dv", "dv")
                sy.done("dv", nc.vector.max(out=mxall[:, 8 * rnd:8 * rnd + 8], in_=lrep[:]))
                sy.waitn("pe", "dv")
                sy.done("pe", nc.tensor.matmul(pdum[0:1, 0:1], ident[0:1, 0:1],
                                               ident[0:1, 0:1], start=True, stop=True,
                                               skip_group_check=True))
                if rnd < 3:
                    sy.waitn("dv", "dv")
                    sy.done("dv", nc.vector.match_replace(
                        out=lrep[:], in_to_replace=mxall[:, 8 * rnd:8 * rnd + 8],
                        in_values=lrep[:], imm_value=-1e30))
            sy.waitn("dv", "dv")
            sy.done("dv", nc.vector.tensor_scalar_mul(negm[:, 0:1], mxall[:, 0:1], -1.0))
            sy.waitn("ac", "dv")
            r_exp = sy.done("ac", nc.scalar.activation(out=ebuf[:], in_=logits[:],
                                                       func=AF.Exp, bias=negm[:, 0:1]))
            sy.waitn("pe", "dv")
            sy.done("pe", nc.tensor.matmul(pdum[0:1, 0:1], ident[0:1, 0:1],
                                           ident[0:1, 0:1], start=True, stop=True,
                                           skip_group_check=True))
            sy.wait("dv", "ac", r_exp)
            r_w = sy.done("dv", nc.vector.scalar_tensor_tensor(
                out=wbuf, in0=logits[:], scalar=mxall[:, 31:32], in1=ebuf[:],
                op0=OP.is_ge, op1=OP.mult, accum_out=zrow[:, 0:1]))
            sy.waitn("dv", "dv")
            r_invz = sy.done("dv", nc.vector.reciprocal(zrow[:, 1:2], zrow[:, 0:1]))
            r_wtev = []
            for g4 in range(4):
                sy.wait("pe", "dv", r_w)
                if g4 >= 2:
                    sy.wait("pe", "dv", r_wtev[g4 - 2])
                inst = None
                for c8 in range(8):
                    c = g4 * 8 + c8
                    inst = nc.tensor.transpose(ptr[g4 % 2][:, c8 * 128:(c8 + 1) * 128],
                                               wbuf[:, c * 128:(c + 1) * 128], ident[:])
                sy.done("pe", inst)
                sy.waitn("dv", "pe")
                src2 = ptr[g4 % 2][:].rearrange("p (c n) -> p c n", n=128)
                r_wtev.append(sy.done("dv", nc.vector.tensor_copy(
                    wT[:].rearrange("p (c n) -> p c n", n=128)[:, g4 * 8:(g4 + 1) * 8, :],
                    src2)))
            for ch in range(2):
                sy.waitn("pe", "dv")
                inst = None
                for g in range(32):
                    inst = nc.tensor.matmul(
                        pex[:, ch * 128:(ch + 1) * 128],
                        genes[:, g * 256 + ch * 128: g * 256 + (ch + 1) * 128],
                        wT[:, g * 128:(g + 1) * 128],
                        start=(g == 0), stop=(g == 31))
                sy.done("pe", inst)
            sy.waitn("dv", "pe")
            r_expr = sy.done("dv", nc.vector.tensor_copy(exprs[:], pex[:]))
            for dp in range(4):
                sy.wait("pe", "dv", r_expr)
                inst = None
                for ch in range(2):
                    inst = nc.tensor.matmul(
                        plg[dp][:], exprs[:, ch * 128:(ch + 1) * 128],
                        wd[:, ch * 2048 + dp * 512: ch * 2048 + (dp + 1) * 512],
                        start=(ch == 0), stop=(ch == 1))
                sy.done("pe", inst)
                sy.waitn("ac", "pe")
                sy.wait("ac", "dv", r_invz)
                sy.done("ac", nc.scalar.activation(out=down[:, dp * 512:(dp + 1) * 512],
                                                   in_=plg[dp][:], func=AF.Tanh,
                                                   scale=zrow[:, 1:2]))
            for dp in range(4):
                sy.waitn("pe", "ac")
                inst = None
                for ch in range(2):
                    inst = nc.tensor.matmul(
                        plg[dp][:], exprs[:, ch * 128:(ch + 1) * 128],
                        wu[:, ch * 2048 + dp * 512: ch * 2048 + (dp + 1) * 512],
                        start=(ch == 0), stop=(ch == 1))
                sy.done("pe", inst)
                sy.waitn("dv", "pe")
                sy.done("dv", nc.vector.tensor_scalar_mul(
                    up[:, dp * 512:(dp + 1) * 512], plg[dp][:], zrow[:, 1:2]))
            sy.waitn("dv", "dv")
            sy.waitn("dv", "ac")
            sy.done("dv", nc.vector.tensor_tensor(out=down[:], in0=down[:], in1=xn_tm[:],
                                                  op=OP.mult))
            sy.waitn("dv", "dv")
            sy.done("dv", nc.vector.tensor_tensor(out=down[:], in0=down[:], in1=up[:],
                                                  op=OP.mult))
            sy.waitn("dv", "dv")
            r_x2 = sy.done("dv", nc.vector.scalar_tensor_tensor(
                out=xw[par][:], in0=down[:], scalar=scalesc[:, 0:1], in1=xw[par][:],
                op0=OP.mult, op1=OP.add))
            sy.wait("ac", "dv", r_x2)
            sy.dma(oq, nc.scalar.dma_start(out=x2v[tt], in_=xw[par][:]))

        N_DV, N_AC, N_PE = 45, 8, 32
        pairs = [("sp", "oa"), ("sp", "ob"), ("sp", "pe"),
                 ("dv", "xa"), ("dv", "xb"), ("ac", "xa"), ("ac", "xb"),
                 ("dv", "dv"), ("dv", "ac"), ("dv", "pe"),
                 ("ac", "dv"), ("ac", "ac"), ("ac", "pe"), ("pe", "dv"), ("pe", "ac"),
                 ("pe", "wca"), ("pe", "wcb")]
        sy.begin_loop(pairs, {"xa": 16, "xb": 16, "oa": 16, "ob": 16,
                              "wca": 2 * 32, "wcb": 2 * 32,
                              "dv": 2 * N_DV, "ac": 2 * N_AC, "pe": 2 * N_PE})
        with ctx.fori(NT // 2 - 1, [PE, DVE, SP, ACT]) as u:
            sy.iter_start(u)
            tile(u, 0)
            tile(u, 1)
        sy.end_loop(NT // 2 - 1)
        tile(NT // 2 - 1, 0)
        tile(NT // 2 - 1, 1, prefetch=False)
        nc.sync.wait_ge(sy.p["oa"], 16 * NT // 2)
        nc.sync.wait_ge(sy.p["ob"], 16 * NT // 2)
        sy.free()
    ctx.barrier()


def emit_p3(ctx):
    nc, d = ctx.nc, ctx.d
    es = ExitStack()
    sb = lambda n, shp, dt=F32: es.enter_context(nc.sbuf_tensor(n, shp, dt))

    ident = sb("p3_ident", [128, 128], BF16)
    ident32 = sb("p3_id32", [128, 128], F32)
    onesr32 = sb("p3_oner32", [1, 128], F32)
    stat = sb("p3_stat", [128, 8], F32)
    tri = sb("p3_tri", [128, 128], F32)
    es_bc = ExitStack()
    sb2 = lambda n, shp, dt: es_bc.enter_context(nc.sbuf_tensor(n, shp, dt))
    Bfm = sb2("p3_Bfm", [128, T], BF16)
    B_tm = sb2("p3_Btm", [128, T], BF16)
    Cbf = sb2("p3_Cbf", [128, T], BF16)
    Cfm32 = sb2("p3_C32", [128, T], F32)
    dtf = sb2("p3_dtf", [64, T], F32)
    cumf = sb2("p3_cumf", [64, T], F32)
    Acol = sb2("p3_Acol", [64, 2], F32)
    dtb = sb2("p3_dtb", [64, 1], F32)
    zer64 = sb2("p3_zer64", [64, 128], F32)
    convb = sb2("p3_convb", [128, NCC], F32)
    cbst = sb2("p3_cbst", [128, 1], F32)

    x2v = d["x2"].rearrange("(t p) n -> t p n", p=128)
    x3v = d["x3"].rearrange("(t p) n -> t p n", p=128)
    wzv = d["wz_bf"].rearrange("(c p) n -> p c n", p=128)
    wxv = d["wxbc_bf"].rearrange("(c p) n -> p c n", p=128)
    wdtv = d["wdt_bf"].rearrange("(c p) n -> p c n", p=128)
    wov = d["wout_bf"].rearrange("(c p) n -> p c n", p=128)
    cdv = d["convdiag_bf"].rearrange("(j c p) n -> p j c n", c=NCC, p=128)
    zsv = d["z_scr"].rearrange("(t p) n -> t p n", p=128)
    xtv = d["xtm_scr"].rearrange("(t p) n -> t p n", p=128)
    ynv = d["ynorm_scr"].rearrange("(c p) t -> p c t", p=128)
    cumd = d["cum_scr"]

    with es:
        cn = ctx.sem("p3cn")
        ncn = 0
        for dst, src in ((ident[:], d["identity"]), (ident32[:], d["identity"]),
                         (onesr32[:], d["ones_row"]), (tri[:], d["tri01"]),
                         (zer64[:], d["diagmask"][0:64, 1536:1664]),
                         (convb[:], d["conv_b_fm"]),
                         (Acol[:, 0:1], d["A_log"]), (dtb[:], d["dt_bias"])):
            nc.gpsimd.dma_start(out=dst, in_=src).then_inc(cn, 16)
            ncn += 16
        for e in (nc.vector, nc.scalar, nc.tensor, nc.sync, nc.gpsimd):
            e.wait_ge(cn, ncn)

        with nc.sbuf_tensor("p3_ln3", [128, ND * T], BF16) as ln3:
            lnv = ln3[:].rearrange("p (c t) -> p c t", t=T)
            with nc.psum_tensor("p3a_ps0", [128, 1024], BF16) as pstr0, \
                 nc.psum_tensor("p3a_ps1", [128, 1024], BF16) as pstr1, \
                 nc.sbuf_tensor("p3a_lnb", [128, D], BF16) as lnbf, \
                 nc.sbuf_tensor("p3a_xw", [128, D], F32) as xw, \
                 nc.sbuf_tensor("p3a_sq", [128, D], F32) as sqs:
                pstr = [pstr0, pstr1]
                sy = Sync(ctx, "p3a")

                def p3a_tile(u):
                    sy.waitn("ac", "dv")
                    sy.dma("xa", nc.scalar.dma_start(out=xw[:], in_=x2v[u]))
                    sy.waitn("dv", "xa")
                    sy.waitn("ac", "xa")
                    sy.waitn("dv", "dv")
                    sy.done("dv", nc.vector.tensor_reduce(out=stat[:, 0:1], in_=xw[:],
                                                          axis=AX.X, op=OP.add))
                    sy.waitn("ac", "ac")
                    sy.done("ac", nc.scalar.activation(out=sqs[:], in_=xw[:],
                                                       func=AF.Square,
                                                       accum_out=stat[:, 1:2]))
                    sy.waitn("dv", "dv")
                    sy.done("dv", nc.vector.tensor_scalar_mul(stat[:, 2:3], stat[:, 0:1],
                                                              float(1.0 / D)))
                    sy.waitn("dv", "dv")
                    sy.done("dv", nc.vector.tensor_tensor(out=stat[:, 3:4],
                                                          in0=stat[:, 2:3],
                                                          in1=stat[:, 2:3], op=OP.mult))
                    sy.waitn("dv", "dv")
                    sy.waitn("dv", "ac")
                    sy.done("dv", nc.vector.scalar_tensor_tensor(
                        out=stat[:, 4:5], in0=stat[:, 1:2], scalar=float(1.0 / D),
                        in1=stat[:, 3:4], op0=OP.mult, op1=OP.subtract))
                    sy.waitn("ac", "dv")
                    sy.done("ac", nc.scalar.activation(out=stat[:, 5:6], in_=stat[:, 4:5],
                                                       func=AF.Sqrt, bias=f32c(1e-5)))
                    sy.waitn("dv", "ac")
                    sy.done("dv", nc.vector.reciprocal(stat[:, 6:7], stat[:, 5:6]))
                    sy.waitn("dv", "dv")
                    sy.done("dv", nc.vector.tensor_tensor(out=stat[:, 7:8],
                                                          in0=stat[:, 2:3],
                                                          in1=stat[:, 6:7], op=OP.mult))
                    sy.waitn("dv", "dv")
                    sy.waitn("dv", "pe")
                    sy.done("dv", nc.vector.tensor_scalar(out=lnbf[:], in0=xw[:],
                                                          scalar1=stat[:, 6:7],
                                                          scalar2=stat[:, 7:8],
                                                          op0=OP.mult, op1=OP.subtract))
                    for half in range(2):
                        sy.waitn("pe", "dv")
                        sy.waitn("pe", "dv", 8 - half)
                        inst = None
                        for c8 in range(8):
                            c = half * 8 + c8
                            inst = nc.tensor.transpose(
                                pstr[half][:, c8 * 128:(c8 + 1) * 128],
                                lnbf[:, c * 128:(c + 1) * 128], ident[:])
                        sy.done("pe", inst)
                    tsl = bass.ts(u, 128)
                    for half in range(2):
                        sy.waitn("dv", "pe", 1 - half)
                        src2 = pstr[half][:].rearrange("p (c n) -> p c n", n=128)
                        sy.done("dv", nc.vector.tensor_copy(
                            lnv[:, half * 8:(half + 1) * 8, tsl], src2))

                sy.begin_loop([("dv", "xa"), ("ac", "xa"), ("dv", "dv"), ("dv", "ac"),
                               ("ac", "dv"), ("ac", "ac"), ("pe", "dv"), ("dv", "pe")],
                              {"xa": 16, "dv": 9, "ac": 2, "pe": 2})
                with ctx.fori(NT, [PE, DVE, ACT]) as u:
                    sy.iter_start(u)
                    p3a_tile(u)
                sy.end_loop(NT)
                sy.free()
            ctx.barrier()

            with nc.psum_tensor("p3b_pz", [128, 2048], F32) as pzh, \
                 nc.sbuf_tensor("p3b_wz", [128, 16 * 2048], BF16) as wzr, \
                 nc.sbuf_tensor("p3b_ls", [128, 16 * 128], BF16) as lst, \
                 nc.sbuf_tensor("p3b_ze", [128, 2048], BF16) as zevh:
                sy = Sync(ctx, "p3b")

                def p3b_tile(u, half):
                    sy.waitn("dv", "pe")
                    sy.done("dv", nc.vector.tensor_copy(
                        lst[:].rearrange("p (c n) -> p c n", n=128),
                        lnv[:, :, bass.ts(u, 128)]))
                    sy.waitn("pe", "dv")
                    for g in range(4):
                        inst = None
                        for c in range(ND):
                            inst = nc.tensor.matmul(
                                pzh[:, g * 512:(g + 1) * 512],
                                lst[:, c * 128:(c + 1) * 128],
                                wzr[:, c * 2048 + g * 512: c * 2048 + (g + 1) * 512],
                                start=(c == 0), stop=(c == ND - 1))
                        sy.done("pe", inst)
                    sy.waitn("dv", "pe")
                    sy.waitn("dv", "zs")
                    sy.done("dv", nc.vector.tensor_copy(zevh[:], pzh[:]))
                    sy.waitn("ac", "dv")
                    sy.dma("zs", nc.scalar.dma_start(
                        out=zsv[u][:, half * 2048:(half + 1) * 2048], in_=zevh[:]))

                for half in range(2):
                    if half == 1:
                        sy.wait("sp", "pe", sy.n["pe"])
                    sy.dma("wz", nc.sync.dma_start(
                        out=wzr[:].rearrange("p (c n) -> p c n", n=2048),
                        in_=wzv[:, :, half * 2048:(half + 1) * 2048]))
                    sy.wait("pe", "wz", 16 * (half + 1))
                    sy.begin_loop([("pe", "dv"), ("dv", "pe"), ("dv", "zs"),
                                   ("ac", "dv")],
                                  {"pe": 4, "dv": 2, "zs": 16})
                    with ctx.fori(NT, [PE, DVE, ACT]) as u:
                        sy.iter_start(u)
                        p3b_tile(u, half)
                    sy.end_loop(NT)
                nc.sync.wait_ge(sy.p["zs"], sy.n["zs"])
                sy.free()
            ctx.barrier()

            with nc.psum_tensor("p3c_px", [128, 2048], F32) as px, \
                 nc.psum_tensor("p3c_pc", [128, 1024], F32) as pc, \
                 nc.psum_tensor("p3c_ptr", [128, 2048], BF16) as ptr, \
                 nc.sbuf_tensor("p3c_wx", [128, 16 * 128], BF16) as wxs, \
                 nc.sbuf_tensor("p3c_dg", [128, 4 * 128], BF16) as dgb, \
                 nc.sbuf_tensor("p3c_xraw", [128, 2051], BF16) as xraw, \
                 nc.sbuf_tensor("p3c_postc", [128, 2048], BF16) as postc, \
                 nc.sbuf_tensor("p3c_xtsb", [128, 2048], BF16) as xtsb:
                nc.gpsimd.dma_start(out=xraw[:, 0:3],
                                    in_=d["diagmask"][:, 1536:1539]).then_inc(cn, 16)
                ncn += 16
                for e in (nc.vector, nc.tensor):
                    e.wait_ge(cn, ncn)
                sy = Sync(ctx, "p3c")

                def p3c_core(cc):
                    sy.waitn("sp", "pe")
                    sy.dma("wx", nc.sync.dma_start(
                        out=wxs[:].rearrange("p (c n) -> p c n", n=128),
                        in_=wxv[:, :, bass.ts(cc, 128)]))
                    sy.dma("wx", nc.sync.dma_start(
                        out=dgb[:].rearrange("p (j n) -> p j n", n=128),
                        in_=cdv[:, :, cc, :]))
                    sy.waitn("po", "ac")
                    sy.done("po", nc.gpsimd.tensor_copy(cbst[:, 0:1],
                                                        convb[:, bass.ts(cc, 1)]))
                    sy.waitn("pe", "wx")
                    sy.waitn("pe", "dv", 1)
                    for tq in range(4):
                        inst = None
                        for c in range(ND):
                            inst = nc.tensor.matmul(
                                px[:, tq * 512:(tq + 1) * 512],
                                wxs[:, c * 128:(c + 1) * 128],
                                ln3[:, c * T + tq * 512: c * T + (tq + 1) * 512],
                                start=(c == 0), stop=(c == ND - 1))
                        sy.done("pe", inst)
                    sy.waitn("dv", "pe")
                    sy.done("dv", nc.vector.tensor_copy(xraw[:, 3:2051], px[:]))

                def p3c_conv(cc, silu_out):
                    for q in range(2):
                        sy.waitn("pe", "dv")
                        sy.waitn("pe", "ac")
                        sy.waitn("pe", "po")
                        inst = None
                        for j in range(4):
                            for th in range(2):
                                inst = nc.tensor.matmul(
                                    pc[:, th * 512:(th + 1) * 512],
                                    dgb[:, j * 128:(j + 1) * 128],
                                    xraw[:, (2 * q + th) * 512 + j:
                                         (2 * q + th) * 512 + j + 512],
                                    start=(j == 0), stop=(j == 3),
                                    skip_group_check=True)
                        sy.done("pe", inst)
                        sy.waitn("ac", "pe")
                        sy.done("ac", nc.scalar.activation(
                            out=silu_out(q), in_=pc[:], func=AF.Silu,
                            bias=cbst[:, 0:1]))

                def p3c_trans(src):
                    for g in range(2):
                        sy.waitn("pe", "ac")
                        sy.waitn("pe", "dv", 1)
                        inst = None
                        for i8 in range(8):
                            i = g * 8 + i8
                            inst = nc.tensor.transpose(
                                ptr[:, i * 128:(i + 1) * 128],
                                src[:, i * 128:(i + 1) * 128], ident[:])
                        sy.done("pe", inst)
                    sy.waitn("dv", "pe")

                def p3c_body(cc):
                    p3c_core(cc)
                    p3c_conv(cc, lambda q: postc[:, q * 1024:(q + 1) * 1024])
                    p3c_trans(postc)
                    sy.waitn("dv", "xs")
                    sy.done("dv", nc.vector.tensor_copy(xtsb[:], ptr[:]))
                    sy.waitn("ac", "dv")
                    xtp = d["xtm_scr"].rearrange("(t p) n -> p t n", p=128)
                    sy.dma("xs", nc.scalar.dma_start(
                        out=xtp[:, :, bass.ts(cc, 128)],
                        in_=xtsb[:].rearrange("p (t n) -> p t n", n=128)))

                sy.begin_loop([("sp", "pe"), ("po", "ac"), ("pe", "wx"),
                               ("pe", "dv"), ("pe", "ac"), ("pe", "po"), ("dv", "pe"),
                               ("dv", "xs"), ("ac", "pe"), ("ac", "dv")],
                              {"wx": 32, "po": 1, "pe": 8, "dv": 2,
                               "ac": 2, "xs": 16})
                with ctx.fori(32, [PE, DVE, ACT, SP, Pool]) as u:
                    sy.iter_start(u)
                    p3c_body(u)
                sy.end_loop(32)
                p3c_core(32)
                p3c_conv(32, lambda q: Bfm[:, q * 1024:(q + 1) * 1024])
                p3c_trans(Bfm)
                sy.done("dv", nc.vector.tensor_copy(B_tm[:], ptr[:]))
                p3c_core(33)
                p3c_conv(33, lambda q: Cfm32[:, q * 1024:(q + 1) * 1024])
                sy.waitn("dv", "ac")
                sy.done("dv", nc.vector.tensor_copy(Cbf[:], Cfm32[:]))
                nc.sync.wait_ge(sy.p["xs"], sy.n["xs"])
                sy.free()
            ctx.barrier()


            with nc.psum_tensor("p3d_pd", [128, 2048], F32) as pd, \
                 nc.sbuf_tensor("p3d_wdt", [128, 16 * 64], BF16) as wdtres, \
                 nc.sbuf_tensor("p3d_dtA", [64, T], F32) as dtA:
                sy = Sync(ctx, "p3d")
                nc.gpsimd.dma_start(
                    out=wdtres[:].rearrange("p (c n) -> p c n", n=64),
                    in_=wdtv).then_inc(cn, 16)
                ncn += 16
                nc.tensor.wait_ge(cn, ncn)
                sy.done("ac", nc.scalar.activation(out=Acol[:, 1:2], in_=Acol[:, 0:1],
                                                   func=AF.Exp))
                sy.waitn("dv", "ac")
                sy.done("dv", nc.vector.tensor_scalar_mul(Acol[:, 1:2], Acol[:, 1:2],
                                                          -1.0))
                for tq in range(4):
                    inst = None
                    for c in range(ND):
                        inst = nc.tensor.matmul(
                            pd[0:64, tq * 512:(tq + 1) * 512],
                            wdtres[:, c * 64:(c + 1) * 64],
                            ln3[:, c * T + tq * 512: c * T + (tq + 1) * 512],
                            start=(c == 0), stop=(c == ND - 1))
                    sy.done("pe", inst)
                    dsl = dtf[:, tq * 512:(tq + 1) * 512]
                    sy.waitn("dv", "pe")
                    sy.done("dv", nc.vector.tensor_scalar(
                        out=dsl, in0=pd[0:64, tq * 512:(tq + 1) * 512],
                        scalar1=dtb[:, 0:1], scalar2=30.0,
                        op0=OP.add, op1=OP.min))
                    sy.waitn("ac", "dv")
                    sy.done("ac", nc.scalar.activation(out=dsl, in_=dsl,
                                                       func=AF.Exp))
                    sy.waitn("dv", "ac")
                    sy.done("dv", nc.vector.tensor_scalar_add(dsl, dsl, 1.0))
                    sy.waitn("ac", "dv")
                    sy.done("ac", nc.scalar.activation(out=dsl, in_=dsl,
                                                       func=AF.Ln))
                sy.waitn("dv", "ac")
                sy.waitn("dv", "ac")
                sy.done("dv", nc.vector.tensor_scalar_mul(dtA[:], dtf[:],
                                                          Acol[:, 1:2]))
                for ch in range(16):
                    sy.waitn("dv", "dv")
                    sy.done("dv", nc.vector.tensor_tensor_scan(
                        out=cumf[:, ch * 128:(ch + 1) * 128],
                        data0=dtA[:, ch * 128:(ch + 1) * 128],
                        data1=zer64[:], initial=0.0, op0=OP.add, op1=OP.add))
                sy.waitn("sp", "dv")
                sy.dma("cs", nc.sync.dma_start(out=cumd, in_=cumf[:]))
                nc.sync.wait_ge(sy.p["cs"], 16)
                sy.free()
            ctx.barrier()

        es_e = ExitStack()
        sbe = lambda n, shp, dt: es_e.enter_context(nc.sbuf_tensor(n, shp, dt))
        with es_e:
            pbig = es_e.enter_context(nc.psum_tensor("p3e_pa", [128, 2048], F32))
            psB = es_e.enter_context(nc.psum_tensor("p3e_pb", [128, 2048], F32))
            S = sbe("p3e_S", [128, 4096], F32)
            EW = sbe("p3e_EW", [128, 2048], BF16)
            dtx = sbe("p3e_dtx", [128, 4096], BF16)
            dtxd = sbe("p3e_dtxd", [128, 4096], BF16)
            Xch = sbe("p3e_X", [128, 4096], BF16)
            zch = sbe("p3e_z", [128, 4096], BF16)
            ysb = sbe("p3e_y", [128, 4096], F32)
            cum_tm = sbe("p3e_ctm", [128, 64], F32)
            dt_tm = sbe("p3e_dttm", [128, 64], F32)
            expcum = sbe("p3e_ecum", [128, 64], F32)
            decf = sbe("p3e_decf", [128, 64], F32)
            dAl = sbe("p3e_dAl", [128, 64], F32)
            Gsb = sbe("p3e_G", [128, 128], BF16)
            cr0 = sbe("p3e_cr0", [1, 4096], F32)
            cr1 = sbe("p3e_cr1", [1, 4096], F32)
            culast = sbe("p3e_cl", [1, 64], F32)
            elast = sbe("p3e_el", [1, 64], F32)
            cumstg = sbe("p3e_cs64", [64, 128], F32)
            dtstg = sbe("p3e_ds64", [64, 128], F32)
            Bstg = sbe("p3e_Bst", [128, 128], BF16)
            Cstg = sbe("p3e_Cst", [128, 128], F32)
            Btmstg = sbe("p3e_Btst", [128, 128], BF16)
            Dbc = sbe("p3e_Dbc", [128, 4096], BF16)
            nw1 = sbe("p3e_nw", [128, 4096], BF16)
            crh = [cr0, cr1]
            nc.gpsimd.dma_start(out=S[:], in_=d["zeros4096"]).then_inc(cn, 16)
            nc.gpsimd.dma_start(out=Dbc[:], in_=d["D_bc"]).then_inc(cn, 16)
            nc.gpsimd.dma_start(out=nw1[:], in_=d["l1_normw_bc"]).then_inc(cn, 16)
            ncn += 48
            for e in (nc.vector, nc.scalar, nc.tensor, nc.sync):
                e.wait_ge(cn, ncn)
            sy = Sync(ctx, "p3e")

            def p3e_body(ch):
                sy.waitn("sp", "dv", 10)
                sy.dma("xz", nc.sync.dma_start(out=Xch[:], in_=xtv[ch]))
                sy.waitn("sp", "yn")
                sy.dma("xz", nc.sync.dma_start(out=zch[:], in_=zsv[ch]))
                for hf in range(2):
                    sy.dma("cr", nc.sync.dma_start(
                        out=crh[hf][0:1, :].rearrange("o (h t) -> o h t", t=128),
                        in_=cumd[hf * 32:(hf + 1) * 32,
                                 bass.ts(ch, 128)].rearrange("h t -> () h t")))
                sy.waitn("ac", "pe")
                sy.done("ac", nc.scalar.activation(out=cumstg[:], func=AF.Copy,
                                                   in_=cumf[:, bass.ts(ch, 128)]))
                sy.done("ac", nc.scalar.activation(out=dtstg[:], func=AF.Copy,
                                                   in_=dtf[:, bass.ts(ch, 128)]))
                sy.done("ac", nc.scalar.activation(out=Bstg[:], func=AF.Copy,
                                                   in_=Bfm[:, bass.ts(ch, 128)]))
                sy.done("ac", nc.scalar.activation(out=Cstg[:], func=AF.Copy,
                                                   in_=Cfm32[:, bass.ts(ch, 128)]))
                sy.done("ac", nc.scalar.activation(out=Btmstg[:], func=AF.Copy,
                                                   in_=B_tm[:, bass.ts(ch, 128)]))
                sy.waitn("pe", "ac")
                sy.waitn("pe", "dv")
                sy.done("pe", nc.tensor.matmul(pbig[:, 0:128], Bstg[:],
                                               Cbf[:, bass.ts(ch, 128)],
                                               start=True, stop=True))
                sy.done("pe", nc.tensor.transpose(pbig[:, 512:576], cumstg[:],
                                                  ident32[0:64, 0:64]))
                sy.done("pe", nc.tensor.transpose(pbig[:, 1024:1088], dtstg[:],
                                                  ident32[0:64, 0:64]))
                sy.waitn("dv", "pe")
                sy.done("dv", nc.vector.tensor_tensor(out=Gsb[:], in0=pbig[:, 0:128],
                                                      in1=tri[:], op=OP.mult))
                sy.done("dv", nc.vector.tensor_copy(cum_tm[:], pbig[:, 512:576]))
                sy.done("dv", nc.vector.tensor_copy(dt_tm[:], pbig[:, 1024:1088]))
                sy.waitn("pe", "dv")
                sy.done("pe", nc.tensor.matmul(psB[0:1, 64:128], ident32[:, 127:128],
                                               cum_tm[:], start=True, stop=True,
                                               skip_group_check=True))
                sy.waitn("dv", "pe")
                sy.done("dv", nc.vector.tensor_copy(culast[:], psB[0:1, 64:128]))
                sy.waitn("ac", "dv")
                sy.done("ac", nc.scalar.activation(out=expcum[:], in_=cum_tm[:],
                                                   func=AF.Exp))
                sy.done("ac", nc.scalar.activation(out=elast[:], in_=culast[:],
                                                   func=AF.Exp))
                sy.waitn("pe", "ac")
                sy.done("pe", nc.tensor.matmul(pbig[:, 1536:1600], onesr32[:],
                                               culast[:], start=True, stop=True,
                                               skip_group_check=True))
                sy.done("pe", nc.tensor.matmul(psB[:, 0:64], onesr32[:],
                                               elast[:], start=True, stop=True,
                                               skip_group_check=True))
                sy.waitn("dv", "pe")
                sy.done("dv", nc.vector.tensor_tensor(out=decf[:],
                                                      in0=pbig[:, 1536:1600],
                                                      in1=cum_tm[:], op=OP.subtract))
                sy.done("dv", nc.vector.tensor_copy(dAl[:], psB[:, 0:64]))
                sy.waitn("ac", "dv")
                sy.done("ac", nc.scalar.activation(out=decf[:], in_=decf[:],
                                                   func=AF.Exp))
                sy.waitn("dv", "xz")
                sy.waitn("dv", "ac")
                d3 = dtx[:].rearrange("p (h q) -> p h q", q=64)
                x3_ = Xch[:].rearrange("p (h q) -> p h q", q=64)
                sy.done("dv", nc.vector.tensor_tensor(
                    out=d3, in0=x3_,
                    in1=dt_tm[:].unsqueeze(2).to_broadcast([128, 64, 64]),
                    op=OP.mult))
                dd3 = dtxd[:].rearrange("p (h q) -> p h q", q=64)
                sy.waitn("dv", "dv")
                sy.done("dv", nc.vector.tensor_tensor(
                    out=dd3, in0=d3,
                    in1=decf[:].unsqueeze(2).to_broadcast([128, 64, 64]),
                    op=OP.mult))
                for hf in range(2):
                    for qq in range(2):
                        q = 2 * hf + qq
                        sy.waitn("pe", "cr")
                        sy.waitn("pe", "ac")
                        sy.waitn("pe", "dv")
                        inst = None
                        for i in range(4):
                            inst = nc.tensor.matmul(
                                pbig[:, i * 512:(i + 1) * 512], onesr32[:],
                                crh[hf][0:1, qq * 2048 + i * 512:
                                        qq * 2048 + (i + 1) * 512],
                                start=True, stop=True)
                        sy.done("pe", inst)
                        sy.waitn("dv", "pe")
                        p3 = pbig[:].rearrange("p (h t) -> p h t", t=128)
                        sy.done("dv", nc.vector.tensor_tensor(
                            out=p3, in0=p3,
                            in1=cum_tm[:, q * 16:(q + 1) * 16].unsqueeze(2)
                                .to_broadcast([128, 16, 128]),
                            op=OP.subtract))
                        sy.waitn("dv", "dv")
                        sy.done("dv", nc.vector.tensor_scalar_min(pbig[:], pbig[:],
                                                                  0.0))
                        sy.waitn("ac", "dv")
                        sy.waitn("ac", "pe")
                        sy.done("ac", nc.scalar.activation(out=EW[:], in_=pbig[:],
                                                           func=AF.Exp))
                        sy.waitn("dv", "ac")
                        e3 = EW[:].rearrange("p (h t) -> p h t", t=128)
                        sy.done("dv", nc.vector.tensor_tensor(
                            out=e3, in0=e3,
                            in1=Gsb[:].unsqueeze(1).to_broadcast([128, 16, 128]),
                            op=OP.mult))
                        sy.waitn("pe", "dv")
                        inst = None
                        for hl in range(16):
                            h = q * 16 + hl
                            inst = nc.tensor.matmul(
                                psB[:, qq * 1024 + hl * 64:
                                    qq * 1024 + (hl + 1) * 64],
                                EW[:, hl * 128:(hl + 1) * 128],
                                dtx[:, h * 64:(h + 1) * 64],
                                start=True, stop=True, skip_group_check=True)
                        sy.done("pe", inst)
                    sy.waitn("pe", "ac")
                    inst = None
                    for i in range(4):
                        inst = nc.tensor.matmul(
                            pbig[:, i * 512:(i + 1) * 512], Cstg[:],
                            S[:, hf * 2048 + i * 512: hf * 2048 + (i + 1) * 512],
                            start=True, stop=True)
                    sy.done("pe", inst)
                    sy.waitn("dv", "pe")
                    ysl = ysb[:, hf * 2048:(hf + 1) * 2048].rearrange(
                        "p (h q) -> p h q", q=64)
                    pb3 = pbig[:].rearrange("p (h q) -> p h q", q=64)
                    sy.done("dv", nc.vector.tensor_tensor(
                        out=ysl, in0=pb3,
                        in1=expcum[:, hf * 32:(hf + 1) * 32].unsqueeze(2)
                            .to_broadcast([128, 32, 64]),
                        op=OP.mult))
                    sy.waitn("dv", "dv")
                    sy.done("dv", nc.vector.tensor_tensor(
                        out=ysb[:, hf * 2048:(hf + 1) * 2048],
                        in0=ysb[:, hf * 2048:(hf + 1) * 2048],
                        in1=psB[:], op=OP.add))
                for hf in range(2):
                    sy.waitn("dv", "pe")
                    sy.waitn("dv", "dv")
                    sy.done("dv", nc.vector.tensor_tensor(
                        out=EW[:], in0=Xch[:, hf * 2048:(hf + 1) * 2048],
                        in1=Dbc[:, hf * 2048:(hf + 1) * 2048], op=OP.mult))
                    sy.waitn("dv", "dv")
                    sy.done("dv", nc.vector.tensor_tensor(
                        out=ysb[:, hf * 2048:(hf + 1) * 2048],
                        in0=ysb[:, hf * 2048:(hf + 1) * 2048],
                        in1=EW[:], op=OP.add))
                sy.waitn("ac", "dv")
                sy.waitn("ac", "xz")
                sy.done("ac", nc.scalar.activation(out=zch[:], in_=zch[:],
                                                   func=AF.Silu))
                sy.waitn("dv", "ac")
                sy.done("dv", nc.vector.tensor_tensor(out=ysb[:], in0=ysb[:],
                                                      in1=zch[:], op=OP.mult))
                sy.waitn("ac", "dv")
                sy.done("ac", nc.scalar.activation(out=dtx[:], in_=ysb[:],
                                                   func=AF.Square,
                                                   accum_out=stat[:, 0:1]))
                sy.waitn("dv", "ac")
                sy.done("dv", nc.vector.tensor_scalar_mul(stat[:, 1:2], stat[:, 0:1],
                                                          float(1.0 / INTER)))
                sy.waitn("ac", "dv")
                sy.done("ac", nc.scalar.activation(out=stat[:, 2:3], in_=stat[:, 1:2],
                                                   func=AF.Sqrt, bias=f32c(1e-5)))
                sy.waitn("dv", "ac")
                sy.done("dv", nc.vector.reciprocal(stat[:, 3:4], stat[:, 2:3]))
                sy.waitn("dv", "dv")
                sy.done("dv", nc.vector.scalar_tensor_tensor(
                    out=dtx[:], in0=ysb[:], scalar=stat[:, 3:4], in1=nw1[:],
                    op0=OP.mult, op1=OP.mult))
                sy.waitn("pe", "dv")
                pbigB = pbig[:, 0:1024].bitcast(BF16)
                psBB = psB[:, 0:1024].bitcast(BF16)
                for g in range(4):
                    inst = None
                    for i8 in range(8):
                        i = g * 8 + i8
                        pt = pbigB if i < 16 else psBB
                        inst = nc.tensor.transpose(
                            pt[:, (i % 16) * 128:((i % 16) + 1) * 128],
                            dtx[:, i * 128:(i + 1) * 128], ident[:])
                    sy.done("pe", inst)
                sy.waitn("dv", "pe")
                sy.done("dv", nc.vector.tensor_copy(zch[:, 0:2048], pbigB[:]))
                sy.done("dv", nc.vector.tensor_copy(zch[:, 2048:4096], psBB[:]))
                sy.waitn("ac", "dv")
                sy.dma("yn", nc.scalar.dma_start(
                    out=ynv[:, :, bass.ts(ch, 128)],
                    in_=zch[:].rearrange("p (c t) -> p c t", t=128)))
                sy.waitn("pe", "dv")
                inst = None
                for i in range(8):
                    pt = pbig if i < 4 else psB
                    inst = nc.tensor.matmul(
                        pt[:, (i % 4) * 512:((i % 4) + 1) * 512], Btmstg[:],
                        dtxd[:, i * 512:(i + 1) * 512],
                        start=True, stop=True, skip_group_check=True)
                sy.done("pe", inst)
                s3 = S[:].rearrange("p (h q) -> p h q", q=64)
                sy.waitn("dv", "pe")
                sy.done("dv", nc.vector.tensor_tensor(
                    out=s3, in0=s3,
                    in1=dAl[:].unsqueeze(2).to_broadcast([128, 64, 64]),
                    op=OP.mult))
                sy.waitn("dv", "dv")
                sy.done("dv", nc.vector.tensor_tensor(out=S[:, 0:2048],
                                                      in0=S[:, 0:2048],
                                                      in1=pbig[:], op=OP.add))
                sy.done("dv", nc.vector.tensor_tensor(out=S[:, 2048:4096],
                                                      in0=S[:, 2048:4096],
                                                      in1=psB[:], op=OP.add))

            pairs = [("sp", "dv"), ("sp", "yn"),
                     ("pe", "dv"), ("pe", "ac"),
                     ("pe", "cr"), ("dv", "pe"), ("dv", "dv"),
                     ("dv", "ac"), ("dv", "xz"), ("ac", "dv"), ("ac", "pe"),
                     ("ac", "xz")]
            sy.begin_loop(pairs, {"xz": 32, "cr": 32, "yn": 16,
                                  "pe": 21, "dv": 37, "ac": 15})
            with ctx.fori(16, [PE, DVE, ACT, SP]) as ch:
                sy.iter_start(ch)
                p3e_body(ch)
            sy.end_loop(16)
            nc.sync.wait_ge(sy.p["yn"], sy.n["yn"])
            sy.free()
        ctx.barrier()
        es_bc.close()

        with nc.psum_tensor("p3f_p", [128, 2048], F32) as pf, \
             nc.sbuf_tensor("p3f_wo", [128, 32 * 2048], BF16) as wout, \
             nc.sbuf_tensor("p3f_yn", [128, 32 * 128], BF16) as ynst, \
             nc.sbuf_tensor("p3f_x2", [128, 2048], F32) as x2t, \
             nc.sbuf_tensor("p3f_x3", [128, 2048], F32) as x3t:
            nc.gpsimd.dma_start(out=wout[:].rearrange("p (c n) -> p c n", n=2048),
                                in_=wov).then_inc(cn, 16)
            ncn += 16
            for e in (nc.tensor, nc.sync):
                e.wait_ge(cn, ncn)
            sy = Sync(ctx, "p3f")

            def p3f_tile(u):
                sy.waitn("sp", "pe")
                sy.dma("ya", nc.sync.dma_start(
                    out=ynst[:].rearrange("p (c t) -> p c t", t=128),
                    in_=ynv[:, :, bass.ts(u, 128)]))
                sy.waitn("ac", "dv")
                sy.dma("xa", nc.scalar.dma_start(out=x2t[:], in_=x2v[u]))
                sy.waitn("pe", "ya")
                sy.waitn("pe", "dv")
                for dp in range(4):
                    inst = None
                    for c in range(32):
                        inst = nc.tensor.matmul(
                            pf[:, dp * 512:(dp + 1) * 512],
                            ynst[:, c * 128:(c + 1) * 128],
                            wout[:, c * 2048 + dp * 512: c * 2048 + (dp + 1) * 512],
                            start=(c == 0), stop=(c == 31))
                    sy.done("pe", inst)
                sy.waitn("dv", "pe")
                sy.waitn("dv", "xa")
                sy.waitn("dv", "oa")
                sy.done("dv", nc.vector.tensor_tensor(out=x3t[:], in0=pf[:],
                                                      in1=x2t[:], op=OP.add))
                sy.waitn("ac", "dv")
                sy.dma("oa", nc.scalar.dma_start(out=x3v[u], in_=x3t[:]))

            sy.begin_loop([("sp", "pe"), ("pe", "ya"), ("pe", "dv"), ("dv", "pe"),
                           ("dv", "xa"), ("dv", "oa"), ("ac", "dv")],
                          {"ya": 16, "xa": 16, "oa": 16, "pe": 4, "dv": 1})
            with ctx.fori(NT, [PE, DVE, SP, ACT]) as u:
                sy.iter_start(u)
                p3f_tile(u)
            sy.end_loop(NT)
            nc.sync.wait_ge(sy.p["oa"], 16 * NT)
            sy.free()
    ctx.barrier()


def emit_p4(ctx):
    nc, d = ctx.nc, ctx.d
    es = ExitStack()
    sb = lambda n, shp, dt=F32: es.enter_context(nc.sbuf_tensor(n, shp, dt))

    ident = sb("p4_ident", [128, 128], BF16)
    onesr32 = sb("p4_oner", [1, 128], F32)
    onec32 = sb("p4_onec", [128, 1], F32)
    stat = sb("p4_stat", [128, 8], F32)
    srow = sb("p4_srow", [1, 4], F32)
    thrc = sb("p4_thrc", [128, 9], F32)
    invs_all = sb("p4_invs", [128, 16], F32)

    x3v = d["x3"].rearrange("(t p) n -> t p n", p=128)
    outv = d["out"].rearrange("(t p) n -> t p n", p=128)
    wgv = d["wgT"].rearrange("(c p) n -> p c n", p=128)
    wuv = d["wuT"].rearrange("(c p) n -> p c n", p=128)
    wdv = d["wdT"].rearrange("(c p) n -> p c n", p=128)
    gsv = d["g_scr"].rearrange("(t p) n -> t p n", p=128)
    gsp = d["g_scr"].rearrange("(t p) n -> p t n", p=128)
    qgv = d["qg_scr"].rearrange("(c p) t -> p c t", p=128)

    with es:
        cn = ctx.sem("p4cn")
        ncn = 0
        for dst, src in ((ident[:], d["identity"]), (onesr32[:], d["ones_row"]),
                         (onec32[:], d["ones_col"])):
            nc.gpsimd.dma_start(out=dst, in_=src).then_inc(cn, 16)
            ncn += 16
        for e in (nc.vector, nc.scalar, nc.tensor, nc.sync):
            e.wait_ge(cn, ncn)

        with nc.psum_tensor("p4a_ps", [128, 512], F32) as psa, \
             nc.sbuf_tensor("p4a_w0", [128, 8192], F32) as ws0, \
             nc.sbuf_tensor("p4a_w1", [128, 8192], F32) as ws1, \
             nc.sbuf_tensor("p4a_ab", [128, 8192], BF16) as absb:
            ws = [ws0, ws1]
            sy = Sync(ctx, "p4a")
            chunks = ([("g", wgv[:, c, :]) for c in range(16)] +
                      [("u", wuv[:, c, :]) for c in range(16)] +
                      [("d", wdv[:, 4 * q:4 * q + 4, :]) for q in range(13)])
            nmat = {"g": 0, "u": 1, "d": 2}
            first = {"g": True, "u": True, "d": True}
            for i, (mk, src) in enumerate(chunks):
                par = i % 2
                wq = ("wa", "wb")[par]
                width = 6656 if mk in ("g", "u") else 8192
                sy.waitn("sp", "ac", 1)
                if mk == "d":
                    sy.dma(wq, nc.sync.dma_start(
                        out=ws[par][:].rearrange("p (c n) -> p c n", n=2048),
                        in_=src))
                else:
                    sy.dma(wq, nc.sync.dma_start(out=ws[par][:, 0:width], in_=src))
                sy.waitn("ac", wq)
                sy.waitn("ac", "dv")
                sy.done("ac", nc.scalar.activation(out=absb[:, 0:width],
                                                   in_=ws[par][:, 0:width],
                                                   func=AF.Abs,
                                                   accum_out=stat[:, 0:1]))
                sy.waitn("pe", "ac")
                sy.done("pe", nc.tensor.matmul(psa[0:1, 200:201], onec32[0:1, 0:1],
                                               onec32[0:1, 0:1], start=True,
                                               stop=True, skip_group_check=True))
                sy.waitn("dv", "ac")
                mi = nmat[mk]
                if first[mk]:
                    first[mk] = False
                    sy.done("dv", nc.vector.tensor_copy(stat[:, 1 + mi:2 + mi],
                                                        stat[:, 0:1]))
                else:
                    sy.done("dv", nc.vector.tensor_tensor(
                        out=stat[:, 1 + mi:2 + mi], in0=stat[:, 1 + mi:2 + mi],
                        in1=stat[:, 0:1], op=OP.add))
            for mi in range(3):
                sy.waitn("pe", "dv")
                sy.done("pe", nc.tensor.matmul(psa[0:1, 0:1], stat[:, 1 + mi:2 + mi],
                                               onec32[:], start=True, stop=True,
                                               skip_group_check=True))
                sy.waitn("dv", "pe")
                sy.done("dv", nc.vector.tensor_scalar(
                    out=srow[0:1, mi:mi + 1], in0=psa[0:1, 0:1],
                    scalar1=float(1.0 / (I2 * D)), scalar2=1e-5,
                    op0=OP.mult, op1=OP.max))
                sy.waitn("pe", "dv")
                sy.done("pe", nc.tensor.matmul(psa[:, 64:65], onesr32[:],
                                               srow[0:1, mi:mi + 1],
                                               start=True, stop=True,
                                               skip_group_check=True))
                sy.waitn("dv", "pe")
                sy.done("dv", nc.vector.tensor_scalar_mul(thrc[:, 2 * mi:2 * mi + 1],
                                                          psa[:, 64:65], 0.5))
                sy.waitn("dv", "dv")
                sy.done("dv", nc.vector.tensor_scalar_mul(
                    thrc[:, 2 * mi + 1:2 * mi + 2], psa[:, 64:65], -0.5))
                sy.waitn("dv", "dv")
                sy.done("dv", nc.vector.tensor_copy(thrc[:, 6 + mi:7 + mi],
                                                    psa[:, 64:65]))
            sy.free()
        ctx.barrier()

        es_q = ExitStack()
        qact = es_q.enter_context(nc.sbuf_tensor("p4_qact", [128, 16 * 2048], BF16))
        qview = qact[:].rearrange("p (c t) -> p c t", t=T)
        with nc.psum_tensor("p4b_ptr", [128, 2048], BF16) as ptr, \
             nc.psum_tensor("p4b_dum", [1, 8], F32) as pdm, \
             nc.sbuf_tensor("p4b_xw", [128, D], F32) as xw, \
             nc.sbuf_tensor("p4b_sq", [128, D], F32) as sqs, \
             nc.sbuf_tensor("p4b_lnf", [128, D], F32) as lnf, \
             nc.sbuf_tensor("p4b_u", [128, D], F32) as ubuf, \
             nc.sbuf_tensor("p4b_ci", [128, D], I32) as cibuf, \
             nc.sbuf_tensor("p4b_cf", [128, D], F32) as cfbuf, \
             nc.sbuf_tensor("p4b_iv", [128, 1], F32) as ivcol, \
             nc.sbuf_tensor("p4b_q", [128, D], BF16) as lnq:
            sy = Sync(ctx, "p4b")

            def p4b_tile(u):
                sy.waitn("ac", "dv")
                sy.dma("xa", nc.scalar.dma_start(out=xw[:], in_=x3v[u]))
                sy.waitn("dv", "xa")
                sy.waitn("ac", "xa")
                sy.waitn("dv", "dv")
                sy.done("dv", nc.vector.tensor_reduce(out=stat[:, 0:1], in_=xw[:],
                                                      axis=AX.X, op=OP.add))
                sy.waitn("ac", "ac")
                sy.done("ac", nc.scalar.activation(out=sqs[:], in_=xw[:],
                                                   func=AF.Square,
                                                   accum_out=stat[:, 1:2]))
                sy.waitn("dv", "dv")
                sy.done("dv", nc.vector.tensor_scalar_mul(stat[:, 2:3], stat[:, 0:1],
                                                          float(1.0 / D)))
                sy.waitn("dv", "dv")
                sy.done("dv", nc.vector.tensor_tensor(out=stat[:, 3:4],
                                                      in0=stat[:, 2:3],
                                                      in1=stat[:, 2:3], op=OP.mult))
                sy.waitn("pe", "dv")
                sy.done("pe", nc.tensor.matmul(pdm[0:1, 0:1], ident[0:1, 0:1],
                                               ident[0:1, 0:1], start=True, stop=True,
                                               skip_group_check=True))

                sy.waitn("dv", "dv")
                sy.waitn("dv", "ac")
                sy.done("dv", nc.vector.scalar_tensor_tensor(
                    out=stat[:, 4:5], in0=stat[:, 1:2], scalar=float(1.0 / D),
                    in1=stat[:, 3:4], op0=OP.mult, op1=OP.subtract))
                sy.waitn("ac", "dv")
                sy.done("ac", nc.scalar.activation(out=stat[:, 5:6], in_=stat[:, 4:5],
                                                   func=AF.Sqrt, bias=f32c(1e-5)))
                sy.waitn("dv", "ac")
                sy.done("dv", nc.vector.reciprocal(stat[:, 6:7], stat[:, 5:6]))
                sy.waitn("dv", "dv")
                sy.done("dv", nc.vector.tensor_tensor(out=stat[:, 7:8],
                                                      in0=stat[:, 2:3],
                                                      in1=stat[:, 6:7], op=OP.mult))
                sy.waitn("dv", "dv")
                sy.done("dv", nc.vector.tensor_scalar(out=lnf[:], in0=xw[:],
                                                      scalar1=stat[:, 6:7],
                                                      scalar2=stat[:, 7:8],
                                                      op0=OP.mult, op1=OP.subtract))
                sy.waitn("pe", "dv")
                sy.done("pe", nc.tensor.matmul(pdm[0:1, 0:1], ident[0:1, 0:1],
                                               ident[0:1, 0:1], start=True, stop=True,
                                               skip_group_check=True))

                sy.waitn("dv", "dv")
                sy.done("dv", nc.vector.scalar_tensor_tensor(
                    out=ubuf[:], in0=lnf[:], scalar=-1.0, in1=lnf[:],
                    op0=OP.mult, op1=OP.max))
                sy.waitn("dv", "dv")
                sy.done("dv", nc.vector.tensor_reduce(out=stat[:, 0:1], in_=ubuf[:],
                                                      axis=AX.X, op=OP.max))
                sy.waitn("dv", "dv")
                sy.done("dv", nc.vector.tensor_scalar_max(stat[:, 1:2], stat[:, 0:1],
                                                          1e-5))
                sy.waitn("dv", "dv")
                sy.done("dv", nc.vector.reciprocal(stat[:, 2:3], stat[:, 1:2]))
                sy.waitn("dv", "dv")
                sy.done("dv", nc.vector.tensor_scalar_mul(stat[:, 3:4], stat[:, 2:3],
                                                          127.0))
                sy.waitn("dv", "dv")
                sy.done("dv", nc.vector.tensor_scalar_mul(
                    invs_all[:, bass.ts(u, 1)], stat[:, 1:2], float(1.0 / 127.0)))
                sy.waitn("pe", "dv")
                sy.done("pe", nc.tensor.matmul(pdm[0:1, 0:1], ident[0:1, 0:1],
                                               ident[0:1, 0:1], start=True, stop=True,
                                               skip_group_check=True))

                sy.waitn("dv", "dv")
                sy.done("dv", nc.vector.tensor_scalar(out=ubuf[:], in0=lnf[:],
                                                      scalar1=stat[:, 3:4],
                                                      scalar2=0.5,
                                                      op0=OP.mult, op1=OP.add))
                sy.waitn("dv", "dv")
                sy.done("dv", nc.vector.tensor_copy(cibuf[:], ubuf[:]))
                sy.waitn("pe", "dv")
                sy.done("pe", nc.tensor.matmul(pdm[0:1, 0:1], ident[0:1, 0:1],
                                               ident[0:1, 0:1], start=True, stop=True,
                                               skip_group_check=True))

                sy.waitn("dv", "dv")
                sy.done("dv", nc.vector.tensor_copy(cfbuf[:], cibuf[:]))
                sy.waitn("dv", "dv")
                sy.done("dv", nc.vector.tensor_tensor(out=ubuf[:], in0=cfbuf[:],
                                                      in1=ubuf[:], op=OP.is_gt))
                sy.waitn("pe", "dv")
                sy.done("pe", nc.tensor.matmul(pdm[0:1, 0:1], ident[0:1, 0:1],
                                               ident[0:1, 0:1], start=True, stop=True,
                                               skip_group_check=True))

                sy.waitn("dv", "dv")
                sy.waitn("dv", "pe")
                sy.done("dv", nc.vector.tensor_tensor(out=lnq[:], in0=cfbuf[:],
                                                      in1=ubuf[:], op=OP.subtract))
                sy.waitn("pe", "dv")
                sy.waitn("pe", "dv", 17)
                inst = None
                for c in range(16):
                    inst = nc.tensor.transpose(ptr[:, c * 128:(c + 1) * 128],
                                               lnq[:, c * 128:(c + 1) * 128],
                                               ident[:])
                sy.done("pe", inst)
                sy.waitn("dv", "pe")
                sy.done("dv", nc.vector.tensor_copy(
                    qview[:, :, bass.ts(u, 128)],
                    ptr[:].rearrange("p (c n) -> p c n", n=128)))
                isv = d["invs_scr"].rearrange("(t p) o -> t p o", p=128)
                sy.waitn("dv", "is")
                sy.done("dv", nc.vector.tensor_scalar_mul(ivcol[:], stat[:, 1:2],
                                                          float(1.0 / 127.0)))
                sy.waitn("ac", "dv")
                sy.dma("is", nc.scalar.dma_start(out=isv[u], in_=ivcol[:]))

            sy.begin_loop([("ac", "dv"), ("dv", "xa"), ("ac", "xa"), ("dv", "dv"),
                           ("dv", "ac"), ("ac", "ac"), ("dv", "pe"), ("pe", "dv"),
                           ("dv", "is")],
                          {"xa": 16, "is": 16, "dv": 20, "ac": 2, "pe": 6})
            with ctx.fori(NT, [PE, DVE, ACT]) as u:
                sy.iter_start(u)
                p4b_tile(u)
            sy.end_loop(NT)
            nc.sync.wait_ge(sy.p["is"], 16 * NT)
            sy.free()
        ctx.barrier()

        with nc.psum_tensor("p4g_pg", [128, 2048], F32) as pg, \
             nc.psum_tensor("p4g_pu", [128, 2048], F32) as pu:
            sy = Sync(ctx, "p4g")
            with nc.sbuf_tensor("p4g_ivr", [1, 2048], F32) as ivrow, \
                 nc.sbuf_tensor("p4g_bcg", [128, 2048], F32) as invsg_bc, \
                 nc.sbuf_tensor("p4g_bcu", [128, 2048], F32) as invsu_bc, \
                 nc.sbuf_tensor("p4g_wf", [128, 16 * 512], F32) as wf, \
                 nc.sbuf_tensor("p4g_qwg", [128, 16 * 512], BF16) as qwg, \
                 nc.sbuf_tensor("p4g_qwu", [128, 16 * 512], BF16) as qwu, \
                 nc.sbuf_tensor("p4g_qtmp", [128, 16 * 512], BF16) as qtmp, \
                 nc.sbuf_tensor("p4g_gfm", [128, 2048], BF16) as gfm, \
                 nc.sbuf_tensor("p4g_gtm", [128, 2048], BF16) as gtm, \
                 nc.sbuf_tensor("p4g_sgb", [128, 4 * 2048], BF16) as sgbuf, \
                 nc.sbuf_tensor("p4g_sg", [128, 512], F32) as sg:
                sy.dma("iv", nc.sync.dma_start(
                    out=ivrow[:], in_=d["invs_scr"].rearrange("t o -> o t")))
                nc.tensor.wait_ge(sy.p["iv"], 16)
                inst = None
                for i in range(4):
                    inst = nc.tensor.matmul(pg[:, i * 512:(i + 1) * 512], onesr32[:],
                                            ivrow[0:1, i * 512:(i + 1) * 512],
                                            start=True, stop=True,
                                            skip_group_check=True)
                r_bc = sy.done("pe", inst)
                sy.waitn("dv", "pe")
                sy.done("dv", nc.vector.tensor_scalar_mul(invsg_bc[:], pg[:],
                                                          thrc[:, 6:7]))
                sy.waitn("dv", "dv")
                sy.done("dv", nc.vector.tensor_scalar_mul(invsu_bc[:], pg[:],
                                                          thrc[:, 7:8]))

                def g1_body(op):
                    sy.waitn("sp", "dv")
                    sy.dma("wf", nc.sync.dma_start(
                        out=wf[:].rearrange("p (c n) -> p c n", n=512),
                        in_=wgv[:, :, bass.ts(op, 512)]))
                    sy.waitn("dv", "wf")
                    sy.waitn("dv", "pe")
                    sy.done("dv", nc.vector.tensor_scalar(
                        out=qtmp[:], in0=wf[:], scalar1=thrc[:, 1:2], scalar2=0.0,
                        op0=OP.is_le, op1=OP.bypass))
                    sy.waitn("dv", "dv")
                    sy.done("dv", nc.vector.scalar_tensor_tensor(
                        out=qwg[:], in0=wf[:], scalar=thrc[:, 0:1], in1=qtmp[:],
                        op0=OP.is_ge, op1=OP.subtract))
                    sy.waitn("pe", "dv")
                    sy.done("pe", nc.tensor.matmul(pu[0:1, 0:1], ident[0:1, 0:1],
                                                   ident[0:1, 0:1], start=True,
                                                   stop=True, skip_group_check=True))
                    sy.waitn("sp", "dv")
                    sy.dma("wf", nc.sync.dma_start(
                        out=wf[:].rearrange("p (c n) -> p c n", n=512),
                        in_=wuv[:, :, bass.ts(op, 512)]))
                    for osub in range(4):
                        sy.waitn("pe", "dv")
                        for tq in range(4):
                            inst = None
                            for c in range(16):
                                inst = nc.tensor.matmul(
                                    pg[:, tq * 512:(tq + 1) * 512],
                                    qwg[:, c * 512 + osub * 128:
                                        c * 512 + osub * 128 + 128],
                                    qact[:, c * 2048 + tq * 512:
                                         c * 2048 + (tq + 1) * 512],
                                    start=(c == 0), stop=(c == 15))
                            sy.done("pe", inst)
                        for tq in range(4):
                            sy.waitn("dv", "pe")
                            sy.waitn("dv", "ac")
                            sy.done("dv", nc.vector.tensor_tensor(
                                out=sg[:], in0=pg[:, tq * 512:(tq + 1) * 512],
                                in1=invsg_bc[:, tq * 512:(tq + 1) * 512],
                                op=OP.mult))
                            sy.waitn("ac", "dv")
                            sy.done("ac", nc.scalar.activation(
                                out=sgbuf[:, osub * 2048 + tq * 512:
                                          osub * 2048 + (tq + 1) * 512],
                                in_=sg[:], func=AF.Silu))
                    sy.waitn("dv", "wf")
                    sy.waitn("dv", "pe")
                    sy.done("dv", nc.vector.tensor_scalar(
                        out=qtmp[:], in0=wf[:], scalar1=thrc[:, 3:4], scalar2=0.0,
                        op0=OP.is_le, op1=OP.bypass))
                    sy.waitn("dv", "dv")
                    sy.done("dv", nc.vector.scalar_tensor_tensor(
                        out=qwu[:], in0=wf[:], scalar=thrc[:, 2:3], in1=qtmp[:],
                        op0=OP.is_ge, op1=OP.subtract))
                    sy.waitn("pe", "dv")
                    sy.done("pe", nc.tensor.matmul(pg[0:1, 0:1], ident[0:1, 0:1],
                                                   ident[0:1, 0:1], start=True,
                                                   stop=True, skip_group_check=True))
                    for osub in range(4):
                        sy.waitn("pe", "dv")
                        for tq in range(4):
                            inst = None
                            for c in range(16):
                                inst = nc.tensor.matmul(
                                    pu[:, tq * 512:(tq + 1) * 512],
                                    qwu[:, c * 512 + osub * 128:
                                        c * 512 + osub * 128 + 128],
                                    qact[:, c * 2048 + tq * 512:
                                         c * 2048 + (tq + 1) * 512],
                                    start=(c == 0), stop=(c == 15))
                            sy.done("pe", inst)
                        for tq in range(4):
                            sy.waitn("dv", "pe")
                            sy.waitn("dv", "dv")
                            sy.done("dv", nc.vector.tensor_tensor(
                                out=gfm[:, tq * 512:(tq + 1) * 512],
                                in0=pu[:, tq * 512:(tq + 1) * 512],
                                in1=invsu_bc[:, tq * 512:(tq + 1) * 512],
                                op=OP.mult))
                            sy.waitn("dv", "ac")
                            sy.waitn("dv", "dv")
                            sy.done("dv", nc.vector.tensor_tensor(
                                out=gfm[:, tq * 512:(tq + 1) * 512],
                                in0=gfm[:, tq * 512:(tq + 1) * 512],
                                in1=sgbuf[:, osub * 2048 + tq * 512:
                                          osub * 2048 + (tq + 1) * 512],
                                op=OP.mult))
                        ptrg = pg[:, 0:1024].bitcast(BF16)
                        sy.waitn("pe", "dv")
                        inst = None
                        for i in range(16):
                            inst = nc.tensor.transpose(
                                ptrg[:, i * 128:(i + 1) * 128],
                                gfm[:, i * 128:(i + 1) * 128], ident[:])
                        sy.done("pe", inst)
                        sy.waitn("dv", "pe")
                        sy.waitn("dv", "gs")
                        sy.done("dv", nc.vector.tensor_copy(gtm[:], ptrg[:]))
                        sy.waitn("sp", "dv", 0)
                        sy.dma("gs", nc.sync.dma_start(
                            out=gsp[:, :, bass.ts(op, 512)][
                                :, :, osub * 128:(osub + 1) * 128],
                            in_=gtm[:].rearrange("p (t n) -> p t n", n=128)))

                sy.begin_loop([("sp", "dv"), ("dv", "wf"), ("dv", "pe"), ("dv", "dv"),
                               ("dv", "ac"), ("dv", "gs"), ("pe", "dv"),
                               ("ac", "dv")],
                              {"wf": 32, "pe": 38, "dv": 4 + 4 * 13, "ac": 16,
                               "gs": 64})
                with ctx.fori(13, [PE, DVE, ACT, SP]) as op:
                    sy.iter_start(op)
                    g1_body(op)
                sy.end_loop(13)
                nc.sync.wait_ge(sy.p["gs"], 64 * 13)
                sy.free()
        es_q.close()
        ctx.barrier()

        with nc.psum_tensor("p4c_ptr", [128, 2048], BF16) as ptr2, \
             nc.psum_tensor("p4c_dum", [1, 8], F32) as pdm2, \
             nc.sbuf_tensor("p4c_g", [128, I2P], BF16) as gbuf, \
             nc.sbuf_tensor("p4c_u", [128, I2P], F32) as u2, \
             nc.sbuf_tensor("p4c_ci", [128, I2P], I32) as ci2, \
             nc.sbuf_tensor("p4c_cf", [128, I2P], F32) as cf2, \
             nc.sbuf_tensor("p4c_q", [128, I2P], BF16) as qgb, \
             nc.sbuf_tensor("p4c_qt", [128, 13 * 128], BF16) as qtm:
            sy = Sync(ctx, "p4c")

            def p4c_tile(u):
                sy.waitn("po", "dv")
                sy.dma("gl", nc.gpsimd.dma_start(out=gbuf[:], in_=gsv[u]))
                sy.waitn("dv", "gl")
                sy.waitn("dv", "dv")
                sy.done("dv", nc.vector.scalar_tensor_tensor(
                    out=u2[:], in0=gbuf[:], scalar=-1.0, in1=gbuf[:],
                    op0=OP.mult, op1=OP.max))
                sy.waitn("dv", "dv")
                sy.done("dv", nc.vector.tensor_reduce(out=stat[:, 0:1], in_=u2[:],
                                                      axis=AX.X, op=OP.max))
                sy.waitn("pe", "dv")
                sy.done("pe", nc.tensor.matmul(pdm2[0:1, 0:1], ident[0:1, 0:1],
                                               ident[0:1, 0:1], start=True, stop=True,
                                               skip_group_check=True))

                sy.waitn("dv", "dv")
                sy.done("dv", nc.vector.tensor_scalar_max(stat[:, 1:2], stat[:, 0:1],
                                                          1e-5))
                sy.waitn("dv", "dv")
                sy.done("dv", nc.vector.reciprocal(stat[:, 2:3], stat[:, 1:2]))
                sy.waitn("dv", "dv")
                sy.done("dv", nc.vector.tensor_scalar_mul(stat[:, 3:4], stat[:, 2:3],
                                                          127.0))
                sy.waitn("dv", "dv")
                sy.done("dv", nc.vector.tensor_scalar_mul(
                    invs_all[:, bass.ts(u, 1)], stat[:, 1:2], float(1.0 / 127.0)))
                sy.waitn("dv", "dv")
                sy.done("dv", nc.vector.tensor_scalar(out=u2[:], in0=gbuf[:],
                                                      scalar1=stat[:, 3:4],
                                                      scalar2=0.5,
                                                      op0=OP.mult, op1=OP.add))
                sy.waitn("pe", "dv")
                sy.done("pe", nc.tensor.matmul(pdm2[0:1, 0:1], ident[0:1, 0:1],
                                               ident[0:1, 0:1], start=True, stop=True,
                                               skip_group_check=True))

                sy.waitn("dv", "dv")
                sy.done("dv", nc.vector.tensor_copy(ci2[:], u2[:]))
                sy.waitn("dv", "dv")
                sy.done("dv", nc.vector.tensor_copy(cf2[:], ci2[:]))
                sy.waitn("pe", "dv")
                sy.done("pe", nc.tensor.matmul(pdm2[0:1, 0:1], ident[0:1, 0:1],
                                               ident[0:1, 0:1], start=True, stop=True,
                                               skip_group_check=True))

                sy.waitn("dv", "dv")
                sy.done("dv", nc.vector.tensor_tensor(out=u2[:], in0=cf2[:],
                                                      in1=u2[:], op=OP.is_gt))
                sy.waitn("pe", "dv")
                sy.done("pe", nc.tensor.matmul(pdm2[0:1, 0:1], ident[0:1, 0:1],
                                               ident[0:1, 0:1], start=True, stop=True,
                                               skip_group_check=True))

                sy.waitn("dv", "dv")
                sy.waitn("dv", "pe")
                sy.done("dv", nc.vector.tensor_tensor(out=qgb[:], in0=cf2[:],
                                                      in1=u2[:], op=OP.subtract))
                for g4 in range(4):
                    sy.waitn("pe", "dv")
                    inst = None
                    for j in range(13):
                        i = g4 * 13 + j
                        inst = nc.tensor.transpose(ptr2[:, j * 128:(j + 1) * 128],
                                                   qgb[:, i * 128:(i + 1) * 128],
                                                   ident[:])
                    sy.done("pe", inst)
                    sy.waitn("dv", "pe")
                    sy.waitn("dv", "qs")
                    sy.done("dv", nc.vector.tensor_copy(qtm[:], ptr2[:, 0:13 * 128]))
                    sy.waitn("po", "dv")
                    sy.dma("qs", nc.gpsimd.dma_start(
                        out=qgv[:, g4 * 13:(g4 + 1) * 13, bass.ts(u, 128)],
                        in_=qtm[:].rearrange("p (c t) -> p c t", t=128)))

            sy.begin_loop([("po", "dv"), ("dv", "gl"), ("dv", "dv"), ("dv", "pe"),
                           ("pe", "dv"), ("dv", "qs")],
                          {"gl": 16, "qs": 64, "dv": 11 + 4, "pe": 8})
            with ctx.fori(NT, [PE, DVE, Pool]) as u:
                sy.iter_start(u)
                p4c_tile(u)
            sy.end_loop(NT)
            sy.waitn("dv", "dv")
            sy.done("dv", nc.vector.tensor_scalar_mul(invs_all[:], invs_all[:],
                                                      thrc[:, 8:9]))
            nc.gpsimd.wait_ge(sy.p["qs"], 64 * NT)
            sy.free()
        ctx.barrier()

        with nc.psum_tensor("p4d_ps", [128, 1024], F32) as pd4, \
             nc.sbuf_tensor("p4d_wf0", [128, 13 * 512], F32) as wdf0, \
             nc.sbuf_tensor("p4d_wf1", [128, 13 * 512], F32) as wdf1, \
             nc.sbuf_tensor("p4d_qw", [128, 52 * 1024], BF16) as qwd, \
             nc.sbuf_tensor("p4d_qt", [128, 13 * 512], BF16) as qdt, \
             nc.sbuf_tensor("p4d_qg", [128, 52 * 128], BF16) as qgt, \
             nc.sbuf_tensor("p4d_x3", [128, 1024], F32) as x3s, \
             nc.sbuf_tensor("p4d_ob", [128, 1024], F32) as ob, \
             nc.sbuf_tensor("p4d_iv", [128, 1], F32) as iv2:
            sy = Sync(ctx, "p4d")
            wdfs = [wdf0, wdf1]
            for dh in range(2):
                for q8 in range(8):
                    q4, dsl = q8 // 2, 2 * dh + q8 % 2
                    par = q8 % 2
                    sy.waitn("sp", "dv", 1)
                    if dh > 0 and q8 < 2:
                        sy.waitn("sp", "pe")
                    sy.dma("wf", nc.sync.dma_start(
                        out=wdfs[par][:].rearrange("p (c n) -> p c n", n=512),
                        in_=wdv[:, q4 * 13:(q4 + 1) * 13,
                                dsl * 512:(dsl + 1) * 512]))
                    sy.waitn("dv", "wf")
                    sy.waitn("dv", "dv")
                    if dh > 0 and q8 == 0:
                        sy.waitn("dv", "pe")
                    sy.done("dv", nc.vector.tensor_scalar(
                        out=qdt[:], in0=wdfs[par][:], scalar1=thrc[:, 5:6],
                        scalar2=0.0, op0=OP.is_le, op1=OP.bypass))
                    sy.waitn("dv", "dv")
                    sy.done("dv", nc.vector.scalar_tensor_tensor(
                        out=qwd[:].rearrange("p (c q) -> p c q", q=1024)[
                            :, q4 * 13:(q4 + 1) * 13,
                            (q8 % 2) * 512:(q8 % 2 + 1) * 512],
                        in0=wdfs[par][:], scalar=thrc[:, 4:5], in1=qdt[:],
                        op0=OP.is_ge, op1=OP.subtract))

                def p4d_tile(u):
                    sy.waitn("ac", "pe")
                    sy.dma("qg", nc.scalar.dma_start(
                        out=qgt[:].rearrange("p (c t) -> p c t", t=128),
                        in_=qgv[:, :, bass.ts(u, 128)]))
                    sy.waitn("po", "dv")
                    sy.dma("xr", nc.gpsimd.dma_start(
                        out=x3s[:], in_=x3v[u][:, dh * 1024:(dh + 1) * 1024]))
                    sy.waitn("dv", "dv")
                    sy.done("dv", nc.vector.tensor_copy(iv2[:],
                                                        invs_all[:, bass.ts(u, 1)]))
                    sy.waitn("pe", "qg")
                    sy.waitn("pe", "dv")
                    for hb in range(2):
                        inst = None
                        for c in range(NOC):
                            inst = nc.tensor.matmul(
                                pd4[:, hb * 512:(hb + 1) * 512],
                                qgt[:, c * 128:(c + 1) * 128],
                                qwd[:, c * 1024 + hb * 512:
                                    c * 1024 + (hb + 1) * 512],
                                start=(c == 0), stop=(c == NOC - 1))
                        sy.done("pe", inst)
                    sy.waitn("dv", "pe")
                    sy.waitn("dv", "xr")
                    sy.waitn("dv", "dv")
                    sy.waitn("dv", "os")
                    sy.done("dv", nc.vector.scalar_tensor_tensor(
                        out=ob[:], in0=pd4[:], scalar=iv2[:, 0:1],
                        in1=x3s[:], op0=OP.mult, op1=OP.add))
                    sy.waitn("ac", "dv")
                    sy.dma("os", nc.scalar.dma_start(
                        out=outv[u][:, dh * 1024:(dh + 1) * 1024], in_=ob[:]))

                sy.begin_loop([("ac", "pe"), ("ac", "dv"), ("po", "dv"),
                               ("pe", "qg"), ("pe", "dv"), ("dv", "pe"),
                               ("dv", "xr"), ("dv", "dv"), ("dv", "os")],
                              {"qg": 16, "xr": 16, "os": 16, "pe": 2, "dv": 2})
                with ctx.fori(NT, [PE, DVE, ACT, Pool]) as u:
                    sy.iter_start(u)
                    p4d_tile(u)
                sy.end_loop(NT)
            nc.sync.wait_ge(sy.p["os"], 16 * NT * 2)
            sy.free()
    ctx.barrier()



PHASE_EMITS = {}


def build_program(phases=("p0", "p1", "p2", "p3", "p4"),
                  external_scratch=(), final_wait=True):
    nc = bass.Bass("TRN2", target_bir_lowering=False)
    es = ExitStack()
    ctx = Ctx(nc, es)
    ctx.d["x0"] = nc.dram_tensor("x0", [T, D], F32, kind="ExternalInput").ap()
    for name, shp in HOST_SHAPES.items():
        ctx.d[name] = nc.dram_tensor(name, list(shp), F32, kind="ExternalInput").ap()
    ctx.d["out"] = nc.dram_tensor("out", [T, D], F32, kind="ExternalOutput").ap()
    for name, (shp, dt) in SCRATCH.items():
        kind = "ExternalOutput" if name in external_scratch else "Internal"
        ctx.d[name] = nc.dram_tensor(name, list(shp), dt, kind=kind).ap()
    emits = {"p0": emit_p0, "p1": emit_p1, "p2": emit_p2, "p3": emit_p3,
             "p4": emit_p4}
    with es:
        for p in phases:
            emits[p](ctx)
    return nc, ctx


def kernel(**inputs):
    Hm = host_prepare(inputs)
    x_all = np.asarray(inputs["L0_attn_hidden_states"], dtype=np.float32)
    nc, ctx = build_program()
    in_maps = []
    for b in range(8):
        m = dict(Hm)
        m["x0"] = np.ascontiguousarray(x_all[b])
        in_maps.append(m)
    res = run_bass_kernel_spmd(nc, in_maps, core_ids=list(range(8)))
    return np.stack([res.results[b]["out"] for b in range(8)], axis=0)
